# version 01
# baseline (speedup 1.0000x reference)
import math
import numpy as np
import concourse.bass as bass
import concourse.mybir as mybir
from concourse.bass_utils import run_bass_kernel_spmd

F32 = mybir.dt.float32
AF = mybir.ActivationFunctionType
ALU = mybir.AluOpType

D = 1024
T = 2048
DC = 8
QT = 256
NQT = T // QT
BW = 3968
BC0 = 1920
NEG = -1.0e30
EPS = 1e-5


def _alibi_slopes32():
    base = 2.0 ** (-(2.0 ** (-(math.log2(32) - 3))))
    powers = np.arange(1, 33, dtype=np.float32)
    return (np.float32(base) ** powers).astype(np.float32)


def _bandmat_l0():
    band = np.arange(4095, dtype=np.float64) - 2047.0
    band[band > 0] = NEG
    p = np.arange(128)[:, None]
    c = np.arange(BW)[None, :]
    idx = p - c + (BW - 1)
    ok = (idx >= 0) & (idx <= 4094)
    return (band[np.clip(idx, 0, 4094)] * ok).astype(np.float32)


class Cnt:
    def __init__(self, sem):
        self.sem = sem
        self.val = 0

    def inc(self, ins, n=1):
        ins.then_inc(self.sem, n)
        self.val += n
        return self.val


class Rel:

    def __init__(self, cnt):
        self.c = cnt
        self.b = cnt.val

    def inc(self, ins, n=1):
        return self.c.inc(ins, n)

    def w(self, eng, v):
        if self.b + v > 0:
            eng.wait_ge(self.c.sem, self.b + v)

    @property
    def val(self):
        return self.c.val

    @property
    def rval(self):
        return self.c.val - self.b

    @property
    def sem(self):
        return self.c.sem


def build(upto=99, dumps=(), race=False):
    nc = bass.Bass("TRN2", target_bir_lowering=False,
                   detect_race_conditions=race, num_devices=8)
    dumps = set(dumps)

    x_in = nc.dram_tensor("x_fm", [D, T], F32, kind="ExternalInput").ap()
    w_qkv = nc.dram_tensor("w_qkv", [D, 4096], F32, kind="ExternalInput").ap()
    w_out = nc.dram_tensor("w_out", [2048, D], F32, kind="ExternalInput").ap()
    cbt = nc.dram_tensor("cbt", [D, D], F32, kind="ExternalInput").ap()
    scb_d = nc.dram_tensor("scb", [D], F32, kind="ExternalInput").ap()
    cbias_d = nc.dram_tensor("cbias", [D], F32, kind="ExternalInput").ap()
    wqT = nc.dram_tensor("wqT", [D, D], F32, kind="ExternalInput").ap()
    wkT = nc.dram_tensor("wkT", [D, D], F32, kind="ExternalInput").ap()
    wvT = nc.dram_tensor("wvT", [D, D], F32, kind="ExternalInput").ap()
    woT = nc.dram_tensor("woT", [D, D], F32, kind="ExternalInput").ap()
    bq_d = nc.dram_tensor("bq", [D], F32, kind="ExternalInput").ap()
    bk_d = nc.dram_tensor("bk", [D], F32, kind="ExternalInput").ap()
    bv_d = nc.dram_tensor("bv", [D], F32, kind="ExternalInput").ap()
    bo_d = nc.dram_tensor("bo", [D], F32, kind="ExternalInput").ap()
    etg = nc.dram_tensor("etg", [1, 1], F32, kind="ExternalInput").ap()
    isg = nc.dram_tensor("isg", [1, 1], F32, kind="ExternalInput").ap()
    bandrev1 = nc.dram_tensor("bandrev1", [4095], F32, kind="ExternalInput").ap()
    w1_d = nc.dram_tensor("w1", [8, D, 4096], F32, kind="ExternalInput").ap()
    w2_d = nc.dram_tensor("w2", [8, 4096, D], F32, kind="ExternalInput").ap()
    wg_d = nc.dram_tensor("wg", [D, 8], F32, kind="ExternalInput").ap()
    bg_d = nc.dram_tensor("bg", [8], F32, kind="ExternalInput").ap()
    out_d = nc.dram_tensor("out_fm", [D, T], F32, kind="ExternalOutput").ap()

    q0_dram = nc.dram_tensor("q0_scr", [2048, T], F32).ap()
    q1_dram = nc.dram_tensor("q1_scr", [D, T], F32).ap()
    xln_dram = nc.dram_tensor("xln_scr", [D, T], F32).ap()
    x1_dram = nc.dram_tensor("x1_scr", [D, T], F32).ap()
    x2_dram = nc.dram_tensor("x2_scr", [D, T], F32).ap()
    x3_dram = nc.dram_tensor("x3_scr", [D, T], F32).ap()
    ctx0_dram = nc.dram_tensor("ctx0_scr", [2048, T], F32).ap()
    ctx1_dram = nc.dram_tensor("ctx1_scr", [D, T], F32).ap()
    agctx = nc.dram_tensor("agctx", [8 * D, T], F32).ap()
    bm1_dram = nc.dram_tensor("bm1_scr", [128, BW], F32).ap()
    bdev_dram = nc.dram_tensor("bdev_scr", [4095], F32).ap()

    bm0 = nc.inline_tensor(_bandmat_l0(), name="bm0").ap()
    selmat = np.zeros((8, 8 * 128), np.float32)
    for e in range(8):
        selmat[e, e * 128:(e + 1) * 128] = 1.0
    sel_c = nc.inline_tensor(selmat, name="selc").ap()
    id8_c = nc.inline_tensor(np.eye(8, dtype=np.float32), name="id8c").ap()
    id128_c = nc.inline_tensor(np.eye(128, dtype=np.float32), name="id128c").ap()
    ucum_np = np.triu(np.ones((8, 8), np.float32), 1).T
    ucum_c = nc.inline_tensor(np.ascontiguousarray(ucum_np.T), name="ucumc").ap()
    slopes8 = [float(s) * 8.0 for s in _alibi_slopes32()]

    dump_aps = {}
    for name, shape in [("d_xln", [D, T]), ("d_k", [D, T]), ("d_v", [128, 16 * 1040]),
                        ("d_q0", [2048, T]), ("d_ctx0", [2048, T]), ("d_x1", [D, T]),
                        ("d_x2", [D, T]), ("d_ctx1", [D, T]), ("d_a2a", [2 * D, T]),
                        ("d_x3", [D, T]), ("d_bm1", [128, BW]), ("d_q1", [D, T]),
                        ("d_mask", [8, T]), ("d_xln2", [D, T]), ("d_scbq", [128, 16]), ("d_csrc", [128, 8 * T]), ("d_gdbg", [128, 32])]:
        if name in dumps:
            dump_aps[name] = nc.dram_tensor(name, shape, F32, kind="ExternalOutput").ap()

    onesc = nc.alloc_sbuf_tensor("onesc", [128, 1], F32).ap()
    ones1 = nc.alloc_sbuf_tensor("ones1", [1, 128], F32).ap()
    scb_sb = nc.alloc_sbuf_tensor("scb_sb", [128, 8], F32).ap()
    cbias_sb = nc.alloc_sbuf_tensor("cbias_sb", [128, 8], F32).ap()
    bq_sb = nc.alloc_sbuf_tensor("bq_sb", [128, 8], F32).ap()
    bk_sb = nc.alloc_sbuf_tensor("bk_sb", [128, 8], F32).ap()
    bo_sb = nc.alloc_sbuf_tensor("bo_sb", [128, 8], F32).ap()
    bg_sb = nc.alloc_sbuf_tensor("bg_sb", [8, 1], F32).ap()
    r127 = nc.alloc_sbuf_tensor("r127", [128, 1], F32).ap()
    r127b = nc.alloc_sbuf_tensor("r127b", [128, 1], F32).ap()
    scbq_sb = nc.alloc_sbuf_tensor("scbq_sb", [128, 8], F32).ap()

    sems = {}

    def C(name):
        if name not in sems:
            sems[name] = Cnt(nc.alloc_semaphore(name))
        return Rel(sems[name])

    def bar():
        nc.all_engine_barrier()

    def w(eng, cnt):
        if cnt.val > 0:
            eng.wait_ge(cnt.sem, cnt.val)

    def dump_rows(name, dram_src, nrow):
        if name not in dump_aps:
            return
        sD = C("dmp")
        with nc.Block() as blk:
            @blk.scalar
            def _(sc):
                for r in range(nrow // 128):
                    sD.inc(sc.dma_start(dump_aps[name][r * 128:(r + 1) * 128, :],
                                        dram_src[r * 128:(r + 1) * 128, :]), 16)
                w(sc, sD)
        bar()

    sP0 = C("p0ld")
    sP0g = C("p0g")
    with nc.Block() as blk:
        @blk.sync
        def _(sy):
            for dst, src in [(scb_sb, scb_d), (cbias_sb, cbias_d), (bq_sb, bq_d),
                             (bk_sb, bk_d), (bo_sb, bo_d)]:
                for dc in range(DC):
                    sP0.inc(sy.dma_start(dst[:, dc:dc + 1],
                                         src[dc * 128:(dc + 1) * 128]), 16)
            sP0.inc(sy.dma_start(bg_sb[:, :], bg_d[:]), 16)
            w(sy, sP0)

        @blk.gpsimd
        def _(g):
            g.memset(onesc[:, :], 1.0)
            g.memset(ones1[:, :], 1.0)
            sP0g.inc(g.memset(r127[:, :], 127.0))

        @blk.vector
        def _(v):
            sP0.w(v, sP0.val)
            sP0g.w(v, 1)
            sP0g.inc(v.reciprocal(r127b[:, :], r127[:, :]))
            sP0g.w(v, 2)
            v.tensor_single_scalar(scbq_sb[:, :], scb_sb[:, :], r127b[:, 0:1], ALU.mult)
    bar()

    def dump_smalls():
        if "d_scbq" not in dump_aps:
            return
        sDx = C("dmpg")
        with nc.Block() as blk:
            @blk.gpsimd
            def _(g):
                sDx.inc(g.dma_start(dump_aps["d_scbq"][:, 0:8], scbq_sb[:, :]), 16)
                sDx.inc(g.dma_start(dump_aps["d_scbq"][:, 8:12], cbias_sb[:, 0:4]), 16)
                sDx.inc(g.dma_start(dump_aps["d_scbq"][:, 12:13], r127[:, :],
                                    allow_slow_non_contiguous=True), 16)
                sDx.inc(g.dma_start(dump_aps["d_scbq"][:, 13:14], r127b[:, :],
                                    allow_slow_non_contiguous=True), 16)
                sDx.inc(g.dma_start(dump_aps["d_scbq"][:, 14:15], scb_sb[:, 0:1],
                                    allow_slow_non_contiguous=True), 16)
                w(g, sDx)
        bar()

    lnid = [0]

    def emit_ln_stream(src_dram, dst_dram):
        lnid[0] += 1
        W_ = 512
        ntt = 4
        tag = "ln"
        utag = f"ln{lnid[0]}"
        sXL, sSQ, sSUM, sVA, sRT, sST2, sBC, sAPP = (C(tag + s) for s in
            ("xl", "sq", "sum", "va", "rt", "st2", "bc", "app"))
        sOSs = [C(tag + "os0"), C(tag + "os1")]
        with (
            nc.psum_tensor(utag + "_s1", [1, W_], F32) as ps1_g,
            nc.psum_tensor(utag + "_s2", [1, W_], F32) as ps2_g,
            nc.psum_tensor(utag + "_a", [128, W_], F32) as psa_g,
            nc.psum_tensor(utag + "_b", [128, W_], F32) as psb_g,
            nc.sbuf_tensor(utag + "_xh", [128, DC * W_], F32) as xh_g,
            nc.sbuf_tensor(utag + "_sq", [128, DC * W_], F32) as sq_g,
            nc.sbuf_tensor(utag + "_ob", [128, 2 * W_], F32) as ob_g,
            nc.sbuf_tensor(utag + "_mean", [1, W_], F32) as mean_g,
            nc.sbuf_tensor(utag + "_varp", [1, W_], F32) as varp_g,
            nc.sbuf_tensor(utag + "_t1", [1, W_], F32) as t1_g,
            nc.sbuf_tensor(utag + "_inv", [1, W_], F32) as inv_g,
        ):
            ps1, ps2, psa, psb = ps1_g.ap(), ps2_g.ap(), psa_g.ap(), psb_g.ap()
            xh, sq, ob = xh_g.ap(), sq_g.ap(), ob_g.ap()
            mean, varp, t1, inv = mean_g.ap(), varp_g.ap(), t1_g.ap(), inv_g.ap()
            with nc.Block() as blk:
                @blk.sync
                def _(sy):
                    for tt in range(ntt):
                        if tt >= 1:
                            sAPP.w(sy, tt * DC)
                        sXL.inc(sy.dma_start(
                            bass.AP(xh.tensor, xh.offset,
                                    [[DC * W_, 128], [W_, DC], [1, W_]]),
                            bass.AP(src_dram.tensor,
                                    src_dram.offset + tt * W_,
                                    [[T, 128], [128 * T, DC], [1, W_]])), 16)

                @blk.scalar
                def _(sc):
                    for tt in range(ntt):
                        if tt >= 1:
                            sSUM.w(sc, tt)
                        sXL.w(sc, 16 * (tt + 1))
                        for dc in range(DC):
                            sSQ.inc(sc.activation(sq[:, dc * W_:(dc + 1) * W_],
                                                  xh[:, dc * W_:(dc + 1) * W_],
                                                  AF.Square))
                        sVA.w(sc, tt + 1)
                        sRT.inc(sc.activation(t1, varp, AF.Sqrt))

                @blk.tensor
                def _(t):
                    for tt in range(ntt):
                        if tt >= 1:
                            sVA.w(t, tt)
                        sXL.w(t, 16 * (tt + 1))
                        ins = None
                        for dc in range(DC):
                            i = tt * DC + dc
                            t.matmul(ps1[:, :], onesc[:, :],
                                     xh[:, dc * W_:(dc + 1) * W_],
                                     start=(dc == 0), stop=(dc == DC - 1))
                            sSQ.w(t, i + 1)
                            ins = t.matmul(ps2[:, :], onesc[:, :],
                                           sq[:, dc * W_:(dc + 1) * W_],
                                           start=(dc == 0), stop=(dc == DC - 1))
                        sSUM.inc(ins)
                        sST2.w(t, tt + 1)
                        if tt >= 1:
                            sAPP.w(t, tt * DC)
                        t.matmul(psa[:, :], ones1[:, :], inv, start=True, stop=True)
                        sBC.inc(t.matmul(psb[:, :], ones1[:, :], mean,
                                         start=True, stop=True))

                @blk.vector
                def _(v):
                    for tt in range(ntt):
                        sSUM.w(v, tt + 1)
                        v.tensor_single_scalar(mean, ps1[0:1, :], 1.0 / 1024.0, ALU.mult)
                        v.tensor_single_scalar(varp, ps2[0:1, :], 1.0 / 1024.0, ALU.mult)
                        v.tensor_tensor(t1, mean, mean, ALU.mult)
                        v.tensor_tensor(varp, varp, t1, ALU.subtract)
                        sVA.inc(v.tensor_single_scalar(varp, varp, EPS, ALU.add))
                        sRT.w(v, tt + 1)
                        v.reciprocal(inv, t1)
                        for _i in range(2):
                            v.tensor_tensor(t1, inv, inv, ALU.mult)
                            v.tensor_tensor(t1, t1, varp, ALU.mult)
                            v.tensor_scalar(t1, t1, -0.5, 1.5, ALU.mult, ALU.add)
                            v.tensor_tensor(inv, inv, t1, ALU.mult)
                        sST2.inc(v.tensor_tensor(mean, mean, inv, ALU.mult))
                        sBC.w(v, tt + 1)
                        for dc in range(DC):
                            i = tt * DC + dc
                            if i >= 2:
                                sOSs[i % 2].w(v, 16 * (i // 2))
                            v.tensor_tensor(ob[:, (i % 2) * W_:(i % 2 + 1) * W_],
                                            xh[:, dc * W_:(dc + 1) * W_],
                                            psa[:, :], ALU.mult)
                            sAPP.inc(v.tensor_tensor(
                                ob[:, (i % 2) * W_:(i % 2 + 1) * W_],
                                ob[:, (i % 2) * W_:(i % 2 + 1) * W_],
                                psb[:, :], ALU.subtract))

                @blk.gpsimd
                def _(g):
                    for tt in range(ntt):
                        for dc in range(DC):
                            i = tt * DC + dc
                            sAPP.w(g, i + 1)
                            sOSs[i % 2].inc(g.dma_start(
                                bass.AP(dst_dram.tensor,
                                        dst_dram.offset + dc * 128 * T + tt * W_,
                                        [[T, 128], [1, W_]]),
                                ob[:, (i % 2) * W_:(i % 2 + 1) * W_]), 16)
                    w(g, sOSs[0])
                    w(g, sOSs[1])
            bar()

    vpid = [0]

    def emit_v(vsb, wv_src, bias_bc):
        vpid[0] += 1
        tag = "vp"
        utag = f"vp{vpid[0]}"
        sWL, sXT, sM, sE, sZ = (C(tag + s) for s in ("wl", "xt", "mm", "ev", "z"))
        with (
            nc.sbuf_tensor(utag + "_wst", [128, 32 * 256], F32) as wst_g,
            nc.sbuf_tensor(utag + "_xt", [128, DC * 512], F32) as xt_g,
            nc.psum_tensor(utag + "_ps0", [128, 256], F32) as ps0_g,
            nc.psum_tensor(utag + "_ps1", [128, 256], F32) as ps1_g,
        ):
            wst = wst_g.ap()
            xt = xt_g.ap()
            psv = [ps0_g.ap(), ps1_g.ap()]
            with nc.Block() as blk:
                @blk.sync
                def _(sy):
                    for sp in range(4):
                        for dc in range(DC):
                            sWL.inc(sy.dma_start(
                                wst[:, (sp * DC + dc) * 256:(sp * DC + dc + 1) * 256],
                                wv_src(sp, dc)), 16)
                    for tt in range(4):
                        if tt >= 1:
                            sM.w(sy, tt * 16)
                        sXT.inc(sy.dma_start(
                            bass.AP(xt.tensor, xt.offset,
                                    [[DC * 512, 128], [512, DC], [1, 512]]),
                            bass.AP(xln_dram.tensor,
                                    xln_dram.offset + tt * 512,
                                    [[T, 128], [128 * T, DC], [1, 512]])), 16)

                @blk.gpsimd
                def _(g):
                    sZ.inc(g.memset(vsb[:, :], 1.0))

                @blk.tensor
                def _(t):
                    w(t, sWL)
                    j = 0
                    for tt in range(4):
                        sXT.w(t, 16 * (tt + 1))
                        for ksub in range(4):
                            for sp in range(4):
                                if j >= 2:
                                    sE.w(t, j - 1)
                                ins = None
                                for dc in range(DC):
                                    ins = t.matmul(
                                        psv[j % 2][:, :],
                                        xt[:, dc * 512 + ksub * 128:
                                           dc * 512 + (ksub + 1) * 128],
                                        wst[:, (sp * DC + dc) * 256:
                                            (sp * DC + dc + 1) * 256],
                                        start=(dc == 0), stop=(dc == DC - 1))
                                sM.inc(ins)
                                j += 1

                @blk.vector
                def _(v):
                    sZ.w(v, 1)
                    j = 0
                    for tt in range(4):
                        for ksub in range(4):
                            kt = tt * 4 + ksub
                            for sp in range(4):
                                sM.w(v, j + 1)
                                dst = bass.AP(vsb.tensor,
                                              vsb.offset + kt * 1040 + sp * 260,
                                              [[16640, 128], [65, 4], [1, 64]])
                                if bias_bc is None:
                                    sE.inc(v.tensor_copy(dst, psv[j % 2][:, :]))
                                else:
                                    sE.inc(v.tensor_tensor(
                                        dst, psv[j % 2][:, :],
                                        bias_bc[:, sp * 256:(sp + 1) * 256], ALU.add))
                                j += 1
            bar()

    qkid = [0]

    def emit_qk(ksb, chunks, wgroup_src, bias_for, q_dram_ap):
        qkid[0] += 1
        tag = "qk"
        utag = f"qk{qkid[0]}"
        sXT, sM, sE = C(tag + "xt"), C(tag + "mm"), C(tag + "ev")
        sWs = [C(tag + "wl0"), C(tag + "wl1")]
        sQs = [C(tag + "qs0"), C(tag + "qs1")]
        nch = len(chunks)
        with (
            nc.sbuf_tensor(utag + "_wst", [128, 2 * 8 * 128], F32) as wst_g,
            nc.sbuf_tensor(utag + "_xt", [128, DC * 512], F32) as xt_g,
            nc.sbuf_tensor(utag + "_qst", [128, 2 * 512], F32) as qst_g,
            nc.psum_tensor(utag + "_ps0", [128, 512], F32) as ps0_g,
            nc.psum_tensor(utag + "_ps1", [128, 512], F32) as ps1_g,
        ):
            wst, xt, qst = wst_g.ap(), xt_g.ap(), qst_g.ap()
            psq = [ps0_g.ap(), ps1_g.ap()]
            with nc.Block() as blk:
                @blk.sync
                def _(sy):
                    for tt in range(4):
                        if tt >= 1:
                            sM.w(sy, tt * nch)
                        sXT.inc(sy.dma_start(
                            bass.AP(xt.tensor, xt.offset,
                                    [[DC * 512, 128], [512, DC], [1, 512]]),
                            bass.AP(xln_dram.tensor,
                                    xln_dram.offset + tt * 512,
                                    [[T, 128], [128 * T, DC], [1, 512]])), 16)
                        for ci, (mch, kind) in enumerate(chunks):
                            j = tt * nch + ci
                            if j >= 2:
                                sM.w(sy, j - 1)
                            sWs[j % 2].inc(sy.dma_start(
                                bass.AP(wst.tensor,
                                        wst.offset + (j % 2) * 1024,
                                        [[2 * 1024, 128], [128, 8], [1, 128]]),
                                wgroup_src(mch, kind)), 16)

                @blk.tensor
                def _(t):
                    for tt in range(4):
                        sXT.w(t, 16 * (tt + 1))
                        for ci, (mch, kind) in enumerate(chunks):
                            j = tt * nch + ci
                            if j >= 2:
                                sE.w(t, j - 1)
                            sWs[j % 2].w(t, 16 * (j // 2 + 1))
                            ins = None
                            for dc in range(DC):
                                ins = t.matmul(
                                    psq[j % 2][:, :],
                                    wst[:, (j % 2) * 1024 + dc * 128:
                                        (j % 2) * 1024 + (dc + 1) * 128],
                                    xt[:, dc * 512:(dc + 1) * 512],
                                    start=(dc == 0), stop=(dc == DC - 1))
                            sM.inc(ins)

                @blk.scalar
                def _(sc):
                    jq = 0
                    for tt in range(4):
                        for ci, (mch, kind) in enumerate(chunks):
                            j = tt * nch + ci
                            sM.w(sc, j + 1)
                            bias = bias_for(mch, kind)
                            if kind == "k":
                                dst = ksb[:, mch * T + tt * 512: mch * T + (tt + 1) * 512]
                            else:
                                if jq >= 2:
                                    sQs[jq % 2].w(sc, 16 * (jq // 2))
                                dst = qst[:, (jq % 2) * 512:(jq % 2 + 1) * 512]
                            if bias is None:
                                sE.inc(sc.copy(dst, psq[j % 2][:, :]))
                            else:
                                sE.inc(sc.activation(dst, psq[j % 2][:, :],
                                                     AF.Identity, bias=bias))
                            if kind == "q":
                                jq += 1

                @blk.gpsimd
                def _(g):
                    jq = 0
                    for tt in range(4):
                        for ci, (mch, kind) in enumerate(chunks):
                            j = tt * nch + ci
                            if kind != "q":
                                continue
                            sE.w(g, j + 1)
                            sQs[jq % 2].inc(g.dma_start(
                                q_dram_ap[mch * 128:(mch + 1) * 128,
                                          tt * 512:(tt + 1) * 512],
                                qst[:, (jq % 2) * 512:(jq % 2 + 1) * 512]), 16)
                            jq += 1
                    w(g, sQs[0])
                    w(g, sQs[1])
            bar()

    atid = [0]

    def emit_attention(layer, ksb, vsb, q_dram_ap, band_dram, ctxl_dram):
        atid[0] += 1
        tag = "at"
        utag = f"at{atid[0]}"
        NH = 32 if layer == 0 else 16
        sBD, sSM, sBA, sEX, sPV, sRC, sBC2, sCX = (
            C(tag + s) for s in
            ("bd", "sm", "ba", "ex", "pv", "rc", "bc", "cx"))
        sQLs = [C(tag + "ql0"), C(tag + "ql1")]
        sCSs = [C(tag + "cs0"), C(tag + "cs1")]
        npair = [(qt + 1) if layer == 0 else 8 for qt in range(NQT)]
        pairs = []
        cum = {}
        tot = 0
        for qt in range(NQT):
            for h in range(NH):
                gi = qt * NH + h
                cum[gi] = tot
                for kp in range(npair[qt]):
                    pairs.append((tot + kp, gi, qt, h, kp))
                tot += npair[qt]
        bd_cum = [0]
        for qt in range(NQT):
            bd_cum.append(bd_cum[-1] + 2 * npair[qt])

        def kv_of(h):
            g = h // 2 if layer == 0 else h
            return g // 2, (g % 2) * 64, 65 * g

        with (
            nc.sbuf_tensor(utag + "_band", [128, 16 * QT], F32) as band_g,
            nc.sbuf_tensor(utag + "_q", [128, 2 * QT], F32) as q_g,
            nc.sbuf_tensor(utag + "_p", [128, 2 * 512], F32) as p_g,
            nc.sbuf_tensor(utag + "_rin", [1, QT], F32) as rin_g,
            nc.sbuf_tensor(utag + "_cst", [64, 2 * QT], F32) as cst_g,
            nc.sbuf_tensor(utag + "_rbs", [64, QT], F32) as rbs_g,
            nc.psum_tensor(utag + "_s0", [128, 512], F32) as s0_g,
            nc.psum_tensor(utag + "_s1", [128, 512], F32) as s1_g,
            nc.psum_tensor(utag + "_c0", [65, QT], F32) as c0_g,
            nc.psum_tensor(utag + "_c1", [65, QT], F32) as c1_g,
            nc.psum_tensor(utag + "_rb", [64, QT], F32) as rb_g,
        ):
            band = band_g.ap()
            qsb = q_g.ap()
            psb = p_g.ap()
            rin = rin_g.ap()
            cst = cst_g.ap()
            rbs = rbs_g.ap()
            spsum = [s0_g.ap(), s1_g.ap()]
            cpsum = [c0_g.ap(), c1_g.ap()]
            rbps = rb_g.ap()

            with nc.Block() as blk:
                @blk.sync
                def _(sy):
                    for qt in range(NQT):
                        if qt >= 1:
                            sBA.w(sy, NH * (bd_cum[qt] // 2))
                        for kc in range(2 * npair[qt]):
                            c_off = qt * QT - kc * 128 + BC0
                            sBD.inc(sy.dma_start(
                                band[:, kc * QT:(kc + 1) * QT],
                                bass.AP(band_dram.tensor, band_dram.offset + c_off,
                                        [[BW, 128], [1, QT]])), 16)
                        for h in range(NH):
                            gi = qt * NH + h
                            kvoff_h = kv_of(h)[1]
                            if gi >= 2:
                                sSM.w(sy, cum[gi - 1])
                            sQLs[gi % 2].inc(sy.dma_start(
                                bass.AP(qsb.tensor,
                                        qsb.offset + kvoff_h * (2 * QT)
                                        + (gi % 2) * QT,
                                        [[2 * QT, 64], [1, QT]]),
                                q_dram_ap[h * 64:(h + 1) * 64,
                                          qt * QT:(qt + 1) * QT]), 16)

                @blk.tensor
                def _(t):
                    def emit_pv(pr):
                        i, gi, qt, h, kp = pr
                        kvc, kvoff, vcol = kv_of(h)
                        sEX.w(t, i + 1)
                        if kp == 0 and gi >= 2:
                            sCX.w(t, gi - 1)
                        ins = None
                        for half in range(2):
                            kc = kp * 2 + half
                            lhs = bass.AP(vsb.tensor, vsb.offset + kc * 1040 + vcol,
                                          [[16640, 128], [1, 65]])
                            ins = t.matmul(
                                cpsum[gi % 2][:, :], lhs,
                                psb[:, (i % 2) * 512 + half * QT:
                                    (i % 2) * 512 + (half + 1) * QT],
                                start=(kp == 0 and half == 0),
                                stop=(kp == npair[qt] - 1 and half == 1))
                        sPV.inc(ins)
                        if kp == npair[qt] - 1:
                            sRC.w(t, gi + 1)
                            if gi >= 1:
                                sCX.w(t, gi)
                            sBC2.inc(t.matmul(rbps[:, :], ones1[0:1, 0:64],
                                              rin[:, :], start=True, stop=True))

                    for idx, pr in enumerate(pairs):
                        i, gi, qt, h, kp = pr
                        kvc, kvoff, vcol = kv_of(h)
                        if kp == 0:
                            sQLs[gi % 2].w(t, 16 * (gi // 2 + 1))
                        if i >= 2:
                            sEX.w(t, i - 1)
                        for half in range(2):
                            kc = kp * 2 + half
                            lhs = bass.AP(ksb.tensor,
                                          ksb.offset + kvoff * (DC * T) + kvc * T + kc * 128,
                                          [[DC * T, 64], [1, 128]])
                            rhsq = bass.AP(qsb.tensor,
                                           qsb.offset + kvoff * (2 * QT)
                                           + (gi % 2) * QT,
                                           [[2 * QT, 64], [1, QT]])
                            ins = t.matmul(
                                spsum[i % 2][:, half * QT:(half + 1) * QT], lhs,
                                rhsq, start=True, stop=True)
                        sSM.inc(ins)
                        if idx >= 1:
                            emit_pv(pairs[idx - 1])
                    emit_pv(pairs[-1])

                @blk.vector
                def _(v):
                    for idx, pr in enumerate(pairs):
                        i, gi, qt, h, kp = pr
                        if kp == 0 and h == 0:
                            sBD.w(v, 16 * bd_cum[qt + 1])
                        sSM.w(v, i + 1)
                        sps = spsum[i % 2]
                        if layer == 0:
                            ins = v.scalar_tensor_tensor(
                                sps[:, :], band[:, kp * 2 * QT:(kp * 2 + 2) * QT],
                                slopes8[h], sps[:, :], ALU.mult, ALU.add)
                        else:
                            ins = v.tensor_tensor(
                                sps[:, :], sps[:, :],
                                band[:, kp * 2 * QT:(kp * 2 + 2) * QT], ALU.add)
                        sBA.inc(ins)
                        if kp == npair[qt] - 1:
                            sPV.w(v, cum[gi] + npair[qt])
                            sRC.inc(v.reciprocal(rin[:, :], cpsum[gi % 2][64:65, :]))
                            sBC2.w(v, gi + 1)
                            if gi >= 2:
                                sCSs[gi % 2].w(v, 16 * (gi // 2))
                            v.tensor_copy(rbs[:, :], rbps[:, :])
                            sCX.inc(v.tensor_tensor(
                                cst[:, (gi % 2) * QT:(gi % 2 + 1) * QT],
                                cpsum[gi % 2][0:64, :], rbs[:, :], ALU.mult))

                @blk.scalar
                def _(sc):
                    for idx, pr in enumerate(pairs):
                        i, gi, qt, h, kp = pr
                        sBA.w(sc, i + 1)
                        if i >= 2:
                            sPV.w(sc, i - 1)
                        sEX.inc(sc.activation(
                            psb[:, (i % 2) * 512:(i % 2 + 1) * 512],
                            spsum[i % 2][:, :], AF.Exp, scale=0.125))

                @blk.gpsimd
                def _(g):
                    for qt in range(NQT):
                        for h in range(NH):
                            gi = qt * NH + h
                            sCX.w(g, gi + 1)
                            sCSs[gi % 2].inc(g.dma_start(
                                ctxl_dram[h * 64:(h + 1) * 64, qt * QT:(qt + 1) * QT],
                                cst[:, (gi % 2) * QT:(gi % 2 + 1) * QT]), 16)
                    w(g, sCSs[0])
                    w(g, sCSs[1])
            bar()

    def emit_wout0():
        sCL, sM, sE = C("w0cl"), C("w0mm"), C("w0ev")
        sWs = [C("w0wl0"), C("w0wl1")]
        sXs = [C("w0xl0"), C("w0xl1")]
        sOs = [C("w0os0"), C("w0os1")]
        with (
            nc.sbuf_tensor("w0_ctx", [128, 16 * 512], F32) as ctx_g,
            nc.sbuf_tensor("w0_wst", [128, 2 * 16 * 128], F32) as wst_g,
            nc.sbuf_tensor("w0_xr", [128, 2 * 512], F32) as xr_g,
            nc.sbuf_tensor("w0_ob", [128, 2 * 512], F32) as ob_g,
            nc.psum_tensor("w0_ps0", [128, 512], F32) as p0_g,
            nc.psum_tensor("w0_ps1", [128, 512], F32) as p1_g,
        ):
            ctxt = ctx_g.ap()
            wst = wst_g.ap()
            xr = xr_g.ap()
            ob = ob_g.ap()
            pj = [p0_g.ap(), p1_g.ap()]
            with nc.Block() as blk:
                @blk.sync
                def _(sy):
                    for tt in range(4):
                        if tt >= 1:
                            sM.w(sy, tt * 8)
                        sCL.inc(sy.dma_start(
                            bass.AP(ctxt.tensor, ctxt.offset,
                                    [[16 * 512, 128], [512, 16], [1, 512]]),
                            bass.AP(ctx0_dram.tensor,
                                    ctx0_dram.offset + tt * 512,
                                    [[T, 128], [128 * T, 16], [1, 512]])), 16)
                        for dc in range(DC):
                            j = tt * 8 + dc
                            if j >= 2:
                                sE.w(sy, j - 1)
                            sXs[j % 2].inc(sy.dma_start(
                                xr[:, (j % 2) * 512:(j % 2 + 1) * 512],
                                x_in[dc * 128:(dc + 1) * 128,
                                     tt * 512:(tt + 1) * 512]), 16)
                            if j >= 2:
                                sM.w(sy, j - 1)
                            sWs[j % 2].inc(sy.dma_start(
                                bass.AP(wst.tensor,
                                        wst.offset + (j % 2) * 2048,
                                        [[2 * 2048, 128], [128, 16], [1, 128]]),
                                bass.AP(w_out.tensor, w_out.offset + dc * 128,
                                        [[D, 128], [128 * D, 16], [1, 128]])), 16)

                @blk.tensor
                def _(t):
                    for tt in range(4):
                        sCL.w(t, 16 * (tt + 1))
                        for dc in range(DC):
                            j = tt * 8 + dc
                            if j >= 2:
                                t_ = sE.w(t, j - 1)
                            sWs[j % 2].w(t, 16 * (j // 2 + 1))
                            ins = None
                            for pr in range(16):
                                ins = t.matmul(pj[j % 2][:, :],
                                               wst[:, (j % 2) * 2048 + pr * 128:
                                                   (j % 2) * 2048 + (pr + 1) * 128],
                                               ctxt[:, pr * 512:(pr + 1) * 512],
                                               start=(pr == 0), stop=(pr == 15))
                            sM.inc(ins)

                @blk.vector
                def _(v):
                    for tt in range(4):
                        for dc in range(DC):
                            j = tt * 8 + dc
                            sM.w(v, j + 1)
                            sXs[j % 2].w(v, 16 * (j // 2 + 1))
                            if j >= 2:
                                sOs[j % 2].w(v, 16 * (j // 2))
                            sE.inc(v.tensor_tensor(
                                ob[:, (j % 2) * 512:(j % 2 + 1) * 512],
                                pj[j % 2][:, :],
                                xr[:, (j % 2) * 512:(j % 2 + 1) * 512], ALU.add))

                @blk.gpsimd
                def _(g):
                    for tt in range(4):
                        for dc in range(DC):
                            j = tt * 8 + dc
                            sE.w(g, j + 1)
                            sOs[j % 2].inc(g.dma_start(
                                bass.AP(x1_dram.tensor,
                                        x1_dram.offset + dc * 128 * T + tt * 512,
                                        [[T, 128], [1, 512]]),
                                ob[:, (j % 2) * 512:(j % 2 + 1) * 512]), 16)
                    w(g, sOs[0])
                    w(g, sOs[1])
            bar()

    def emit_cb():
        sWL, sXT, sM, sE = C("cbwl"), C("cbxt"), C("cbmm"), C("cbev")
        sXs = [C("cbxr0"), C("cbxr1")]
        sOs = [C("cbos0"), C("cbos1")]
        with (
            nc.sbuf_tensor("cbt_sb", [128, DC * 1024], F32) as cbt_g,
            nc.sbuf_tensor("cb_xt", [128, DC * 512], F32) as xt_g,
            nc.sbuf_tensor("cb_xr", [128, 2 * 512], F32) as xr_g,
            nc.sbuf_tensor("cb_ob", [128, 2 * 512], F32) as ob_g,
            nc.sbuf_tensor("cb_obt", [128, 2 * 512], F32) as obt_g,
            nc.psum_tensor("cb_ps0", [128, 512], F32) as cps0,
            nc.psum_tensor("cb_ps1", [128, 512], F32) as cps1,
        ):
            cbtsb = cbt_g.ap()
            xt = xt_g.ap()
            xr = xr_g.ap()
            ob = ob_g.ap()
            obt = obt_g.ap()
            cps = [cps0.ap(), cps1.ap()]
            with nc.Block() as blk:
                @blk.sync
                def _(sy):
                    for dc in range(DC):
                        srcw = bass.AP(cbt.tensor, cbt.offset + dc * 128 * D,
                                       [[D, 128], [1, D]])
                        sWL.inc(sy.dma_start(cbtsb[:, dc * 1024:(dc + 1) * 1024],
                                             srcw), 16)
                    for tt in range(4):
                        if tt >= 1:
                            sM.w(sy, tt * 8)
                        sXT.inc(sy.dma_start(
                            bass.AP(xt.tensor, xt.offset,
                                    [[DC * 512, 128], [512, DC], [1, 512]]),
                            bass.AP(xln_dram.tensor,
                                    xln_dram.offset + tt * 512,
                                    [[T, 128], [128 * T, DC], [1, 512]])), 16)
                        for nc_ in range(8):
                            j = tt * 8 + nc_
                            if j >= 2:
                                sE.w(sy, j - 1)
                            sXs[j % 2].inc(sy.dma_start(
                                xr[:, (j % 2) * 512:(j % 2 + 1) * 512],
                                bass.AP(x1_dram.tensor,
                                        x1_dram.offset + nc_ * 128 * T + tt * 512,
                                        [[T, 128], [1, 512]])), 16)

                @blk.tensor
                def _(t):
                    w(t, sWL)
                    for tt in range(4):
                        sXT.w(t, 16 * (tt + 1))
                        for nc_ in range(8):
                            j = tt * 8 + nc_
                            if j >= 2:
                                sE.w(t, j - 1)
                            ins = None
                            for dc in range(DC):
                                ins = t.matmul(
                                    cps[j % 2][:, :],
                                    cbtsb[:, dc * 1024 + nc_ * 128:
                                          dc * 1024 + (nc_ + 1) * 128],
                                    xt[:, dc * 512:(dc + 1) * 512],
                                    start=(dc == 0), stop=(dc == DC - 1))
                            sM.inc(ins)

                @blk.vector
                def _(v):
                    for tt in range(4):
                        for nc_ in range(8):
                            j = tt * 8 + nc_
                            sM.w(v, j + 1)
                            sXs[j % 2].w(v, 16 * (j // 2 + 1))
                            if j >= 2:
                                sOs[j % 2].w(v, 16 * (j // 2))
                            v.tensor_copy(obt[:, (j % 2) * 512:(j % 2 + 1) * 512],
                                          cps[j % 2][:, :])
                            v.tensor_scalar(ob[:, (j % 2) * 512:(j % 2 + 1) * 512],
                                            obt[:, (j % 2) * 512:(j % 2 + 1) * 512],
                                            scbq_sb[:, nc_:nc_ + 1],
                                            cbias_sb[:, nc_:nc_ + 1],
                                            ALU.mult, ALU.add)
                            sE.inc(v.tensor_tensor(
                                ob[:, (j % 2) * 512:(j % 2 + 1) * 512],
                                ob[:, (j % 2) * 512:(j % 2 + 1) * 512],
                                xr[:, (j % 2) * 512:(j % 2 + 1) * 512], ALU.add))

                @blk.gpsimd
                def _(g):
                    for tt in range(4):
                        for nc_ in range(8):
                            j = tt * 8 + nc_
                            sE.w(g, j + 1)
                            sOs[j % 2].inc(g.dma_start(
                                bass.AP(x2_dram.tensor,
                                        x2_dram.offset + nc_ * 128 * T + tt * 512,
                                        [[T, 128], [1, 512]]),
                                ob[:, (j % 2) * 512:(j % 2 + 1) * 512]), 16)
                    w(g, sOs[0])
                    w(g, sOs[1])
            bar()

    def emit_bvbc(bvbc):
        sB = C("bvb")
        with (
            nc.sbuf_tensor("bvrow", [1, D], F32) as bvr_g,
            nc.psum_tensor("bv_ps", [128, 512], F32) as bvps_g,
        ):
            bvr = bvr_g.ap()
            bvps = bvps_g.ap()
            with nc.Block() as blk:
                @blk.sync
                def _(sy):
                    sB.inc(sy.dma_start(bvr[:, :], bv_d[:]), 16)

                @blk.tensor
                def _(t):
                    sB.w(t, 16)
                    sB.inc(t.matmul(bvps[:, :], ones1[:, :], bvr[:, 0:512],
                                    start=True, stop=True))
                    sB.w(t, 18)
                    sB.inc(t.matmul(bvps[:, :], ones1[:, :], bvr[:, 512:1024],
                                    start=True, stop=True))

                @blk.vector
                def _(v):
                    sB.w(v, 17)
                    sB.inc(v.tensor_copy(bvbc[:, 0:512], bvps[:, :]))
                    sB.w(v, 19)
                    v.tensor_copy(bvbc[:, 512:1024], bvps[:, :])
            bar()

    def emit_band1():
        sBP = C("bprol")
        sBPg = C("bprolg")
        with (
            nc.sbuf_tensor("brsb", [1, 4095], F32) as brsb_g,
            nc.sbuf_tensor("etsb", [1, 2], F32) as etsb_g,
        ):
            brsb = brsb_g.ap()
            etsb = etsb_g.ap()
            with nc.Block() as blk:
                @blk.sync
                def _(sy):
                    sBP.inc(sy.dma_start(brsb[:, :], bandrev1[:]), 16)
                    sBP.inc(sy.dma_start(etsb[:, 0:1], etg[:, :]), 16)
                    sBP.inc(sy.dma_start(etsb[:, 1:2], isg[:, :]), 16)

                @blk.scalar
                def _(sc):
                    sBP.w(sc, 48)
                    sBP.inc(sc.activation(etsb[:, 0:1], etsb[:, 0:1], AF.Sigmoid))

                @blk.vector
                def _(v):
                    sBP.w(v, 49)
                    v.tensor_single_scalar(etsb[:, 0:1], etsb[:, 0:1], 8.0, ALU.mult)
                    sBP.inc(v.tensor_single_scalar(etsb[:, 1:2], etsb[:, 1:2], 8.0,
                                                   ALU.mult))
                    sBP.w(v, 50)
                    sBP.inc(v.tensor_scalar(brsb[:, :], brsb[:, :], etsb[:, 0:1],
                                            etsb[:, 1:2], ALU.mult, ALU.add))

                @blk.gpsimd
                def _(g):
                    sBP.w(g, 51)
                    sBPg.inc(g.dma_start(bdev_dram[:], brsb[:, :]), 16)
                    sBPg.w(g, 16)
                    for p in range(128):
                        sBPg.inc(g.dma_start(
                            bm1_dram[p:p + 1, :],
                            bass.AP(bdev_dram.tensor, bdev_dram.offset + 127 - p,
                                    [[1, 1], [1, BW]])), 16)
                    w(g, sBPg)
            bar()

    def emit_scramble_proj():
        sCL, sM, sE = C("p12cl"), C("p12mm"), C("p12ev")
        sCLF = C("p12clf")
        sWs = [C("p12wl0"), C("p12wl1")]
        sXs = [C("p12xl0"), C("p12xl1")]
        sSs = [C("p12st0"), C("p12st1")]
        with (
            nc.sbuf_tensor("csrc", [128, 8 * T], F32) as csrc_g,
            nc.sbuf_tensor("wost", [128, 2 * 2048], F32) as wost_g,
            nc.sbuf_tensor("x2st", [128, 2 * T], F32) as x2st_g,
            nc.sbuf_tensor("x3st", [128, 2 * T], F32) as x3st_g,
            nc.psum_tensor("p12_ps", [128, 2048], F32) as ps_g,
        ):
            csrc = csrc_g.ap()
            wost = wost_g.ap()
            x2st = x2st_g.ap()
            x3st = x3st_g.ap()
            ps = ps_g.ap()
            with nc.Block() as blk:
                @blk.sync
                def _(sy):
                    for di_, dc in enumerate([1, 2, 3, 4, 5, 6, 7, 0]):
                        if di_ >= 2:
                            sM.w(sy, di_ - 1)
                        for uh in range(2):
                            srcw = bass.AP(woT.tensor, woT.offset + dc * 128,
                                           [[D, 64], [64 * D, 16], [1, 128]])
                            sWs[di_ % 2].inc(sy.dma_start(
                                bass.AP(wost.tensor,
                                        wost.offset + uh * 64 * (2 * 2048)
                                        + (di_ % 2) * 2048,
                                        [[2 * 2048, 64], [128, 16], [1, 128]]),
                                srcw), 16)
                        if di_ >= 2:
                            sE.w(sy, di_ - 1)
                        sXs[di_ % 2].inc(sy.dma_start(
                            x2st[:, (di_ % 2) * T:(di_ % 2 + 1) * T],
                            x2_dram[dc * 128:(dc + 1) * 128, :]), 16)

                @blk.tensor
                def _(t):
                    w(t, sCL)
                    sCLF.w(t, 16)
                    for di_, dc in enumerate([1, 2, 3, 4, 5, 6, 7, 0]):
                        sWs[di_ % 2].w(t, 32 * (di_ // 2 + 1))
                        if di_ >= 1:
                            sE.w(t, di_)
                        ins = None
                        for s in range(16):
                            for u in range(2):
                                lhs = bass.AP(wost.tensor,
                                              wost.offset + u * 64 * (2 * 2048)
                                              + (di_ % 2) * 2048 + s * 128,
                                              [[2 * 2048, 64], [1, 128]])
                                for m in range(8):
                                    k = u * 8 + m
                                    rhs = bass.AP(csrc.tensor,
                                                  csrc.offset + m * T
                                                  + (u * 64) * (8 * T) + s,
                                                  [[8 * T, 64], [16, 128]])
                                    ins = t.matmul(
                                        ps[:, k * 128:(k + 1) * 128], lhs, rhs,
                                        start=(s == 0 and k % 4 == 0),
                                        stop=(s == 15))
                        sM.inc(ins)

                @blk.vector
                def _(v):
                    for di_, dc in enumerate([1, 2, 3, 4, 5, 6, 7, 0]):
                        sM.w(v, di_ + 1)
                        sXs[di_ % 2].w(v, 16 * (di_ // 2 + 1))
                        if di_ >= 2:
                            sSs[di_ % 2].w(v, 16 * (di_ // 2))
                        sE.inc(v.scalar_tensor_tensor(
                            x3st[:, (di_ % 2) * T:(di_ % 2 + 1) * T],
                            ps[:, :], bo_sb[:, dc:dc + 1],
                            x2st[:, (di_ % 2) * T:(di_ % 2 + 1) * T],
                            ALU.add, ALU.add))

                @blk.gpsimd
                def _(g):
                    with g.register("p12r") as r12:
                        pid = g.partition_id()
                        for m in range(8):
                            g.reg_mul(r12, pid, 128 * T)
                            g.reg_add(r12, r12, m * D * T)
                            sCL.inc(g.dma_start(
                                csrc[:, m * T:(m + 1) * T],
                                bass.AP(agctx.tensor, r12,
                                        [[T, 128], [1, T]])), 16)
                    sCL.w(g, 128)
                    sCLF.inc(g.dma_start(x3st[:, 0:4], x2st[:, 0:4]), 16)
                    if "d_csrc" in dump_aps:
                        sDc = C("dmpg")
                        sDc.inc(g.dma_start(dump_aps["d_csrc"][:, :], csrc[:, :]), 16)
                        w(g, sDc)
                    for di_, dc in enumerate([1, 2, 3, 4, 5, 6, 7, 0]):
                        sE.w(g, di_ + 1)
                        sSs[di_ % 2].inc(g.dma_start(
                            x3_dram[dc * 128:(dc + 1) * 128, :],
                            x3st[:, (di_ % 2) * T:(di_ % 2 + 1) * T]), 16)
                    w(g, sSs[0])
                    w(g, sSs[1])
            bar()

    def emit_moe():
        TT = 512
        (sXL, sLSQ, sLSM, sLVA, sLRT, sLST, sLBC, sLAP) = (
            C("m" + s) for s in ("xl", "lsq", "lsm", "lva", "lrt", "lst", "lbc", "lap"))
        (sGM, sGB, sTR, sMK, sT2, sMF, sMB, sXE, sCU, sFW) = (
            C("m" + s) for s in
            ("gm", "gb", "tr", "mk", "t2", "mf", "mb", "xe", "cu", "fw"))
        (sHM, sGE, sW2T, sYE, sEV) = (
            C("m" + s) for s in ("hm", "ge", "w2t", "ye", "ev"))
        sW1s = [C("mw1a"), C("mw1b")]
        sW2s = [C("mw2a"), C("mw2b")]
        sOSs2 = [C("mosa"), C("mosb")]
        import contextlib
        with contextlib.ExitStack() as _st:
            x3_g = _st.enter_context(nc.sbuf_tensor("m_x3", [128, DC * TT], F32))
            x3ln_g = _st.enter_context(nc.sbuf_tensor("m_x3ln", [128, DC * TT], F32))
            xe_g = _st.enter_context(nc.sbuf_tensor("m_xe", [128, DC * TT], F32))
            h_g = _st.enter_context(nc.sbuf_tensor("m_h", [128, 32 * TT], F32))
            ysb_g = _st.enter_context(nc.sbuf_tensor("m_ysb", [128, DC * TT], F32))
            w1st_g = _st.enter_context(nc.sbuf_tensor("m_w1st", [128, 2 * 1024], F32))
            w2st_g = _st.enter_context(nc.sbuf_tensor("m_w2st", [128, 2 * 1024], F32))
            lg_g = _st.enter_context(nc.sbuf_tensor("m_lg", [8, TT], F32))
            mtk_g = _st.enter_context(nc.sbuf_tensor("m_mtk", [128, 8], F32))
            mtk2_g = _st.enter_context(nc.sbuf_tensor("m_mtk2", [128, 8], F32))
            gcap_g = _st.enter_context(nc.sbuf_tensor("m_gcap", [128, 32], F32))
            run_g = _st.enter_context(nc.sbuf_tensor("m_run", [128, 2], F32))
            mfm_g = _st.enter_context(nc.sbuf_tensor("m_mfm", [8, TT], F32))
            mbc_g = _st.enter_context(nc.sbuf_tensor("m_mbc", [128, TT], F32))
            ost_g = _st.enter_context(nc.sbuf_tensor("m_ost", [128, 2 * TT], F32))
            msq_g = _st.enter_context(nc.sbuf_tensor("m_sq", [128, DC * TT], F32))
            mmean_g = _st.enter_context(nc.sbuf_tensor("m_mean", [1, TT], F32))
            mvarp_g = _st.enter_context(nc.sbuf_tensor("m_varp", [1, TT], F32))
            mt1_g = _st.enter_context(nc.sbuf_tensor("m_t1", [1, TT], F32))
            minv_g = _st.enter_context(nc.sbuf_tensor("m_inv", [1, TT], F32))
            selsb_g = _st.enter_context(nc.sbuf_tensor("m_selsb", [8, 8 * 128], F32))
            id8_g = _st.enter_context(nc.sbuf_tensor("m_id8", [8, 8], F32))
            id128_g = _st.enter_context(nc.sbuf_tensor("m_id128", [128, 128], F32))
            wg_g = _st.enter_context(nc.sbuf_tensor("m_wg", [128, 64], F32))
            ucum_g = _st.enter_context(nc.sbuf_tensor("m_ucum", [8, 8], F32))
            ls_g = _st.enter_context(nc.psum_tensor("m_ls12", [33, TT], F32))
            gbc_g = _st.enter_context(nc.psum_tensor("m_gbc", [128, 512], F32))
            hp0_g = _st.enter_context(nc.psum_tensor("m_hps0", [128, 512], F32))
            hp1_g = _st.enter_context(nc.psum_tensor("m_hps1", [128, 512], F32))
            y0_g = _st.enter_context(nc.psum_tensor("m_y0", [128, 512], F32))
            y1_g = _st.enter_context(nc.psum_tensor("m_y1", [128, 512], F32))
            y2_g = _st.enter_context(nc.psum_tensor("m_y2", [128, 512], F32))
            y3_g = _st.enter_context(nc.psum_tensor("m_y3", [128, 512], F32))
            x3sb = x3_g.ap()
            x3ln = x3ln_g.ap()
            xesb = xe_g.ap()
            hsb = h_g.ap()
            ysb = ysb_g.ap()
            w1st = w1st_g.ap()
            w2st = w2st_g.ap()
            lg = lg_g.ap()
            mtk = mtk_g.ap()
            mtk2 = mtk2_g.ap()
            gcap = gcap_g.ap()
            runb = run_g.ap()
            mfm = mfm_g.ap()
            mbcsb = mbc_g.ap()
            ost = ost_g.ap()
            msq = msq_g.ap()
            mean, varp, t1v, inv = (mmean_g.ap(), mvarp_g.ap(), mt1_g.ap(),
                                    minv_g.ap())
            selsb = selsb_g.ap()
            id8 = id8_g.ap()
            id128 = id128_g.ap()
            wg_sb = wg_g.ap()
            ucum = ucum_g.ap()
            ls12 = ls_g.ap()
            gbc = gbc_g.ap()
            hps = [hp0_g.ap(), hp1_g.ap()]
            ypsl = [y0_g.ap(), y1_g.ap(), y2_g.ap(), y3_g.ap()]

            prol = C("mprol")
            with nc.Block() as blk:
                @blk.sync
                def _(sy):
                    prol.inc(sy.dma_start(selsb[:, :], sel_c[:, :]), 16)
                    prol.inc(sy.dma_start(id8[:, :], id8_c[:, :]), 16)
                    prol.inc(sy.dma_start(id128[:, :], id128_c[:, :]), 16)
                    prol.inc(sy.dma_start(ucum[:, :], ucum_c[:, :]), 16)
                    for dc in range(DC):
                        prol.inc(sy.dma_start(wg_sb[:, dc * 8:(dc + 1) * 8],
                                              wg_d[dc * 128:(dc + 1) * 128, :]), 16)
                    w(sy, prol)
            bar()

            for tt in range(4):
                with nc.Block() as blk:
                    @blk.sync
                    def _(sy):
                        if tt >= 1:
                            sEV.w(sy, tt * 8)
                            sXE.w(sy, tt * 8)
                        sXL.inc(sy.dma_start(
                            bass.AP(x3sb.tensor, x3sb.offset,
                                    [[DC * TT, 128], [TT, DC], [1, TT]]),
                            bass.AP(x3_dram.tensor,
                                    x3_dram.offset + tt * TT,
                                    [[T, 128], [128 * T, DC], [1, TT]])), 16)

                    @blk.scalar
                    def _(sc):
                        sXL.w(sc, 16 * (tt + 1))
                        for dc in range(DC):
                            i = tt * DC + dc
                            sLSQ.inc(sc.activation(
                                msq[:, dc * TT:(dc + 1) * TT],
                                x3sb[:, dc * TT:(dc + 1) * TT], AF.Square))
                        sLVA.w(sc, tt + 1)
                        sLRT.inc(sc.activation(t1v, varp, AF.Sqrt))

                    @blk.tensor
                    def _(t):
                        sXL.w(t, 16 * (tt + 1))
                        for dc in range(DC):
                            i = tt * DC + dc
                            t.matmul(ls12[0:1, :], onesc[:, :],
                                     x3sb[:, dc * TT:(dc + 1) * TT],
                                     start=(dc == 0), stop=(dc == DC - 1))
                            sLSQ.w(t, i + 1)
                            ins = t.matmul(ls12[32:33, :], onesc[:, :],
                                           msq[:, dc * TT:(dc + 1) * TT],
                                           start=(dc == 0), stop=(dc == DC - 1))
                        sLSM.inc(ins)
                        sLST.w(t, tt + 1)
                        t.matmul(gbc[:, :], ones1[:, :], inv, start=True, stop=True)
                        sLBC.inc(t.matmul(hps[0][:, :], ones1[:, :], mean,
                                          start=True, stop=True))

                    @blk.vector
                    def _(v):
                        sLSM.w(v, tt + 1)
                        v.tensor_single_scalar(mean, ls12[0:1, :], 1.0 / 1024.0,
                                               ALU.mult)
                        v.tensor_single_scalar(varp, ls12[32:33, :], 1.0 / 1024.0,
                                               ALU.mult)
                        v.tensor_tensor(t1v, mean, mean, ALU.mult)
                        v.tensor_tensor(varp, varp, t1v, ALU.subtract)
                        sLVA.inc(v.tensor_single_scalar(varp, varp, EPS, ALU.add))
                        sLRT.w(v, tt + 1)
                        v.reciprocal(inv, t1v)
                        for _i in range(2):
                            v.tensor_tensor(t1v, inv, inv, ALU.mult)
                            v.tensor_tensor(t1v, t1v, varp, ALU.mult)
                            v.tensor_scalar(t1v, t1v, -0.5, 1.5, ALU.mult, ALU.add)
                            v.tensor_tensor(inv, inv, t1v, ALU.mult)
                        sLST.inc(v.tensor_tensor(mean, mean, inv, ALU.mult))
                        sLBC.w(v, tt + 1)
                        ins = None
                        for dc in range(DC):
                            sl = slice(dc * TT, (dc + 1) * TT)
                            v.tensor_tensor(x3ln[:, sl], x3sb[:, sl],
                                            gbc[:, :], ALU.mult)
                            ins = v.tensor_tensor(x3ln[:, sl], x3ln[:, sl],
                                                  hps[0][:, :], ALU.subtract)
                        sLAP.inc(ins)

                    @blk.gpsimd
                    def _(g):
                        g.memset(ysb[:, :], 0.0)
                bar()
                gm0, gb0, tr0, mk0, t20, mf0 = (sGM.rval, sGB.rval, sTR.rval,
                                                sMK.rval, sT2.rval, sMF.rval)
                cu0, fw0 = sCU.rval, sFW.rval
                with nc.Block() as blk:
                    @blk.tensor
                    def _(t):
                        ins = None
                        for dc in range(DC):
                            ins = t.matmul(gbc[0:8, :], wg_sb[:, dc * 8:(dc + 1) * 8],
                                           x3ln[:, dc * TT:(dc + 1) * TT],
                                           start=(dc == 0), stop=(dc == DC - 1))
                        sGM.inc(ins)
                        for q in range(4):
                            sGB.w(t, gb0 + 1)
                            if q >= 1:
                                sFW.w(t, fw0 + q)
                                sMK.w(t, mk0 + 3 * q)
                            sTR.inc(t.transpose(gbc[0:128, 0:8],
                                                lg[:, q * 128:(q + 1) * 128],
                                                id8[:, :]))
                            sMK.w(t, mk0 + 3 * q + 3)
                            sT2.inc(t.transpose(gbc[0:8, 64:192],
                                                mtk[:, :], id128[:, :]))
                            sMF.w(t, mf0 + q + 1)
                            sCU.inc(t.matmul(gbc[0:8, 192:320], ucum[:, :],
                                             mfm[:, q * 128:(q + 1) * 128],
                                             start=True, stop=True))

                    @blk.scalar
                    def _(sc):
                        sGM.w(sc, gm0 + 1)
                        sGB.inc(sc.activation(lg[:, :], gbc[0:8, :], AF.Identity,
                                              bias=bg_sb[:, :]))
                        for q in range(4):
                            sT2.w(sc, t20 + q + 1)
                            sMF.inc(sc.copy(mfm[:, q * 128:(q + 1) * 128],
                                            gbc[0:8, 64:192]))

                    @blk.vector
                    def _(v):
                        for q in range(4):
                            sTR.w(v, tr0 + q + 1)
                            sMK.inc(v.tensor_copy(mtk2[:, :], gbc[0:128, 0:8]))
                            sMK.w(v, mk0 + 3 * q + 1)
                            sMK.inc(v.reduce_max(runb[:, 0:1], mtk2[:, :],
                                                 mybir.AxisListType.X))
                            sMK.w(v, mk0 + 3 * q + 2)
                            sMK.inc(v.tensor_single_scalar(mtk[:, :], mtk2[:, :],
                                                           runb[:, 0:1], ALU.is_ge))
                            sCU.w(v, cu0 + q + 1)
                            sFW.inc(v.scalar_tensor_tensor(
                                mfm[:, q * 128:(q + 1) * 128],
                                gbc[0:8, 192:320], 0.0,
                                mfm[:, q * 128:(q + 1) * 128],
                                ALU.is_equal, ALU.mult))
                            if "d_gdbg" in dump_aps and tt == 0 and q == 0:
                                v.tensor_copy(gcap[:, 0:8], mtk2[:, :])
                                v.tensor_copy(gcap[:, 8:9], runb[:, 0:1])
                                v.tensor_copy(gcap[:, 9:17], mtk[:, :])
                bar()
                if "d_mask" in dump_aps:
                    sD = C(f"dmask{tt}")
                    with nc.Block() as blk:
                        @blk.gpsimd
                        def _(g):
                            sD.inc(g.dma_start(
                                dump_aps["d_mask"][:, tt * TT:(tt + 1) * TT],
                                mfm[:, :]), 16)
                            w(g, sD)
                    bar()
                hm0, ge0, mb0, xe0 = sHM.rval, sGE.rval, sMB.rval, sXE.val
                w2t0, ye0 = sW2T.rval, sYE.rval
                w1s0 = [sW1s[0].rval, sW1s[1].rval]
                w2s0 = [sW2s[0].rval, sW2s[1].rval]
                with nc.Block() as blk:
                    @blk.sync
                    def _(sy):
                        for e in range(8):
                            for hc in range(32):
                                i = e * 32 + hc
                                if i >= 2:
                                    sHM.w(sy, hm0 + i - 1)
                                src = bass.AP(
                                    w1_d.tensor,
                                    w1_d.offset + e * D * 4096 + hc * 128,
                                    [[4096, 128], [4096 * 128, 8], [1, 128]])
                                sW1s[i % 2].inc(sy.dma_start(
                                    bass.AP(w1st.tensor,
                                            w1st.offset + (i % 2) * 1024,
                                            [[2 * 1024, 128], [128, 8], [1, 128]]),
                                    src), 16)

                    @blk.scalar
                    def _(sc):
                        for e in range(8):
                            for hc in range(32):
                                i = e * 32 + hc
                                sHM.w(sc, hm0 + i + 1)
                                sGE.inc(sc.activation(
                                    hsb[:, hc * TT:(hc + 1) * TT],
                                    hps[i % 2][:, :], AF.Gelu_apprx_tanh))

                    @blk.gpsimd
                    def _(g):
                        for e in range(8):
                            for half in range(2):
                                for hc in range(32):
                                    ii = (2 * e + half) * 32 + hc
                                    if ii >= 2:
                                        sW2T.w(g, w2t0 + ii - 1)
                                    src = bass.AP(
                                        w2_d.tensor,
                                        w2_d.offset + (e * 4096 + hc * 128) * D
                                        + half * 512,
                                        [[D, 128], [1, 512]])
                                    sW2s[ii % 2].inc(g.dma_start(
                                        w2st[:, (ii % 2) * 512:(ii % 2 + 1) * 512],
                                        src), 16)

                    @blk.tensor
                    def _(t):
                        for e in range(8):
                            sFW.w(t, fw0 + 4)
                            sLAP.w(t, tt + 1)
                            if e >= 1:
                                sXE.w(t, xe0 + e)
                            sMB.inc(t.matmul(gbc[0:128, :],
                                             selsb[:, e * 128:(e + 1) * 128],
                                             mfm[:, :], start=True, stop=True))
                            for hc in range(32):
                                i = e * 32 + hc
                                sW1s[i % 2].w(t, w1s0[i % 2] + 16 * (i // 2 + 1))
                                sXE.w(t, xe0 + e + 1)
                                if i >= 2:
                                    sGE.w(t, ge0 + i - 1)
                                ins = None
                                for dc in range(DC):
                                    ins = t.matmul(
                                        hps[i % 2][:, :],
                                        w1st[:, (i % 2) * 1024 + dc * 128:
                                             (i % 2) * 1024 + (dc + 1) * 128],
                                        xesb[:, dc * TT:(dc + 1) * TT],
                                        start=(dc == 0), stop=(dc == DC - 1))
                                sHM.inc(ins)
                            for half in range(2):
                                if 2 * e + half >= 1:
                                    sYE.w(t, ye0 + 2 * e + half)
                                for hc in range(32):
                                    ii = (2 * e + half) * 32 + hc
                                    sW2s[ii % 2].w(t, w2s0[ii % 2] + 16 * (ii // 2 + 1))
                                    sGE.w(t, ge0 + e * 32 + hc + 1)
                                    ins = None
                                    for di in range(4):
                                        dcw = half * 4 + di
                                        ins = t.matmul(
                                            ypsl[di][:, :],
                                            w2st[:, (ii % 2) * 512 + di * 128:
                                                 (ii % 2) * 512 + (di + 1) * 128],
                                            hsb[:, hc * TT:(hc + 1) * TT],
                                            start=(hc == 0), stop=(hc == 31))
                                    sW2T.inc(ins)

                    @blk.vector
                    def _(v):
                        for e in range(8):
                            sMB.w(v, mb0 + e + 1)
                            v.tensor_copy(mbcsb[:, :], gbc[0:128, :])
                            ins = None
                            for dc in range(DC):
                                ins = v.tensor_tensor(
                                    xesb[:, dc * TT:(dc + 1) * TT],
                                    x3ln[:, dc * TT:(dc + 1) * TT],
                                    mbcsb[:, :], ALU.mult)
                            sXE.inc(ins)
                            for half in range(2):
                                sW2T.w(v, w2t0 + 32 * (2 * e + half + 1))
                                ins = None
                                for di in range(4):
                                    dcw = half * 4 + di
                                    ins = v.tensor_tensor(
                                        ysb[:, dcw * TT:(dcw + 1) * TT],
                                        ysb[:, dcw * TT:(dcw + 1) * TT],
                                        ypsl[di][:, :], ALU.add)
                                sYE.inc(ins)
                bar()
                with nc.Block() as blk:
                    @blk.vector
                    def _(v):
                        for dc in range(DC):
                            j = tt * 8 + dc
                            if j >= 2:
                                sOSs2[j % 2].w(v, 16 * (j // 2))
                            sEV.inc(v.tensor_tensor(
                                ost[:, (j % 2) * TT:(j % 2 + 1) * TT],
                                ysb[:, dc * TT:(dc + 1) * TT],
                                x3sb[:, dc * TT:(dc + 1) * TT], ALU.add))

                    @blk.gpsimd
                    def _(g):
                        for dc in range(DC):
                            j = tt * 8 + dc
                            sEV.w(g, j + 1)
                            sOSs2[j % 2].inc(g.dma_start(
                                bass.AP(out_d.tensor,
                                        out_d.offset + dc * 128 * T + tt * TT,
                                        [[T, 128], [1, TT]]),
                                ost[:, (j % 2) * TT:(j % 2 + 1) * TT]), 16)
                bar()
            if "d_gdbg" in dump_aps:
                sDg = C("dmpg")
                with nc.Block() as blk:
                    @blk.gpsimd
                    def _(g):
                        sDg.inc(g.dma_start(dump_aps["d_gdbg"][:, :], gcap[:, :]), 16)
                        w(g, sDg)
                bar()
            with nc.Block() as blk:
                @blk.gpsimd
                def _(g):
                    w(g, sOSs2[0])
                    w(g, sOSs2[1])
            bar()

    if "sm0" in dumps:
        dump_smalls()
    emit_ln_stream(x_in, xln_dram)
    if "sm1" in dumps:
        dump_smalls()
    dump_rows("d_xln", xln_dram, D)
    with (
        nc.sbuf_tensor("vsb0", [128, 16 * 1040], F32) as vsb0_g,
        nc.sbuf_tensor("ksb0", [128, DC * T], F32) as ksb0_g,
    ):
        vsb0, ksb0 = vsb0_g.ap(), ksb0_g.ap()
        if upto >= 2:
            emit_v(vsb0,
                   lambda sp, dc: w_qkv[dc * 128:(dc + 1) * 128,
                                        3072 + sp * 256:3072 + (sp + 1) * 256],
                   None)
            chunks0 = [(m, "q") for m in range(16)] + [(m, "k") for m in range(8)]

            def wsrc0(mch, kind):
                c0 = mch * 128 if kind == "q" else 2048 + mch * 128
                return bass.AP(w_qkv.tensor, w_qkv.offset + c0,
                               [[4096, 128], [128 * 4096, 8], [1, 128]])

            emit_qk(ksb0, chunks0, wsrc0, lambda mch, kind: None, q0_dram)
            if "sm2" in dumps:
                dump_smalls()
            if "d_v" in dump_aps:
                sD = C("dmpg")
                with nc.Block() as blk:
                    @blk.gpsimd
                    def _(g):
                        sD.inc(g.dma_start(dump_aps["d_v"][:, :], vsb0[:, :]), 16)
                        w(g, sD)
                bar()
            if "d_k" in dump_aps:
                sD = C("dmpg")
                with nc.Block() as blk:
                    @blk.gpsimd
                    def _(g):
                        for dc in range(DC):
                            sD.inc(g.dma_start(
                                dump_aps["d_k"][dc * 128:(dc + 1) * 128, :],
                                ksb0[:, dc * T:(dc + 1) * T]), 16)
                        w(g, sD)
                bar()
            dump_rows("d_q0", q0_dram, 2048)
        if upto >= 3:
            emit_attention(0, ksb0, vsb0, q0_dram, bm0, ctx0_dram)
            if "sm3" in dumps:
                dump_smalls()
            dump_rows("d_ctx0", ctx0_dram, 2048)
    if upto >= 4:
        emit_wout0()
        if "sm4" in dumps:
            dump_smalls()
        dump_rows("d_x1", x1_dram, D)
    if upto >= 5:
        emit_ln_stream(x1_dram, xln_dram)
        dump_rows("d_xln2", xln_dram, D)
        emit_cb()
        dump_rows("d_x2", x2_dram, D)
    if upto >= 6:
        emit_ln_stream(x2_dram, xln_dram)
        with (
            nc.sbuf_tensor("vsb1", [128, 16 * 1040], F32) as vsb1_g,
            nc.sbuf_tensor("ksb1", [128, DC * T], F32) as ksb1_g,
            nc.sbuf_tensor("bvbc", [128, D], F32) as bvbc_g,
        ):
            vsb1, ksb1, bvbc = vsb1_g.ap(), ksb1_g.ap(), bvbc_g.ap()
            emit_bvbc(bvbc)
            emit_v(vsb1,
                   lambda sp, dc: wvT[dc * 128:(dc + 1) * 128,
                                      sp * 256:(sp + 1) * 256],
                   bvbc)
            chunks1 = [(m, "q") for m in range(8)] + [(m, "k") for m in range(8)]

            def wsrc1(mch, kind):
                wsrc = wqT if kind == "q" else wkT
                return bass.AP(wsrc.tensor, wsrc.offset + mch * 128,
                               [[D, 128], [128 * D, 8], [1, 128]])

            def bias1(mch, kind):
                return (bq_sb if kind == "q" else bk_sb)[:, mch:mch + 1]

            emit_qk(ksb1, chunks1, wsrc1, bias1, q1_dram)
            dump_rows("d_q1", q1_dram, D)
            emit_band1()
            dump_rows("d_bm1", bm1_dram, 128)
            if upto >= 7:
                emit_attention(1, ksb1, vsb1, q1_dram, bm1_dram, ctx1_dram)
                dump_rows("d_ctx1", ctx1_dram, D)
    if upto >= 8:
        sA2 = C("a2a")
        with nc.Block() as blk:
            @blk.gpsimd
            def _(g):
                ins = g.collective_compute(
                    "AllGather", ALU.bypass, replica_groups=[list(range(8))],
                    ins=[ctx1_dram[:, :]], outs=[agctx[:, :]])
                sA2.inc(ins, 1)
                sA2.w(g, 1)
        bar()
        nc.all_core_barrier()
        bar()
        dump_rows("d_a2a", agctx, 2 * D)
    if upto >= 9:
        emit_scramble_proj()
        dump_rows("d_x3", x3_dram, D)
    if upto >= 10:
        emit_moe()
    bar()
    return nc


def _host_inputs(inputs):
    f32 = np.float32
    c = np.ascontiguousarray
    x = np.asarray(inputs["L0_attn_inputs"], f32)
    tb = np.asarray(inputs["L1_attn_time_bias"], f32).reshape(500)
    idx = np.clip(np.arange(4095) - 2047 + 250, 0, 499)
    band1 = tb[idx]
    band1_rev = c(band1[::-1])
    common = {
        "w_qkv": c(np.asarray(inputs["L0_attn_W_qkv"], f32)),
        "w_out": c(np.asarray(inputs["L0_attn_W_out"], f32)),
        "cbt": c(np.asarray(inputs["L0_ffn_CB"], f32).T),
        "scb": c(np.asarray(inputs["L0_ffn_SCB"], f32)),
        "cbias": c(np.asarray(inputs["L0_ffn_bias"], f32)),
        "wqT": c(np.asarray(inputs["L1_attn_Wq"], f32).T),
        "wkT": c(np.asarray(inputs["L1_attn_Wk"], f32).T),
        "wvT": c(np.asarray(inputs["L1_attn_Wv"], f32).T),
        "woT": c(np.asarray(inputs["L1_attn_Wo"], f32).T),
        "bq": c(np.asarray(inputs["L1_attn_bq"], f32)),
        "bk": c(np.asarray(inputs["L1_attn_bk"], f32)),
        "bv": c(np.asarray(inputs["L1_attn_bv"], f32)),
        "bo": c(np.asarray(inputs["L1_attn_bo"], f32)),
        "etg": c(np.asarray(inputs["L1_attn_et_gate"], f32).reshape(1, 1)),
        "isg": c(np.asarray(inputs["L1_attn_is_gate"], f32).reshape(1, 1)),
        "bandrev1": band1_rev,
        "w1": c(np.asarray(inputs["L1_ffn_W1"], f32)),
        "w2": c(np.asarray(inputs["L1_ffn_W2"], f32)),
        "wg": c(np.asarray(inputs["L1_ffn_Wg"], f32)),
        "bg": c(np.asarray(inputs["L1_ffn_bg"], f32)),
    }
    in_maps = []
    for j in range(8):
        m = dict(common)
        m["x_fm"] = c(x[j].T)
        in_maps.append(m)
    return in_maps


def kernel(**inputs):
    in_maps = _host_inputs(inputs)
    nc = build()
    res = run_bass_kernel_spmd(nc, in_maps, core_ids=list(range(8)))
    out = np.empty((8, T, D), np.float32)
    for j in range(8):
        out[j] = np.asarray(res.results[j]["out_fm"]).T
    return out


# version 02
# speedup vs baseline: 1.0316x; 1.0316x over previous
import math
import numpy as np
import concourse.bass as bass
import concourse.mybir as mybir
from concourse.bass_utils import run_bass_kernel_spmd

F32 = mybir.dt.float32
AF = mybir.ActivationFunctionType
ALU = mybir.AluOpType

D = 1024
T = 2048
DC = 8
QT = 256
NQT = T // QT
BW = 3968
BC0 = 1920
NEG = -1.0e30
EPS = 1e-5


def _alibi_slopes32():
    base = 2.0 ** (-(2.0 ** (-(math.log2(32) - 3))))
    powers = np.arange(1, 33, dtype=np.float32)
    return (np.float32(base) ** powers).astype(np.float32)


def _bandmat_l0():
    band = np.arange(4095, dtype=np.float64) - 2047.0
    band[band > 0] = NEG
    p = np.arange(128)[:, None]
    c = np.arange(BW)[None, :]
    idx = p - c + (BW - 1)
    ok = (idx >= 0) & (idx <= 4094)
    return (band[np.clip(idx, 0, 4094)] * ok).astype(np.float32)


class Cnt:
    def __init__(self, sem):
        self.sem = sem
        self.val = 0

    def inc(self, ins, n=1):
        ins.then_inc(self.sem, n)
        self.val += n
        return self.val


class Rel:

    def __init__(self, cnt):
        self.c = cnt
        self.b = cnt.val

    def inc(self, ins, n=1):
        return self.c.inc(ins, n)

    def w(self, eng, v):
        if self.b + v > 0:
            eng.wait_ge(self.c.sem, self.b + v)

    @property
    def val(self):
        return self.c.val

    @property
    def rval(self):
        return self.c.val - self.b

    @property
    def sem(self):
        return self.c.sem


def build(upto=99, dumps=(), race=False):
    nc = bass.Bass("TRN2", target_bir_lowering=False,
                   detect_race_conditions=race, num_devices=8)
    dumps = set(dumps)

    x_in = nc.dram_tensor("x_fm", [D, T], F32, kind="ExternalInput").ap()
    w_qkv = nc.dram_tensor("w_qkv", [D, 4096], F32, kind="ExternalInput").ap()
    w_out = nc.dram_tensor("w_out", [2048, D], F32, kind="ExternalInput").ap()
    cbt = nc.dram_tensor("cbt", [D, D], F32, kind="ExternalInput").ap()
    scb_d = nc.dram_tensor("scb", [D], F32, kind="ExternalInput").ap()
    cbias_d = nc.dram_tensor("cbias", [D], F32, kind="ExternalInput").ap()
    wqT = nc.dram_tensor("wqT", [D, D], F32, kind="ExternalInput").ap()
    wkT = nc.dram_tensor("wkT", [D, D], F32, kind="ExternalInput").ap()
    wvT = nc.dram_tensor("wvT", [D, D], F32, kind="ExternalInput").ap()
    woT = nc.dram_tensor("woT", [D, D], F32, kind="ExternalInput").ap()
    bq_d = nc.dram_tensor("bq", [D], F32, kind="ExternalInput").ap()
    bk_d = nc.dram_tensor("bk", [D], F32, kind="ExternalInput").ap()
    bv_d = nc.dram_tensor("bv", [D], F32, kind="ExternalInput").ap()
    bo_d = nc.dram_tensor("bo", [D], F32, kind="ExternalInput").ap()
    etg = nc.dram_tensor("etg", [1, 1], F32, kind="ExternalInput").ap()
    isg = nc.dram_tensor("isg", [1, 1], F32, kind="ExternalInput").ap()
    bandrev1 = nc.dram_tensor("bandrev1", [4095], F32, kind="ExternalInput").ap()
    w1_d = nc.dram_tensor("w1", [8, D, 4096], F32, kind="ExternalInput").ap()
    w2_d = nc.dram_tensor("w2", [8, 4096, D], F32, kind="ExternalInput").ap()
    wg_d = nc.dram_tensor("wg", [D, 8], F32, kind="ExternalInput").ap()
    bg_d = nc.dram_tensor("bg", [8], F32, kind="ExternalInput").ap()
    out_d = nc.dram_tensor("out_fm", [D, T], F32, kind="ExternalOutput").ap()

    q0_dram = nc.dram_tensor("q0_scr", [2048, T], F32).ap()
    q1_dram = nc.dram_tensor("q1_scr", [D, T], F32).ap()
    xln_dram = nc.dram_tensor("xln_scr", [D, T], F32).ap()
    x1_dram = nc.dram_tensor("x1_scr", [D, T], F32).ap()
    x2_dram = nc.dram_tensor("x2_scr", [D, T], F32).ap()
    x3_dram = nc.dram_tensor("x3_scr", [D, T], F32).ap()
    ctx0_dram = nc.dram_tensor("ctx0_scr", [2048, T], F32).ap()
    ctx1_dram = nc.dram_tensor("ctx1_scr", [D, T], F32).ap()
    agctx = nc.dram_tensor("agctx", [8 * D, T], F32).ap()
    bm1_dram = nc.dram_tensor("bm1_scr", [128, BW], F32).ap()
    bdev_dram = nc.dram_tensor("bdev_scr", [4095], F32).ap()

    bm0 = nc.inline_tensor(_bandmat_l0(), name="bm0").ap()
    selmat = np.zeros((8, 8 * 128), np.float32)
    for e in range(8):
        selmat[e, e * 128:(e + 1) * 128] = 1.0
    sel_c = nc.inline_tensor(selmat, name="selc").ap()
    id8_c = nc.inline_tensor(np.eye(8, dtype=np.float32), name="id8c").ap()
    id128_c = nc.inline_tensor(np.eye(128, dtype=np.float32), name="id128c").ap()
    ucum_np = np.triu(np.ones((8, 8), np.float32), 1).T
    ucum_c = nc.inline_tensor(np.ascontiguousarray(ucum_np.T), name="ucumc").ap()
    slopes8 = [float(s) * 8.0 for s in _alibi_slopes32()]

    dump_aps = {}
    for name, shape in [("d_xln", [D, T]), ("d_k", [D, T]), ("d_v", [128, 16 * 1040]),
                        ("d_q0", [2048, T]), ("d_ctx0", [2048, T]), ("d_x1", [D, T]),
                        ("d_x2", [D, T]), ("d_ctx1", [D, T]), ("d_a2a", [2 * D, T]),
                        ("d_x3", [D, T]), ("d_bm1", [128, BW]), ("d_q1", [D, T]),
                        ("d_mask", [8, T]), ("d_xln2", [D, T]), ("d_scbq", [128, 16]), ("d_csrc", [128, 8 * T]), ("d_gdbg", [128, 32])]:
        if name in dumps:
            dump_aps[name] = nc.dram_tensor(name, shape, F32, kind="ExternalOutput").ap()

    onesc = nc.alloc_sbuf_tensor("onesc", [128, 1], F32).ap()
    ones1 = nc.alloc_sbuf_tensor("ones1", [1, 128], F32).ap()
    scb_sb = nc.alloc_sbuf_tensor("scb_sb", [128, 8], F32).ap()
    cbias_sb = nc.alloc_sbuf_tensor("cbias_sb", [128, 8], F32).ap()
    bq_sb = nc.alloc_sbuf_tensor("bq_sb", [128, 8], F32).ap()
    bk_sb = nc.alloc_sbuf_tensor("bk_sb", [128, 8], F32).ap()
    bo_sb = nc.alloc_sbuf_tensor("bo_sb", [128, 8], F32).ap()
    bg_sb = nc.alloc_sbuf_tensor("bg_sb", [8, 1], F32).ap()
    r127 = nc.alloc_sbuf_tensor("r127", [128, 1], F32).ap()
    r127b = nc.alloc_sbuf_tensor("r127b", [128, 1], F32).ap()
    scbq_sb = nc.alloc_sbuf_tensor("scbq_sb", [128, 8], F32).ap()

    sems = {}

    def C(name):
        if name not in sems:
            sems[name] = Cnt(nc.alloc_semaphore(name))
        return Rel(sems[name])

    def bar():
        nc.all_engine_barrier()

    def w(eng, cnt):
        if cnt.val > 0:
            eng.wait_ge(cnt.sem, cnt.val)

    def dump_rows(name, dram_src, nrow):
        if name not in dump_aps:
            return
        sD = C("dmp")
        with nc.Block() as blk:
            @blk.scalar
            def _(sc):
                for r in range(nrow // 128):
                    sD.inc(sc.dma_start(dump_aps[name][r * 128:(r + 1) * 128, :],
                                        dram_src[r * 128:(r + 1) * 128, :]), 16)
                w(sc, sD)
        bar()

    sP0 = C("p0ld")
    sP0g = C("p0g")
    with nc.Block() as blk:
        @blk.sync
        def _(sy):
            for dst, src in [(scb_sb, scb_d), (cbias_sb, cbias_d), (bq_sb, bq_d),
                             (bk_sb, bk_d), (bo_sb, bo_d)]:
                for dc in range(DC):
                    sP0.inc(sy.dma_start(dst[:, dc:dc + 1],
                                         src[dc * 128:(dc + 1) * 128]), 16)
            sP0.inc(sy.dma_start(bg_sb[:, :], bg_d[:]), 16)
            w(sy, sP0)

        @blk.gpsimd
        def _(g):
            g.memset(onesc[:, :], 1.0)
            g.memset(ones1[:, :], 1.0)
            sP0g.inc(g.memset(r127[:, :], 127.0))

        @blk.vector
        def _(v):
            sP0.w(v, sP0.val)
            sP0g.w(v, 1)
            sP0g.inc(v.reciprocal(r127b[:, :], r127[:, :]))
            sP0g.w(v, 2)
            v.tensor_single_scalar(scbq_sb[:, :], scb_sb[:, :], r127b[:, 0:1], ALU.mult)
    bar()

    def dump_smalls():
        if "d_scbq" not in dump_aps:
            return
        sDx = C("dmpg")
        with nc.Block() as blk:
            @blk.gpsimd
            def _(g):
                sDx.inc(g.dma_start(dump_aps["d_scbq"][:, 0:8], scbq_sb[:, :]), 16)
                sDx.inc(g.dma_start(dump_aps["d_scbq"][:, 8:12], cbias_sb[:, 0:4]), 16)
                sDx.inc(g.dma_start(dump_aps["d_scbq"][:, 12:13], r127[:, :],
                                    allow_slow_non_contiguous=True), 16)
                sDx.inc(g.dma_start(dump_aps["d_scbq"][:, 13:14], r127b[:, :],
                                    allow_slow_non_contiguous=True), 16)
                sDx.inc(g.dma_start(dump_aps["d_scbq"][:, 14:15], scb_sb[:, 0:1],
                                    allow_slow_non_contiguous=True), 16)
                w(g, sDx)
        bar()

    lnid = [0]

    def emit_ln_stream(src_dram, dst_dram):
        lnid[0] += 1
        W_ = 512
        ntt = 4
        tag = "ln"
        utag = f"ln{lnid[0]}"
        sXL, sSQ, sSUM, sVA, sRT, sST2, sBC, sAPP = (C(tag + s) for s in
            ("xl", "sq", "sum", "va", "rt", "st2", "bc", "app"))
        sOSs = [C(tag + "os0"), C(tag + "os1")]
        with (
            nc.psum_tensor(utag + "_s1", [1, W_], F32) as ps1_g,
            nc.psum_tensor(utag + "_s2", [1, W_], F32) as ps2_g,
            nc.psum_tensor(utag + "_a", [128, W_], F32) as psa_g,
            nc.psum_tensor(utag + "_b", [128, W_], F32) as psb_g,
            nc.sbuf_tensor(utag + "_xh", [128, DC * W_], F32) as xh_g,
            nc.sbuf_tensor(utag + "_sq", [128, DC * W_], F32) as sq_g,
            nc.sbuf_tensor(utag + "_ob", [128, 2 * W_], F32) as ob_g,
            nc.sbuf_tensor(utag + "_mean", [1, W_], F32) as mean_g,
            nc.sbuf_tensor(utag + "_varp", [1, W_], F32) as varp_g,
            nc.sbuf_tensor(utag + "_t1", [1, W_], F32) as t1_g,
            nc.sbuf_tensor(utag + "_inv", [1, W_], F32) as inv_g,
        ):
            ps1, ps2, psa, psb = ps1_g.ap(), ps2_g.ap(), psa_g.ap(), psb_g.ap()
            xh, sq, ob = xh_g.ap(), sq_g.ap(), ob_g.ap()
            mean, varp, t1, inv = mean_g.ap(), varp_g.ap(), t1_g.ap(), inv_g.ap()
            with nc.Block() as blk:
                @blk.sync
                def _(sy):
                    for tt in range(ntt):
                        if tt >= 1:
                            sAPP.w(sy, tt * DC)
                        sXL.inc(sy.dma_start(
                            bass.AP(xh.tensor, xh.offset,
                                    [[DC * W_, 128], [W_, DC], [1, W_]]),
                            bass.AP(src_dram.tensor,
                                    src_dram.offset + tt * W_,
                                    [[T, 128], [128 * T, DC], [1, W_]])), 16)

                @blk.scalar
                def _(sc):
                    for tt in range(ntt):
                        if tt >= 1:
                            sSUM.w(sc, tt)
                        sXL.w(sc, 16 * (tt + 1))
                        for dc in range(DC):
                            sSQ.inc(sc.activation(sq[:, dc * W_:(dc + 1) * W_],
                                                  xh[:, dc * W_:(dc + 1) * W_],
                                                  AF.Square))
                        sVA.w(sc, tt + 1)
                        sRT.inc(sc.activation(t1, varp, AF.Sqrt))

                @blk.tensor
                def _(t):
                    for tt in range(ntt):
                        if tt >= 1:
                            sVA.w(t, tt)
                        sXL.w(t, 16 * (tt + 1))
                        ins = None
                        for dc in range(DC):
                            i = tt * DC + dc
                            t.matmul(ps1[:, :], onesc[:, :],
                                     xh[:, dc * W_:(dc + 1) * W_],
                                     start=(dc == 0), stop=(dc == DC - 1))
                            sSQ.w(t, i + 1)
                            ins = t.matmul(ps2[:, :], onesc[:, :],
                                           sq[:, dc * W_:(dc + 1) * W_],
                                           start=(dc == 0), stop=(dc == DC - 1))
                        sSUM.inc(ins)
                        sST2.w(t, tt + 1)
                        if tt >= 1:
                            sAPP.w(t, tt * DC)
                        t.matmul(psa[:, :], ones1[:, :], inv, start=True, stop=True)
                        sBC.inc(t.matmul(psb[:, :], ones1[:, :], mean,
                                         start=True, stop=True))

                @blk.vector
                def _(v):
                    for tt in range(ntt):
                        sSUM.w(v, tt + 1)
                        v.tensor_single_scalar(mean, ps1[0:1, :], 1.0 / 1024.0, ALU.mult)
                        v.tensor_single_scalar(varp, ps2[0:1, :], 1.0 / 1024.0, ALU.mult)
                        v.tensor_tensor(t1, mean, mean, ALU.mult)
                        v.tensor_tensor(varp, varp, t1, ALU.subtract)
                        sVA.inc(v.tensor_single_scalar(varp, varp, EPS, ALU.add))
                        sRT.w(v, tt + 1)
                        v.reciprocal(inv, t1)
                        for _i in range(2):
                            v.tensor_tensor(t1, inv, inv, ALU.mult)
                            v.tensor_tensor(t1, t1, varp, ALU.mult)
                            v.tensor_scalar(t1, t1, -0.5, 1.5, ALU.mult, ALU.add)
                            v.tensor_tensor(inv, inv, t1, ALU.mult)
                        sST2.inc(v.tensor_tensor(mean, mean, inv, ALU.mult))
                        sBC.w(v, tt + 1)
                        for dc in range(DC):
                            i = tt * DC + dc
                            if i >= 2:
                                sOSs[i % 2].w(v, 16 * (i // 2))
                            v.tensor_tensor(ob[:, (i % 2) * W_:(i % 2 + 1) * W_],
                                            xh[:, dc * W_:(dc + 1) * W_],
                                            psa[:, :], ALU.mult)
                            sAPP.inc(v.tensor_tensor(
                                ob[:, (i % 2) * W_:(i % 2 + 1) * W_],
                                ob[:, (i % 2) * W_:(i % 2 + 1) * W_],
                                psb[:, :], ALU.subtract))

                @blk.gpsimd
                def _(g):
                    for tt in range(ntt):
                        for dc in range(DC):
                            i = tt * DC + dc
                            sAPP.w(g, i + 1)
                            sOSs[i % 2].inc(g.dma_start(
                                bass.AP(dst_dram.tensor,
                                        dst_dram.offset + dc * 128 * T + tt * W_,
                                        [[T, 128], [1, W_]]),
                                ob[:, (i % 2) * W_:(i % 2 + 1) * W_]), 16)
                    w(g, sOSs[0])
                    w(g, sOSs[1])
            bar()

    vpid = [0]

    def emit_v(vsb, wv_src, bias_bc):
        vpid[0] += 1
        tag = "vp"
        utag = f"vp{vpid[0]}"
        sWL, sXT, sM, sE, sZ = (C(tag + s) for s in ("wl", "xt", "mm", "ev", "z"))
        with (
            nc.sbuf_tensor(utag + "_wst", [128, 32 * 256], F32) as wst_g,
            nc.sbuf_tensor(utag + "_xt", [128, DC * 512], F32) as xt_g,
            nc.psum_tensor(utag + "_ps0", [128, 256], F32) as ps0_g,
            nc.psum_tensor(utag + "_ps1", [128, 256], F32) as ps1_g,
        ):
            wst = wst_g.ap()
            xt = xt_g.ap()
            psv = [ps0_g.ap(), ps1_g.ap()]
            with nc.Block() as blk:
                @blk.sync
                def _(sy):
                    for sp in range(4):
                        for dc in range(DC):
                            sWL.inc(sy.dma_start(
                                wst[:, (sp * DC + dc) * 256:(sp * DC + dc + 1) * 256],
                                wv_src(sp, dc)), 16)
                    for tt in range(4):
                        if tt >= 1:
                            sM.w(sy, tt * 16)
                        sXT.inc(sy.dma_start(
                            bass.AP(xt.tensor, xt.offset,
                                    [[DC * 512, 128], [512, DC], [1, 512]]),
                            bass.AP(xln_dram.tensor,
                                    xln_dram.offset + tt * 512,
                                    [[T, 128], [128 * T, DC], [1, 512]])), 16)

                @blk.gpsimd
                def _(g):
                    sZ.inc(g.memset(vsb[:, :], 1.0))

                @blk.tensor
                def _(t):
                    w(t, sWL)
                    j = 0
                    for tt in range(4):
                        sXT.w(t, 16 * (tt + 1))
                        for ksub in range(4):
                            for sp in range(4):
                                if j >= 2:
                                    sE.w(t, j - 1)
                                ins = None
                                for dc in range(DC):
                                    ins = t.matmul(
                                        psv[j % 2][:, :],
                                        xt[:, dc * 512 + ksub * 128:
                                           dc * 512 + (ksub + 1) * 128],
                                        wst[:, (sp * DC + dc) * 256:
                                            (sp * DC + dc + 1) * 256],
                                        start=(dc == 0), stop=(dc == DC - 1))
                                sM.inc(ins)
                                j += 1

                @blk.vector
                def _(v):
                    sZ.w(v, 1)
                    j = 0
                    for tt in range(4):
                        for ksub in range(4):
                            kt = tt * 4 + ksub
                            for sp in range(4):
                                sM.w(v, j + 1)
                                dst = bass.AP(vsb.tensor,
                                              vsb.offset + kt * 1040 + sp * 260,
                                              [[16640, 128], [65, 4], [1, 64]])
                                if bias_bc is None:
                                    sE.inc(v.tensor_copy(dst, psv[j % 2][:, :]))
                                else:
                                    sE.inc(v.tensor_tensor(
                                        dst, psv[j % 2][:, :],
                                        bias_bc[:, sp * 256:(sp + 1) * 256], ALU.add))
                                j += 1
            bar()

    qkid = [0]

    def emit_qk(ksb, chunks, wgroup_src, bias_for, q_dram_ap):
        qkid[0] += 1
        tag = "qk"
        utag = f"qk{qkid[0]}"
        sXT, sM, sE = C(tag + "xt"), C(tag + "mm"), C(tag + "ev")
        sWs = [C(tag + "wl0"), C(tag + "wl1")]
        sQs = [C(tag + "qs0"), C(tag + "qs1")]
        nch = len(chunks)
        with (
            nc.sbuf_tensor(utag + "_wst", [128, 2 * 8 * 128], F32) as wst_g,
            nc.sbuf_tensor(utag + "_xt", [128, DC * 512], F32) as xt_g,
            nc.sbuf_tensor(utag + "_qst", [128, 2 * 512], F32) as qst_g,
            nc.psum_tensor(utag + "_ps0", [128, 512], F32) as ps0_g,
            nc.psum_tensor(utag + "_ps1", [128, 512], F32) as ps1_g,
        ):
            wst, xt, qst = wst_g.ap(), xt_g.ap(), qst_g.ap()
            psq = [ps0_g.ap(), ps1_g.ap()]
            with nc.Block() as blk:
                @blk.sync
                def _(sy):
                    for tt in range(4):
                        if tt >= 1:
                            sM.w(sy, tt * nch)
                        sXT.inc(sy.dma_start(
                            bass.AP(xt.tensor, xt.offset,
                                    [[DC * 512, 128], [512, DC], [1, 512]]),
                            bass.AP(xln_dram.tensor,
                                    xln_dram.offset + tt * 512,
                                    [[T, 128], [128 * T, DC], [1, 512]])), 16)
                        for ci, (mch, kind) in enumerate(chunks):
                            j = tt * nch + ci
                            if j >= 2:
                                sM.w(sy, j - 1)
                            sWs[j % 2].inc(sy.dma_start(
                                bass.AP(wst.tensor,
                                        wst.offset + (j % 2) * 1024,
                                        [[2 * 1024, 128], [128, 8], [1, 128]]),
                                wgroup_src(mch, kind)), 16)

                @blk.tensor
                def _(t):
                    for tt in range(4):
                        sXT.w(t, 16 * (tt + 1))
                        for ci, (mch, kind) in enumerate(chunks):
                            j = tt * nch + ci
                            if j >= 2:
                                sE.w(t, j - 1)
                            sWs[j % 2].w(t, 16 * (j // 2 + 1))
                            ins = None
                            for dc in range(DC):
                                ins = t.matmul(
                                    psq[j % 2][:, :],
                                    wst[:, (j % 2) * 1024 + dc * 128:
                                        (j % 2) * 1024 + (dc + 1) * 128],
                                    xt[:, dc * 512:(dc + 1) * 512],
                                    start=(dc == 0), stop=(dc == DC - 1))
                            sM.inc(ins)

                @blk.scalar
                def _(sc):
                    jq = 0
                    for tt in range(4):
                        for ci, (mch, kind) in enumerate(chunks):
                            j = tt * nch + ci
                            sM.w(sc, j + 1)
                            bias = bias_for(mch, kind)
                            if kind == "k":
                                dst = ksb[:, mch * T + tt * 512: mch * T + (tt + 1) * 512]
                            else:
                                if jq >= 2:
                                    sQs[jq % 2].w(sc, 16 * (jq // 2))
                                dst = qst[:, (jq % 2) * 512:(jq % 2 + 1) * 512]
                            if bias is None:
                                sE.inc(sc.copy(dst, psq[j % 2][:, :]))
                            else:
                                sE.inc(sc.activation(dst, psq[j % 2][:, :],
                                                     AF.Identity, bias=bias))
                            if kind == "q":
                                jq += 1

                @blk.gpsimd
                def _(g):
                    jq = 0
                    for tt in range(4):
                        for ci, (mch, kind) in enumerate(chunks):
                            j = tt * nch + ci
                            if kind != "q":
                                continue
                            sE.w(g, j + 1)
                            sQs[jq % 2].inc(g.dma_start(
                                q_dram_ap[mch * 128:(mch + 1) * 128,
                                          tt * 512:(tt + 1) * 512],
                                qst[:, (jq % 2) * 512:(jq % 2 + 1) * 512]), 16)
                            jq += 1
                    w(g, sQs[0])
                    w(g, sQs[1])
            bar()

    atid = [0]

    def emit_attention(layer, ksb, vsb, q_dram_ap, band_dram, ctxl_dram):
        atid[0] += 1
        tag = "at"
        utag = f"at{atid[0]}"
        NH = 32 if layer == 0 else 16
        LAG = 3
        sBD, sSM, sBA, sEX, sPV, sRC, sBC2, sCX, sCP = (
            C(tag + s) for s in
            ("bd", "sm", "ba", "ex", "pv", "rc", "bc", "cx", "cp"))
        sQLs = [C(tag + "ql0"), C(tag + "ql1"), C(tag + "ql2")]
        sCSs = [C(tag + "cs0"), C(tag + "cs1")]
        npair = [(qt + 1) if layer == 0 else 8 for qt in range(NQT)]
        pairs = []
        cum = {}
        tot = 0
        for qt in range(NQT):
            for h in range(NH):
                gi = qt * NH + h
                cum[gi] = tot
                for kp in range(npair[qt]):
                    pairs.append((tot + kp, gi, qt, h, kp))
                tot += npair[qt]
        n_items = tot
        bd_cum = [0]
        for qt in range(NQT):
            bd_cum.append(bd_cum[-1] + 2 * npair[qt])
        jlast = {}
        for (i, gi, qt, h, kp) in pairs:
            if kp == npair[qt] - 1:
                jlast[gi] = i
        NHEADS = NH * NQT
        bcast_after_pv = {}
        recip_after = {}
        evac_after = {}
        for g in range(NHEADS):
            bcast_after_pv.setdefault(min(jlast[g] + 1, n_items - 1), []).append(g)
            recip_after.setdefault(min(jlast[g] + LAG, n_items - 1), []).append(g)
            evac_after.setdefault(min(jlast[g] + LAG + 1, n_items - 1), []).append(g)

        def kv_of(h):
            g = h // 2 if layer == 0 else h
            return g // 2, (g % 2) * 64, 65 * g

        with (
            nc.sbuf_tensor(utag + "_band", [128, 16 * QT], F32) as band_g,
            nc.sbuf_tensor(utag + "_q", [128, 3 * QT], F32) as q_g,
            nc.sbuf_tensor(utag + "_p", [128, 4 * 512], F32) as p_g,
            nc.sbuf_tensor(utag + "_rin", [1, 2 * QT], F32) as rin_g,
            nc.sbuf_tensor(utag + "_cst", [64, 2 * QT], F32) as cst_g,
            nc.sbuf_tensor(utag + "_rbs", [64, 2 * QT], F32) as rbs_g,
            nc.psum_tensor(utag + "_s0", [128, 512], F32) as s0_g,
            nc.psum_tensor(utag + "_s1", [128, 512], F32) as s1_g,
            nc.psum_tensor(utag + "_s2", [128, 512], F32) as s2_g,
            nc.psum_tensor(utag + "_s3", [128, 512], F32) as s3_g,
            nc.psum_tensor(utag + "_c0", [65, QT], F32) as c0_g,
            nc.psum_tensor(utag + "_c1", [65, QT], F32) as c1_g,
            nc.psum_tensor(utag + "_rb", [64, 2 * QT], F32) as rb_g,
        ):
            band = band_g.ap()
            qsb = q_g.ap()
            psb = p_g.ap()
            rin = rin_g.ap()
            cst = cst_g.ap()
            rbs = rbs_g.ap()
            spsum = [s0_g.ap(), s1_g.ap(), s2_g.ap(), s3_g.ap()]
            cpsum = [c0_g.ap(), c1_g.ap()]
            rbps = rb_g.ap()

            with nc.Block() as blk:
                @blk.sync
                def _(sy):
                    for qt in range(NQT):
                        if qt >= 1:
                            sBA.w(sy, NH * (bd_cum[qt] // 2))
                        for kc in range(2 * npair[qt]):
                            c_off = qt * QT - kc * 128 + BC0
                            sBD.inc(sy.dma_start(
                                band[:, kc * QT:(kc + 1) * QT],
                                bass.AP(band_dram.tensor, band_dram.offset + c_off,
                                        [[BW, 128], [1, QT]])), 16)
                        for h in range(NH):
                            gi = qt * NH + h
                            kvoff_h = kv_of(h)[1]
                            if gi >= 3:
                                sSM.w(sy, cum[gi - 2])
                            sQLs[gi % 3].inc(sy.dma_start(
                                bass.AP(qsb.tensor,
                                        qsb.offset + kvoff_h * (3 * QT)
                                        + (gi % 3) * QT,
                                        [[3 * QT, 64], [1, QT]]),
                                q_dram_ap[h * 64:(h + 1) * 64,
                                          qt * QT:(qt + 1) * QT]), 16)

                @blk.tensor
                def _(t):
                    def emit_scores(pr):
                        i, gi, qt, h, kp = pr
                        kvc, kvoff, vcol = kv_of(h)
                        if kp == 0:
                            sQLs[gi % 3].w(t, 16 * (gi // 3 + 1))
                        if i >= 4:
                            sEX.w(t, i - 3)
                        ins = None
                        for half in range(2):
                            kc = kp * 2 + half
                            lhs = bass.AP(ksb.tensor,
                                          ksb.offset + kvoff * (DC * T) + kvc * T + kc * 128,
                                          [[DC * T, 64], [1, 128]])
                            rhsq = bass.AP(qsb.tensor,
                                           qsb.offset + kvoff * (3 * QT)
                                           + (gi % 3) * QT,
                                           [[3 * QT, 64], [1, QT]])
                            ins = t.matmul(
                                spsum[i % 4][:, half * QT:(half + 1) * QT], lhs,
                                rhsq, start=True, stop=True)
                        sSM.inc(ins)

                    def emit_pv(pr):
                        i, gi, qt, h, kp = pr
                        kvc, kvoff, vcol = kv_of(h)
                        sEX.w(t, i + 1)
                        if kp == 0 and gi >= 2:
                            sCX.w(t, gi - 1)
                        ins = None
                        for half in range(2):
                            kc = kp * 2 + half
                            lhs = bass.AP(vsb.tensor, vsb.offset + kc * 1040 + vcol,
                                          [[16640, 128], [1, 65]])
                            ins = t.matmul(
                                cpsum[gi % 2][:, :], lhs,
                                psb[:, (i % 4) * 512 + half * QT:
                                    (i % 4) * 512 + (half + 1) * QT],
                                start=(kp == 0 and half == 0),
                                stop=(kp == npair[qt] - 1 and half == 1))
                        sPV.inc(ins)
                        for g2 in bcast_after_pv.get(i, []):
                            sRC.w(t, g2 + 1)
                            if g2 >= 2:
                                sCP.w(t, g2 - 1)
                            sBC2.inc(t.matmul(
                                rbps[:, (g2 % 2) * QT:(g2 % 2 + 1) * QT],
                                ones1[0:1, 0:64],
                                rin[:, (g2 % 2) * QT:(g2 % 2 + 1) * QT],
                                start=True, stop=True))

                    for idx, pr in enumerate(pairs):
                        emit_scores(pr)
                        if idx >= LAG:
                            emit_pv(pairs[idx - LAG])
                    for j in range(max(n_items - LAG, 0), n_items):
                        emit_pv(pairs[j])

                @blk.vector
                def _(v):
                    def vhead_recip(g):
                        sPV.w(v, jlast[g] + 1)
                        if g >= 2:
                            sBC2.w(v, g - 1)
                        sRC.inc(v.reciprocal(
                            rin[:, (g % 2) * QT:(g % 2 + 1) * QT],
                            cpsum[g % 2][64:65, :]))

                    def vhead_evac(g):
                        sBC2.w(v, g + 1)
                        if g >= 2:
                            sCSs[g % 2].w(v, 16 * (g // 2))
                        sCP.inc(v.tensor_copy(
                            rbs[:, (g % 2) * QT:(g % 2 + 1) * QT],
                            rbps[:, (g % 2) * QT:(g % 2 + 1) * QT]))
                        sCX.inc(v.tensor_tensor(
                            cst[:, (g % 2) * QT:(g % 2 + 1) * QT],
                            cpsum[g % 2][0:64, :],
                            rbs[:, (g % 2) * QT:(g % 2 + 1) * QT], ALU.mult))

                    for idx, pr in enumerate(pairs):
                        i, gi, qt, h, kp = pr
                        if kp == 0 and h == 0:
                            sBD.w(v, 16 * bd_cum[qt + 1])
                        sSM.w(v, i + 1)
                        sps = spsum[i % 4]
                        if layer == 0:
                            ins = v.scalar_tensor_tensor(
                                sps[:, :], band[:, kp * 2 * QT:(kp * 2 + 2) * QT],
                                slopes8[h], sps[:, :], ALU.mult, ALU.add)
                        else:
                            ins = v.tensor_tensor(
                                sps[:, :], sps[:, :],
                                band[:, kp * 2 * QT:(kp * 2 + 2) * QT], ALU.add)
                        sBA.inc(ins)
                        for g2 in recip_after.get(i, []):
                            vhead_recip(g2)
                        for g2 in evac_after.get(i, []):
                            vhead_evac(g2)

                @blk.scalar
                def _(sc):
                    for idx, pr in enumerate(pairs):
                        i, gi, qt, h, kp = pr
                        sBA.w(sc, i + 1)
                        if i >= 4:
                            sPV.w(sc, i - 3)
                        sEX.inc(sc.activation(
                            psb[:, (i % 4) * 512:(i % 4 + 1) * 512],
                            spsum[i % 4][:, :], AF.Exp, scale=0.125))

                @blk.gpsimd
                def _(g):
                    for qt in range(NQT):
                        for h in range(NH):
                            gi = qt * NH + h
                            sCX.w(g, gi + 1)
                            sCSs[gi % 2].inc(g.dma_start(
                                ctxl_dram[h * 64:(h + 1) * 64, qt * QT:(qt + 1) * QT],
                                cst[:, (gi % 2) * QT:(gi % 2 + 1) * QT]), 16)
                    w(g, sCSs[0])
                    w(g, sCSs[1])
            bar()

    def emit_wout0():
        sCL, sM, sE = C("w0cl"), C("w0mm"), C("w0ev")
        sWs = [C("w0wl0"), C("w0wl1")]
        sXs = [C("w0xl0"), C("w0xl1")]
        sOs = [C("w0os0"), C("w0os1")]
        with (
            nc.sbuf_tensor("w0_ctx", [128, 16 * 512], F32) as ctx_g,
            nc.sbuf_tensor("w0_wst", [128, 2 * 16 * 128], F32) as wst_g,
            nc.sbuf_tensor("w0_xr", [128, 2 * 512], F32) as xr_g,
            nc.sbuf_tensor("w0_ob", [128, 2 * 512], F32) as ob_g,
            nc.psum_tensor("w0_ps0", [128, 512], F32) as p0_g,
            nc.psum_tensor("w0_ps1", [128, 512], F32) as p1_g,
        ):
            ctxt = ctx_g.ap()
            wst = wst_g.ap()
            xr = xr_g.ap()
            ob = ob_g.ap()
            pj = [p0_g.ap(), p1_g.ap()]
            with nc.Block() as blk:
                @blk.sync
                def _(sy):
                    for tt in range(4):
                        if tt >= 1:
                            sM.w(sy, tt * 8)
                        sCL.inc(sy.dma_start(
                            bass.AP(ctxt.tensor, ctxt.offset,
                                    [[16 * 512, 128], [512, 16], [1, 512]]),
                            bass.AP(ctx0_dram.tensor,
                                    ctx0_dram.offset + tt * 512,
                                    [[T, 128], [128 * T, 16], [1, 512]])), 16)
                        for dc in range(DC):
                            j = tt * 8 + dc
                            if j >= 2:
                                sE.w(sy, j - 1)
                            sXs[j % 2].inc(sy.dma_start(
                                xr[:, (j % 2) * 512:(j % 2 + 1) * 512],
                                x_in[dc * 128:(dc + 1) * 128,
                                     tt * 512:(tt + 1) * 512]), 16)
                            if j >= 2:
                                sM.w(sy, j - 1)
                            sWs[j % 2].inc(sy.dma_start(
                                bass.AP(wst.tensor,
                                        wst.offset + (j % 2) * 2048,
                                        [[2 * 2048, 128], [128, 16], [1, 128]]),
                                bass.AP(w_out.tensor, w_out.offset + dc * 128,
                                        [[D, 128], [128 * D, 16], [1, 128]])), 16)

                @blk.tensor
                def _(t):
                    for tt in range(4):
                        sCL.w(t, 16 * (tt + 1))
                        for dc in range(DC):
                            j = tt * 8 + dc
                            if j >= 2:
                                t_ = sE.w(t, j - 1)
                            sWs[j % 2].w(t, 16 * (j // 2 + 1))
                            ins = None
                            for pr in range(16):
                                ins = t.matmul(pj[j % 2][:, :],
                                               wst[:, (j % 2) * 2048 + pr * 128:
                                                   (j % 2) * 2048 + (pr + 1) * 128],
                                               ctxt[:, pr * 512:(pr + 1) * 512],
                                               start=(pr == 0), stop=(pr == 15))
                            sM.inc(ins)

                @blk.vector
                def _(v):
                    for tt in range(4):
                        for dc in range(DC):
                            j = tt * 8 + dc
                            sM.w(v, j + 1)
                            sXs[j % 2].w(v, 16 * (j // 2 + 1))
                            if j >= 2:
                                sOs[j % 2].w(v, 16 * (j // 2))
                            sE.inc(v.tensor_tensor(
                                ob[:, (j % 2) * 512:(j % 2 + 1) * 512],
                                pj[j % 2][:, :],
                                xr[:, (j % 2) * 512:(j % 2 + 1) * 512], ALU.add))

                @blk.gpsimd
                def _(g):
                    for tt in range(4):
                        for dc in range(DC):
                            j = tt * 8 + dc
                            sE.w(g, j + 1)
                            sOs[j % 2].inc(g.dma_start(
                                bass.AP(x1_dram.tensor,
                                        x1_dram.offset + dc * 128 * T + tt * 512,
                                        [[T, 128], [1, 512]]),
                                ob[:, (j % 2) * 512:(j % 2 + 1) * 512]), 16)
                    w(g, sOs[0])
                    w(g, sOs[1])
            bar()

    def emit_cb():
        sWL, sXT, sM, sE = C("cbwl"), C("cbxt"), C("cbmm"), C("cbev")
        sXs = [C("cbxr0"), C("cbxr1")]
        sOs = [C("cbos0"), C("cbos1")]
        with (
            nc.sbuf_tensor("cbt_sb", [128, DC * 1024], F32) as cbt_g,
            nc.sbuf_tensor("cb_xt", [128, DC * 512], F32) as xt_g,
            nc.sbuf_tensor("cb_xr", [128, 2 * 512], F32) as xr_g,
            nc.sbuf_tensor("cb_ob", [128, 2 * 512], F32) as ob_g,
            nc.sbuf_tensor("cb_obt", [128, 2 * 512], F32) as obt_g,
            nc.psum_tensor("cb_ps0", [128, 512], F32) as cps0,
            nc.psum_tensor("cb_ps1", [128, 512], F32) as cps1,
        ):
            cbtsb = cbt_g.ap()
            xt = xt_g.ap()
            xr = xr_g.ap()
            ob = ob_g.ap()
            obt = obt_g.ap()
            cps = [cps0.ap(), cps1.ap()]
            with nc.Block() as blk:
                @blk.sync
                def _(sy):
                    for dc in range(DC):
                        srcw = bass.AP(cbt.tensor, cbt.offset + dc * 128 * D,
                                       [[D, 128], [1, D]])
                        sWL.inc(sy.dma_start(cbtsb[:, dc * 1024:(dc + 1) * 1024],
                                             srcw), 16)
                    for tt in range(4):
                        if tt >= 1:
                            sM.w(sy, tt * 8)
                        sXT.inc(sy.dma_start(
                            bass.AP(xt.tensor, xt.offset,
                                    [[DC * 512, 128], [512, DC], [1, 512]]),
                            bass.AP(xln_dram.tensor,
                                    xln_dram.offset + tt * 512,
                                    [[T, 128], [128 * T, DC], [1, 512]])), 16)
                        for nc_ in range(8):
                            j = tt * 8 + nc_
                            if j >= 2:
                                sE.w(sy, j - 1)
                            sXs[j % 2].inc(sy.dma_start(
                                xr[:, (j % 2) * 512:(j % 2 + 1) * 512],
                                bass.AP(x1_dram.tensor,
                                        x1_dram.offset + nc_ * 128 * T + tt * 512,
                                        [[T, 128], [1, 512]])), 16)

                @blk.tensor
                def _(t):
                    w(t, sWL)
                    for tt in range(4):
                        sXT.w(t, 16 * (tt + 1))
                        for nc_ in range(8):
                            j = tt * 8 + nc_
                            if j >= 2:
                                sE.w(t, j - 1)
                            ins = None
                            for dc in range(DC):
                                ins = t.matmul(
                                    cps[j % 2][:, :],
                                    cbtsb[:, dc * 1024 + nc_ * 128:
                                          dc * 1024 + (nc_ + 1) * 128],
                                    xt[:, dc * 512:(dc + 1) * 512],
                                    start=(dc == 0), stop=(dc == DC - 1))
                            sM.inc(ins)

                @blk.vector
                def _(v):
                    for tt in range(4):
                        for nc_ in range(8):
                            j = tt * 8 + nc_
                            sM.w(v, j + 1)
                            sXs[j % 2].w(v, 16 * (j // 2 + 1))
                            if j >= 2:
                                sOs[j % 2].w(v, 16 * (j // 2))
                            v.tensor_copy(obt[:, (j % 2) * 512:(j % 2 + 1) * 512],
                                          cps[j % 2][:, :])
                            v.tensor_scalar(ob[:, (j % 2) * 512:(j % 2 + 1) * 512],
                                            obt[:, (j % 2) * 512:(j % 2 + 1) * 512],
                                            scbq_sb[:, nc_:nc_ + 1],
                                            cbias_sb[:, nc_:nc_ + 1],
                                            ALU.mult, ALU.add)
                            sE.inc(v.tensor_tensor(
                                ob[:, (j % 2) * 512:(j % 2 + 1) * 512],
                                ob[:, (j % 2) * 512:(j % 2 + 1) * 512],
                                xr[:, (j % 2) * 512:(j % 2 + 1) * 512], ALU.add))

                @blk.gpsimd
                def _(g):
                    for tt in range(4):
                        for nc_ in range(8):
                            j = tt * 8 + nc_
                            sE.w(g, j + 1)
                            sOs[j % 2].inc(g.dma_start(
                                bass.AP(x2_dram.tensor,
                                        x2_dram.offset + nc_ * 128 * T + tt * 512,
                                        [[T, 128], [1, 512]]),
                                ob[:, (j % 2) * 512:(j % 2 + 1) * 512]), 16)
                    w(g, sOs[0])
                    w(g, sOs[1])
            bar()

    def emit_bvbc(bvbc):
        sB = C("bvb")
        with (
            nc.sbuf_tensor("bvrow", [1, D], F32) as bvr_g,
            nc.psum_tensor("bv_ps", [128, 512], F32) as bvps_g,
        ):
            bvr = bvr_g.ap()
            bvps = bvps_g.ap()
            with nc.Block() as blk:
                @blk.sync
                def _(sy):
                    sB.inc(sy.dma_start(bvr[:, :], bv_d[:]), 16)

                @blk.tensor
                def _(t):
                    sB.w(t, 16)
                    sB.inc(t.matmul(bvps[:, :], ones1[:, :], bvr[:, 0:512],
                                    start=True, stop=True))
                    sB.w(t, 18)
                    sB.inc(t.matmul(bvps[:, :], ones1[:, :], bvr[:, 512:1024],
                                    start=True, stop=True))

                @blk.vector
                def _(v):
                    sB.w(v, 17)
                    sB.inc(v.tensor_copy(bvbc[:, 0:512], bvps[:, :]))
                    sB.w(v, 19)
                    v.tensor_copy(bvbc[:, 512:1024], bvps[:, :])
            bar()

    def emit_band1():
        sBP = C("bprol")
        sBPg = C("bprolg")
        with (
            nc.sbuf_tensor("brsb", [1, 4095], F32) as brsb_g,
            nc.sbuf_tensor("etsb", [1, 2], F32) as etsb_g,
        ):
            brsb = brsb_g.ap()
            etsb = etsb_g.ap()
            with nc.Block() as blk:
                @blk.sync
                def _(sy):
                    sBP.inc(sy.dma_start(brsb[:, :], bandrev1[:]), 16)
                    sBP.inc(sy.dma_start(etsb[:, 0:1], etg[:, :]), 16)
                    sBP.inc(sy.dma_start(etsb[:, 1:2], isg[:, :]), 16)

                @blk.scalar
                def _(sc):
                    sBP.w(sc, 48)
                    sBP.inc(sc.activation(etsb[:, 0:1], etsb[:, 0:1], AF.Sigmoid))

                @blk.vector
                def _(v):
                    sBP.w(v, 49)
                    v.tensor_single_scalar(etsb[:, 0:1], etsb[:, 0:1], 8.0, ALU.mult)
                    sBP.inc(v.tensor_single_scalar(etsb[:, 1:2], etsb[:, 1:2], 8.0,
                                                   ALU.mult))
                    sBP.w(v, 50)
                    sBP.inc(v.tensor_scalar(brsb[:, :], brsb[:, :], etsb[:, 0:1],
                                            etsb[:, 1:2], ALU.mult, ALU.add))

                @blk.gpsimd
                def _(g):
                    sBP.w(g, 51)
                    sBPg.inc(g.dma_start(bdev_dram[:], brsb[:, :]), 16)
                    sBPg.w(g, 16)
                    for p in range(128):
                        sBPg.inc(g.dma_start(
                            bm1_dram[p:p + 1, :],
                            bass.AP(bdev_dram.tensor, bdev_dram.offset + 127 - p,
                                    [[1, 1], [1, BW]])), 16)
                    w(g, sBPg)
            bar()

    def emit_scramble_proj():
        sCL, sM, sE = C("p12cl"), C("p12mm"), C("p12ev")
        sCLF = C("p12clf")
        sWs = [C("p12wl0"), C("p12wl1")]
        sXs = [C("p12xl0"), C("p12xl1")]
        sSs = [C("p12st0"), C("p12st1")]
        with (
            nc.sbuf_tensor("csrc", [128, 8 * T], F32) as csrc_g,
            nc.sbuf_tensor("wost", [128, 2 * 2048], F32) as wost_g,
            nc.sbuf_tensor("x2st", [128, 2 * T], F32) as x2st_g,
            nc.sbuf_tensor("x3st", [128, 2 * T], F32) as x3st_g,
            nc.psum_tensor("p12_ps", [128, 2048], F32) as ps_g,
        ):
            csrc = csrc_g.ap()
            wost = wost_g.ap()
            x2st = x2st_g.ap()
            x3st = x3st_g.ap()
            ps = ps_g.ap()
            with nc.Block() as blk:
                @blk.sync
                def _(sy):
                    for di_, dc in enumerate([1, 2, 3, 4, 5, 6, 7, 0]):
                        if di_ >= 2:
                            sM.w(sy, di_ - 1)
                        for uh in range(2):
                            srcw = bass.AP(woT.tensor, woT.offset + dc * 128,
                                           [[D, 64], [64 * D, 16], [1, 128]])
                            sWs[di_ % 2].inc(sy.dma_start(
                                bass.AP(wost.tensor,
                                        wost.offset + uh * 64 * (2 * 2048)
                                        + (di_ % 2) * 2048,
                                        [[2 * 2048, 64], [128, 16], [1, 128]]),
                                srcw), 16)
                        if di_ >= 2:
                            sE.w(sy, di_ - 1)
                        sXs[di_ % 2].inc(sy.dma_start(
                            x2st[:, (di_ % 2) * T:(di_ % 2 + 1) * T],
                            x2_dram[dc * 128:(dc + 1) * 128, :]), 16)

                @blk.tensor
                def _(t):
                    w(t, sCL)
                    sCLF.w(t, 16)
                    for di_, dc in enumerate([1, 2, 3, 4, 5, 6, 7, 0]):
                        sWs[di_ % 2].w(t, 32 * (di_ // 2 + 1))
                        if di_ >= 1:
                            sE.w(t, di_)
                        ins = None
                        for s in range(16):
                            for u in range(2):
                                lhs = bass.AP(wost.tensor,
                                              wost.offset + u * 64 * (2 * 2048)
                                              + (di_ % 2) * 2048 + s * 128,
                                              [[2 * 2048, 64], [1, 128]])
                                for m in range(8):
                                    k = u * 8 + m
                                    rhs = bass.AP(csrc.tensor,
                                                  csrc.offset + m * T
                                                  + (u * 64) * (8 * T) + s,
                                                  [[8 * T, 64], [16, 128]])
                                    ins = t.matmul(
                                        ps[:, k * 128:(k + 1) * 128], lhs, rhs,
                                        start=(s == 0 and k % 4 == 0),
                                        stop=(s == 15))
                        sM.inc(ins)

                @blk.vector
                def _(v):
                    for di_, dc in enumerate([1, 2, 3, 4, 5, 6, 7, 0]):
                        sM.w(v, di_ + 1)
                        sXs[di_ % 2].w(v, 16 * (di_ // 2 + 1))
                        if di_ >= 2:
                            sSs[di_ % 2].w(v, 16 * (di_ // 2))
                        sE.inc(v.scalar_tensor_tensor(
                            x3st[:, (di_ % 2) * T:(di_ % 2 + 1) * T],
                            ps[:, :], bo_sb[:, dc:dc + 1],
                            x2st[:, (di_ % 2) * T:(di_ % 2 + 1) * T],
                            ALU.add, ALU.add))

                @blk.gpsimd
                def _(g):
                    with g.register("p12r") as r12:
                        pid = g.partition_id()
                        for m in range(8):
                            g.reg_mul(r12, pid, 128 * T)
                            g.reg_add(r12, r12, m * D * T)
                            sCL.inc(g.dma_start(
                                csrc[:, m * T:(m + 1) * T],
                                bass.AP(agctx.tensor, r12,
                                        [[T, 128], [1, T]])), 16)
                    sCL.w(g, 128)
                    sCLF.inc(g.dma_start(x3st[:, 0:4], x2st[:, 0:4]), 16)
                    if "d_csrc" in dump_aps:
                        sDc = C("dmpg")
                        sDc.inc(g.dma_start(dump_aps["d_csrc"][:, :], csrc[:, :]), 16)
                        w(g, sDc)
                    for di_, dc in enumerate([1, 2, 3, 4, 5, 6, 7, 0]):
                        sE.w(g, di_ + 1)
                        sSs[di_ % 2].inc(g.dma_start(
                            x3_dram[dc * 128:(dc + 1) * 128, :],
                            x3st[:, (di_ % 2) * T:(di_ % 2 + 1) * T]), 16)
                    w(g, sSs[0])
                    w(g, sSs[1])
            bar()

    def emit_moe():
        TT = 512
        (sXL, sLSQ, sLSM, sLVA, sLRT, sLST, sLBC, sLAP) = (
            C("m" + s) for s in ("xl", "lsq", "lsm", "lva", "lrt", "lst", "lbc", "lap"))
        (sGM, sGB, sTR, sMK, sT2, sMF, sMB, sXE, sCU, sFW) = (
            C("m" + s) for s in
            ("gm", "gb", "tr", "mk", "t2", "mf", "mb", "xe", "cu", "fw"))
        (sHM, sGE, sW2T, sYE, sEV) = (
            C("m" + s) for s in ("hm", "ge", "w2t", "ye", "ev"))
        sW1s = [C("mw1a"), C("mw1b")]
        sW2s = [C("mw2a"), C("mw2b")]
        sOSs2 = [C("mosa"), C("mosb")]
        import contextlib
        with contextlib.ExitStack() as _st:
            x3_g = _st.enter_context(nc.sbuf_tensor("m_x3", [128, DC * TT], F32))
            x3ln_g = _st.enter_context(nc.sbuf_tensor("m_x3ln", [128, DC * TT], F32))
            xe_g = _st.enter_context(nc.sbuf_tensor("m_xe", [128, DC * TT], F32))
            h_g = _st.enter_context(nc.sbuf_tensor("m_h", [128, 32 * TT], F32))
            ysb_g = _st.enter_context(nc.sbuf_tensor("m_ysb", [128, DC * TT], F32))
            w1st_g = _st.enter_context(nc.sbuf_tensor("m_w1st", [128, 2 * 1024], F32))
            w2st_g = _st.enter_context(nc.sbuf_tensor("m_w2st", [128, 2 * 1024], F32))
            lg_g = _st.enter_context(nc.sbuf_tensor("m_lg", [8, TT], F32))
            mtk_g = _st.enter_context(nc.sbuf_tensor("m_mtk", [128, 8], F32))
            mtk2_g = _st.enter_context(nc.sbuf_tensor("m_mtk2", [128, 8], F32))
            gcap_g = _st.enter_context(nc.sbuf_tensor("m_gcap", [128, 32], F32))
            run_g = _st.enter_context(nc.sbuf_tensor("m_run", [128, 2], F32))
            mfm_g = _st.enter_context(nc.sbuf_tensor("m_mfm", [8, TT], F32))
            mbc_g = _st.enter_context(nc.sbuf_tensor("m_mbc", [128, TT], F32))
            ost_g = _st.enter_context(nc.sbuf_tensor("m_ost", [128, 2 * TT], F32))
            msq_g = _st.enter_context(nc.sbuf_tensor("m_sq", [128, DC * TT], F32))
            mmean_g = _st.enter_context(nc.sbuf_tensor("m_mean", [1, TT], F32))
            mvarp_g = _st.enter_context(nc.sbuf_tensor("m_varp", [1, TT], F32))
            mt1_g = _st.enter_context(nc.sbuf_tensor("m_t1", [1, TT], F32))
            minv_g = _st.enter_context(nc.sbuf_tensor("m_inv", [1, TT], F32))
            selsb_g = _st.enter_context(nc.sbuf_tensor("m_selsb", [8, 8 * 128], F32))
            id8_g = _st.enter_context(nc.sbuf_tensor("m_id8", [8, 8], F32))
            id128_g = _st.enter_context(nc.sbuf_tensor("m_id128", [128, 128], F32))
            wg_g = _st.enter_context(nc.sbuf_tensor("m_wg", [128, 64], F32))
            ucum_g = _st.enter_context(nc.sbuf_tensor("m_ucum", [8, 8], F32))
            ls_g = _st.enter_context(nc.psum_tensor("m_ls12", [33, TT], F32))
            gbc_g = _st.enter_context(nc.psum_tensor("m_gbc", [128, 512], F32))
            hp0_g = _st.enter_context(nc.psum_tensor("m_hps0", [128, 512], F32))
            hp1_g = _st.enter_context(nc.psum_tensor("m_hps1", [128, 512], F32))
            y0_g = _st.enter_context(nc.psum_tensor("m_y0", [128, 512], F32))
            y1_g = _st.enter_context(nc.psum_tensor("m_y1", [128, 512], F32))
            y2_g = _st.enter_context(nc.psum_tensor("m_y2", [128, 512], F32))
            y3_g = _st.enter_context(nc.psum_tensor("m_y3", [128, 512], F32))
            x3sb = x3_g.ap()
            x3ln = x3ln_g.ap()
            xesb = xe_g.ap()
            hsb = h_g.ap()
            ysb = ysb_g.ap()
            w1st = w1st_g.ap()
            w2st = w2st_g.ap()
            lg = lg_g.ap()
            mtk = mtk_g.ap()
            mtk2 = mtk2_g.ap()
            gcap = gcap_g.ap()
            runb = run_g.ap()
            mfm = mfm_g.ap()
            mbcsb = mbc_g.ap()
            ost = ost_g.ap()
            msq = msq_g.ap()
            mean, varp, t1v, inv = (mmean_g.ap(), mvarp_g.ap(), mt1_g.ap(),
                                    minv_g.ap())
            selsb = selsb_g.ap()
            id8 = id8_g.ap()
            id128 = id128_g.ap()
            wg_sb = wg_g.ap()
            ucum = ucum_g.ap()
            ls12 = ls_g.ap()
            gbc = gbc_g.ap()
            hps = [hp0_g.ap(), hp1_g.ap()]
            ypsl = [y0_g.ap(), y1_g.ap(), y2_g.ap(), y3_g.ap()]

            prol = C("mprol")
            with nc.Block() as blk:
                @blk.sync
                def _(sy):
                    prol.inc(sy.dma_start(selsb[:, :], sel_c[:, :]), 16)
                    prol.inc(sy.dma_start(id8[:, :], id8_c[:, :]), 16)
                    prol.inc(sy.dma_start(id128[:, :], id128_c[:, :]), 16)
                    prol.inc(sy.dma_start(ucum[:, :], ucum_c[:, :]), 16)
                    for dc in range(DC):
                        prol.inc(sy.dma_start(wg_sb[:, dc * 8:(dc + 1) * 8],
                                              wg_d[dc * 128:(dc + 1) * 128, :]), 16)
                    w(sy, prol)
            bar()

            for tt in range(4):
                with nc.Block() as blk:
                    @blk.sync
                    def _(sy):
                        if tt >= 1:
                            sEV.w(sy, tt * 8)
                            sXE.w(sy, tt * 8)
                        sXL.inc(sy.dma_start(
                            bass.AP(x3sb.tensor, x3sb.offset,
                                    [[DC * TT, 128], [TT, DC], [1, TT]]),
                            bass.AP(x3_dram.tensor,
                                    x3_dram.offset + tt * TT,
                                    [[T, 128], [128 * T, DC], [1, TT]])), 16)

                    @blk.scalar
                    def _(sc):
                        sXL.w(sc, 16 * (tt + 1))
                        for dc in range(DC):
                            i = tt * DC + dc
                            sLSQ.inc(sc.activation(
                                msq[:, dc * TT:(dc + 1) * TT],
                                x3sb[:, dc * TT:(dc + 1) * TT], AF.Square))
                        sLVA.w(sc, tt + 1)
                        sLRT.inc(sc.activation(t1v, varp, AF.Sqrt))

                    @blk.tensor
                    def _(t):
                        sXL.w(t, 16 * (tt + 1))
                        for dc in range(DC):
                            i = tt * DC + dc
                            t.matmul(ls12[0:1, :], onesc[:, :],
                                     x3sb[:, dc * TT:(dc + 1) * TT],
                                     start=(dc == 0), stop=(dc == DC - 1))
                            sLSQ.w(t, i + 1)
                            ins = t.matmul(ls12[32:33, :], onesc[:, :],
                                           msq[:, dc * TT:(dc + 1) * TT],
                                           start=(dc == 0), stop=(dc == DC - 1))
                        sLSM.inc(ins)
                        sLST.w(t, tt + 1)
                        t.matmul(gbc[:, :], ones1[:, :], inv, start=True, stop=True)
                        sLBC.inc(t.matmul(hps[0][:, :], ones1[:, :], mean,
                                          start=True, stop=True))

                    @blk.vector
                    def _(v):
                        sLSM.w(v, tt + 1)
                        v.tensor_single_scalar(mean, ls12[0:1, :], 1.0 / 1024.0,
                                               ALU.mult)
                        v.tensor_single_scalar(varp, ls12[32:33, :], 1.0 / 1024.0,
                                               ALU.mult)
                        v.tensor_tensor(t1v, mean, mean, ALU.mult)
                        v.tensor_tensor(varp, varp, t1v, ALU.subtract)
                        sLVA.inc(v.tensor_single_scalar(varp, varp, EPS, ALU.add))
                        sLRT.w(v, tt + 1)
                        v.reciprocal(inv, t1v)
                        for _i in range(2):
                            v.tensor_tensor(t1v, inv, inv, ALU.mult)
                            v.tensor_tensor(t1v, t1v, varp, ALU.mult)
                            v.tensor_scalar(t1v, t1v, -0.5, 1.5, ALU.mult, ALU.add)
                            v.tensor_tensor(inv, inv, t1v, ALU.mult)
                        sLST.inc(v.tensor_tensor(mean, mean, inv, ALU.mult))
                        sLBC.w(v, tt + 1)
                        ins = None
                        for dc in range(DC):
                            sl = slice(dc * TT, (dc + 1) * TT)
                            v.tensor_tensor(x3ln[:, sl], x3sb[:, sl],
                                            gbc[:, :], ALU.mult)
                            ins = v.tensor_tensor(x3ln[:, sl], x3ln[:, sl],
                                                  hps[0][:, :], ALU.subtract)
                        sLAP.inc(ins)

                    @blk.gpsimd
                    def _(g):
                        g.memset(ysb[:, :], 0.0)
                bar()
                gm0, gb0, tr0, mk0, t20, mf0 = (sGM.rval, sGB.rval, sTR.rval,
                                                sMK.rval, sT2.rval, sMF.rval)
                cu0, fw0 = sCU.rval, sFW.rval
                with nc.Block() as blk:
                    @blk.tensor
                    def _(t):
                        ins = None
                        for dc in range(DC):
                            ins = t.matmul(gbc[0:8, :], wg_sb[:, dc * 8:(dc + 1) * 8],
                                           x3ln[:, dc * TT:(dc + 1) * TT],
                                           start=(dc == 0), stop=(dc == DC - 1))
                        sGM.inc(ins)
                        for q in range(4):
                            sGB.w(t, gb0 + 1)
                            if q >= 1:
                                sFW.w(t, fw0 + q)
                                sMK.w(t, mk0 + 3 * q)
                            sTR.inc(t.transpose(gbc[0:128, 0:8],
                                                lg[:, q * 128:(q + 1) * 128],
                                                id8[:, :]))
                            sMK.w(t, mk0 + 3 * q + 3)
                            sT2.inc(t.transpose(gbc[0:8, 64:192],
                                                mtk[:, :], id128[:, :]))
                            sMF.w(t, mf0 + q + 1)
                            sCU.inc(t.matmul(gbc[0:8, 192:320], ucum[:, :],
                                             mfm[:, q * 128:(q + 1) * 128],
                                             start=True, stop=True))

                    @blk.scalar
                    def _(sc):
                        sGM.w(sc, gm0 + 1)
                        sGB.inc(sc.activation(lg[:, :], gbc[0:8, :], AF.Identity,
                                              bias=bg_sb[:, :]))
                        for q in range(4):
                            sT2.w(sc, t20 + q + 1)
                            sMF.inc(sc.copy(mfm[:, q * 128:(q + 1) * 128],
                                            gbc[0:8, 64:192]))

                    @blk.vector
                    def _(v):
                        for q in range(4):
                            sTR.w(v, tr0 + q + 1)
                            sMK.inc(v.tensor_copy(mtk2[:, :], gbc[0:128, 0:8]))
                            sMK.w(v, mk0 + 3 * q + 1)
                            sMK.inc(v.reduce_max(runb[:, 0:1], mtk2[:, :],
                                                 mybir.AxisListType.X))
                            sMK.w(v, mk0 + 3 * q + 2)
                            sMK.inc(v.tensor_single_scalar(mtk[:, :], mtk2[:, :],
                                                           runb[:, 0:1], ALU.is_ge))
                            sCU.w(v, cu0 + q + 1)
                            sFW.inc(v.scalar_tensor_tensor(
                                mfm[:, q * 128:(q + 1) * 128],
                                gbc[0:8, 192:320], 0.0,
                                mfm[:, q * 128:(q + 1) * 128],
                                ALU.is_equal, ALU.mult))
                            if "d_gdbg" in dump_aps and tt == 0 and q == 0:
                                v.tensor_copy(gcap[:, 0:8], mtk2[:, :])
                                v.tensor_copy(gcap[:, 8:9], runb[:, 0:1])
                                v.tensor_copy(gcap[:, 9:17], mtk[:, :])
                bar()
                if "d_mask" in dump_aps:
                    sD = C(f"dmask{tt}")
                    with nc.Block() as blk:
                        @blk.gpsimd
                        def _(g):
                            sD.inc(g.dma_start(
                                dump_aps["d_mask"][:, tt * TT:(tt + 1) * TT],
                                mfm[:, :]), 16)
                            w(g, sD)
                    bar()
                hm0, ge0, mb0, xe0 = sHM.rval, sGE.rval, sMB.rval, sXE.val
                w2t0, ye0 = sW2T.rval, sYE.rval
                w1s0 = [sW1s[0].rval, sW1s[1].rval]
                w2s0 = [sW2s[0].rval, sW2s[1].rval]
                with nc.Block() as blk:
                    @blk.sync
                    def _(sy):
                        for e in range(8):
                            for hc in range(32):
                                i = e * 32 + hc
                                if i >= 2:
                                    sHM.w(sy, hm0 + i - 1)
                                src = bass.AP(
                                    w1_d.tensor,
                                    w1_d.offset + e * D * 4096 + hc * 128,
                                    [[4096, 128], [4096 * 128, 8], [1, 128]])
                                sW1s[i % 2].inc(sy.dma_start(
                                    bass.AP(w1st.tensor,
                                            w1st.offset + (i % 2) * 1024,
                                            [[2 * 1024, 128], [128, 8], [1, 128]]),
                                    src), 16)

                    @blk.scalar
                    def _(sc):
                        for e in range(8):
                            for hc in range(32):
                                i = e * 32 + hc
                                sHM.w(sc, hm0 + i + 1)
                                sGE.inc(sc.activation(
                                    hsb[:, hc * TT:(hc + 1) * TT],
                                    hps[i % 2][:, :], AF.Gelu_apprx_tanh))

                    @blk.gpsimd
                    def _(g):
                        for e in range(8):
                            for half in range(2):
                                for hc in range(32):
                                    ii = (2 * e + half) * 32 + hc
                                    if ii >= 2:
                                        sW2T.w(g, w2t0 + ii - 1)
                                    src = bass.AP(
                                        w2_d.tensor,
                                        w2_d.offset + (e * 4096 + hc * 128) * D
                                        + half * 512,
                                        [[D, 128], [1, 512]])
                                    sW2s[ii % 2].inc(g.dma_start(
                                        w2st[:, (ii % 2) * 512:(ii % 2 + 1) * 512],
                                        src), 16)

                    @blk.tensor
                    def _(t):
                        for e in range(8):
                            sFW.w(t, fw0 + 4)
                            sLAP.w(t, tt + 1)
                            if e >= 1:
                                sXE.w(t, xe0 + e)
                            sMB.inc(t.matmul(gbc[0:128, :],
                                             selsb[:, e * 128:(e + 1) * 128],
                                             mfm[:, :], start=True, stop=True))
                            for hc in range(32):
                                i = e * 32 + hc
                                sW1s[i % 2].w(t, w1s0[i % 2] + 16 * (i // 2 + 1))
                                sXE.w(t, xe0 + e + 1)
                                if i >= 2:
                                    sGE.w(t, ge0 + i - 1)
                                ins = None
                                for dc in range(DC):
                                    ins = t.matmul(
                                        hps[i % 2][:, :],
                                        w1st[:, (i % 2) * 1024 + dc * 128:
                                             (i % 2) * 1024 + (dc + 1) * 128],
                                        xesb[:, dc * TT:(dc + 1) * TT],
                                        start=(dc == 0), stop=(dc == DC - 1))
                                sHM.inc(ins)
                            for half in range(2):
                                if 2 * e + half >= 1:
                                    sYE.w(t, ye0 + 2 * e + half)
                                for hc in range(32):
                                    ii = (2 * e + half) * 32 + hc
                                    sW2s[ii % 2].w(t, w2s0[ii % 2] + 16 * (ii // 2 + 1))
                                    sGE.w(t, ge0 + e * 32 + hc + 1)
                                    ins = None
                                    for di in range(4):
                                        dcw = half * 4 + di
                                        ins = t.matmul(
                                            ypsl[di][:, :],
                                            w2st[:, (ii % 2) * 512 + di * 128:
                                                 (ii % 2) * 512 + (di + 1) * 128],
                                            hsb[:, hc * TT:(hc + 1) * TT],
                                            start=(hc == 0), stop=(hc == 31))
                                    sW2T.inc(ins)

                    @blk.vector
                    def _(v):
                        for e in range(8):
                            sMB.w(v, mb0 + e + 1)
                            v.tensor_copy(mbcsb[:, :], gbc[0:128, :])
                            ins = None
                            for dc in range(DC):
                                ins = v.tensor_tensor(
                                    xesb[:, dc * TT:(dc + 1) * TT],
                                    x3ln[:, dc * TT:(dc + 1) * TT],
                                    mbcsb[:, :], ALU.mult)
                            sXE.inc(ins)
                            for half in range(2):
                                sW2T.w(v, w2t0 + 32 * (2 * e + half + 1))
                                ins = None
                                for di in range(4):
                                    dcw = half * 4 + di
                                    ins = v.tensor_tensor(
                                        ysb[:, dcw * TT:(dcw + 1) * TT],
                                        ysb[:, dcw * TT:(dcw + 1) * TT],
                                        ypsl[di][:, :], ALU.add)
                                sYE.inc(ins)
                bar()
                with nc.Block() as blk:
                    @blk.vector
                    def _(v):
                        for dc in range(DC):
                            j = tt * 8 + dc
                            if j >= 2:
                                sOSs2[j % 2].w(v, 16 * (j // 2))
                            sEV.inc(v.tensor_tensor(
                                ost[:, (j % 2) * TT:(j % 2 + 1) * TT],
                                ysb[:, dc * TT:(dc + 1) * TT],
                                x3sb[:, dc * TT:(dc + 1) * TT], ALU.add))

                    @blk.gpsimd
                    def _(g):
                        for dc in range(DC):
                            j = tt * 8 + dc
                            sEV.w(g, j + 1)
                            sOSs2[j % 2].inc(g.dma_start(
                                bass.AP(out_d.tensor,
                                        out_d.offset + dc * 128 * T + tt * TT,
                                        [[T, 128], [1, TT]]),
                                ost[:, (j % 2) * TT:(j % 2 + 1) * TT]), 16)
                bar()
            if "d_gdbg" in dump_aps:
                sDg = C("dmpg")
                with nc.Block() as blk:
                    @blk.gpsimd
                    def _(g):
                        sDg.inc(g.dma_start(dump_aps["d_gdbg"][:, :], gcap[:, :]), 16)
                        w(g, sDg)
                bar()
            with nc.Block() as blk:
                @blk.gpsimd
                def _(g):
                    w(g, sOSs2[0])
                    w(g, sOSs2[1])
            bar()

    if "sm0" in dumps:
        dump_smalls()
    emit_ln_stream(x_in, xln_dram)
    if "sm1" in dumps:
        dump_smalls()
    dump_rows("d_xln", xln_dram, D)
    with (
        nc.sbuf_tensor("vsb0", [128, 16 * 1040], F32) as vsb0_g,
        nc.sbuf_tensor("ksb0", [128, DC * T], F32) as ksb0_g,
    ):
        vsb0, ksb0 = vsb0_g.ap(), ksb0_g.ap()
        if upto >= 2:
            emit_v(vsb0,
                   lambda sp, dc: w_qkv[dc * 128:(dc + 1) * 128,
                                        3072 + sp * 256:3072 + (sp + 1) * 256],
                   None)
            chunks0 = [(m, "q") for m in range(16)] + [(m, "k") for m in range(8)]

            def wsrc0(mch, kind):
                c0 = mch * 128 if kind == "q" else 2048 + mch * 128
                return bass.AP(w_qkv.tensor, w_qkv.offset + c0,
                               [[4096, 128], [128 * 4096, 8], [1, 128]])

            emit_qk(ksb0, chunks0, wsrc0, lambda mch, kind: None, q0_dram)
            if "sm2" in dumps:
                dump_smalls()
            if "d_v" in dump_aps:
                sD = C("dmpg")
                with nc.Block() as blk:
                    @blk.gpsimd
                    def _(g):
                        sD.inc(g.dma_start(dump_aps["d_v"][:, :], vsb0[:, :]), 16)
                        w(g, sD)
                bar()
            if "d_k" in dump_aps:
                sD = C("dmpg")
                with nc.Block() as blk:
                    @blk.gpsimd
                    def _(g):
                        for dc in range(DC):
                            sD.inc(g.dma_start(
                                dump_aps["d_k"][dc * 128:(dc + 1) * 128, :],
                                ksb0[:, dc * T:(dc + 1) * T]), 16)
                        w(g, sD)
                bar()
            dump_rows("d_q0", q0_dram, 2048)
        if upto >= 3:
            emit_attention(0, ksb0, vsb0, q0_dram, bm0, ctx0_dram)
            if "sm3" in dumps:
                dump_smalls()
            dump_rows("d_ctx0", ctx0_dram, 2048)
    if upto >= 4:
        emit_wout0()
        if "sm4" in dumps:
            dump_smalls()
        dump_rows("d_x1", x1_dram, D)
    if upto >= 5:
        emit_ln_stream(x1_dram, xln_dram)
        dump_rows("d_xln2", xln_dram, D)
        emit_cb()
        dump_rows("d_x2", x2_dram, D)
    if upto >= 6:
        emit_ln_stream(x2_dram, xln_dram)
        with (
            nc.sbuf_tensor("vsb1", [128, 16 * 1040], F32) as vsb1_g,
            nc.sbuf_tensor("ksb1", [128, DC * T], F32) as ksb1_g,
            nc.sbuf_tensor("bvbc", [128, D], F32) as bvbc_g,
        ):
            vsb1, ksb1, bvbc = vsb1_g.ap(), ksb1_g.ap(), bvbc_g.ap()
            emit_bvbc(bvbc)
            emit_v(vsb1,
                   lambda sp, dc: wvT[dc * 128:(dc + 1) * 128,
                                      sp * 256:(sp + 1) * 256],
                   bvbc)
            chunks1 = [(m, "q") for m in range(8)] + [(m, "k") for m in range(8)]

            def wsrc1(mch, kind):
                wsrc = wqT if kind == "q" else wkT
                return bass.AP(wsrc.tensor, wsrc.offset + mch * 128,
                               [[D, 128], [128 * D, 8], [1, 128]])

            def bias1(mch, kind):
                return (bq_sb if kind == "q" else bk_sb)[:, mch:mch + 1]

            emit_qk(ksb1, chunks1, wsrc1, bias1, q1_dram)
            dump_rows("d_q1", q1_dram, D)
            emit_band1()
            dump_rows("d_bm1", bm1_dram, 128)
            if upto >= 7:
                emit_attention(1, ksb1, vsb1, q1_dram, bm1_dram, ctx1_dram)
                dump_rows("d_ctx1", ctx1_dram, D)
    if upto >= 8:
        sA2 = C("a2a")
        with nc.Block() as blk:
            @blk.gpsimd
            def _(g):
                ins = g.collective_compute(
                    "AllGather", ALU.bypass, replica_groups=[list(range(8))],
                    ins=[ctx1_dram[:, :]], outs=[agctx[:, :]])
                sA2.inc(ins, 1)
                sA2.w(g, 1)
        bar()
        nc.all_core_barrier()
        bar()
        dump_rows("d_a2a", agctx, 2 * D)
    if upto >= 9:
        emit_scramble_proj()
        dump_rows("d_x3", x3_dram, D)
    if upto >= 10:
        emit_moe()
    bar()
    return nc


def _host_inputs(inputs):
    f32 = np.float32
    c = np.ascontiguousarray
    x = np.asarray(inputs["L0_attn_inputs"], f32)
    tb = np.asarray(inputs["L1_attn_time_bias"], f32).reshape(500)
    idx = np.clip(np.arange(4095) - 2047 + 250, 0, 499)
    band1 = tb[idx]
    band1_rev = c(band1[::-1])
    common = {
        "w_qkv": c(np.asarray(inputs["L0_attn_W_qkv"], f32)),
        "w_out": c(np.asarray(inputs["L0_attn_W_out"], f32)),
        "cbt": c(np.asarray(inputs["L0_ffn_CB"], f32).T),
        "scb": c(np.asarray(inputs["L0_ffn_SCB"], f32)),
        "cbias": c(np.asarray(inputs["L0_ffn_bias"], f32)),
        "wqT": c(np.asarray(inputs["L1_attn_Wq"], f32).T),
        "wkT": c(np.asarray(inputs["L1_attn_Wk"], f32).T),
        "wvT": c(np.asarray(inputs["L1_attn_Wv"], f32).T),
        "woT": c(np.asarray(inputs["L1_attn_Wo"], f32).T),
        "bq": c(np.asarray(inputs["L1_attn_bq"], f32)),
        "bk": c(np.asarray(inputs["L1_attn_bk"], f32)),
        "bv": c(np.asarray(inputs["L1_attn_bv"], f32)),
        "bo": c(np.asarray(inputs["L1_attn_bo"], f32)),
        "etg": c(np.asarray(inputs["L1_attn_et_gate"], f32).reshape(1, 1)),
        "isg": c(np.asarray(inputs["L1_attn_is_gate"], f32).reshape(1, 1)),
        "bandrev1": band1_rev,
        "w1": c(np.asarray(inputs["L1_ffn_W1"], f32)),
        "w2": c(np.asarray(inputs["L1_ffn_W2"], f32)),
        "wg": c(np.asarray(inputs["L1_ffn_Wg"], f32)),
        "bg": c(np.asarray(inputs["L1_ffn_bg"], f32)),
    }
    in_maps = []
    for j in range(8):
        m = dict(common)
        m["x_fm"] = c(x[j].T)
        in_maps.append(m)
    return in_maps


def kernel(**inputs):
    in_maps = _host_inputs(inputs)
    nc = build()
    res = run_bass_kernel_spmd(nc, in_maps, core_ids=list(range(8)))
    out = np.empty((8, T, D), np.float32)
    for j in range(8):
        out[j] = np.asarray(res.results[j]["out_fm"]).T
    return out


# version 03
# speedup vs baseline: 1.1740x; 1.1380x over previous
import math
import numpy as np
import concourse.bass as bass
import concourse.mybir as mybir
from concourse.bass_utils import run_bass_kernel_spmd

F32 = mybir.dt.float32
AF = mybir.ActivationFunctionType
ALU = mybir.AluOpType

D = 1024
T = 2048
DC = 8
QT = 256
NQT = T // QT
BW = 3968
BC0 = 1920
NEG = -1.0e30
EPS = 1e-5


def _alibi_slopes32():
    base = 2.0 ** (-(2.0 ** (-(math.log2(32) - 3))))
    powers = np.arange(1, 33, dtype=np.float32)
    return (np.float32(base) ** powers).astype(np.float32)


def _bandmat_l0():
    band = np.arange(4095, dtype=np.float64) - 2047.0
    band[band > 0] = NEG
    p = np.arange(128)[:, None]
    c = np.arange(BW)[None, :]
    idx = p - c + (BW - 1)
    ok = (idx >= 0) & (idx <= 4094)
    return (band[np.clip(idx, 0, 4094)] * ok).astype(np.float32)


class Cnt:
    def __init__(self, sem):
        self.sem = sem
        self.val = 0

    def inc(self, ins, n=1):
        ins.then_inc(self.sem, n)
        self.val += n
        return self.val


class Rel:

    def __init__(self, cnt):
        self.c = cnt
        self.b = cnt.val

    def inc(self, ins, n=1):
        return self.c.inc(ins, n)

    def w(self, eng, v):
        if self.b + v > 0:
            eng.wait_ge(self.c.sem, self.b + v)

    @property
    def val(self):
        return self.c.val

    @property
    def rval(self):
        return self.c.val - self.b

    @property
    def sem(self):
        return self.c.sem


def build(upto=99, dumps=(), race=False):
    nc = bass.Bass("TRN2", target_bir_lowering=False,
                   detect_race_conditions=race, num_devices=8)
    dumps = set(dumps)

    x_in = nc.dram_tensor("x_fm", [D, T], F32, kind="ExternalInput").ap()
    w_qkv = nc.dram_tensor("w_qkv", [D, 4096], F32, kind="ExternalInput").ap()
    w_out = nc.dram_tensor("w_out", [2048, D], F32, kind="ExternalInput").ap()
    cbt = nc.dram_tensor("cbt", [D, D], F32, kind="ExternalInput").ap()
    scb_d = nc.dram_tensor("scb", [D], F32, kind="ExternalInput").ap()
    cbias_d = nc.dram_tensor("cbias", [D], F32, kind="ExternalInput").ap()
    wqT = nc.dram_tensor("wqT", [D, D], F32, kind="ExternalInput").ap()
    wkT = nc.dram_tensor("wkT", [D, D], F32, kind="ExternalInput").ap()
    wvT = nc.dram_tensor("wvT", [D, D], F32, kind="ExternalInput").ap()
    woT = nc.dram_tensor("woT", [D, D], F32, kind="ExternalInput").ap()
    bq_d = nc.dram_tensor("bq", [D], F32, kind="ExternalInput").ap()
    bk_d = nc.dram_tensor("bk", [D], F32, kind="ExternalInput").ap()
    bv_d = nc.dram_tensor("bv", [D], F32, kind="ExternalInput").ap()
    bo_d = nc.dram_tensor("bo", [D], F32, kind="ExternalInput").ap()
    etg = nc.dram_tensor("etg", [1, 1], F32, kind="ExternalInput").ap()
    isg = nc.dram_tensor("isg", [1, 1], F32, kind="ExternalInput").ap()
    bandrev1 = nc.dram_tensor("bandrev1", [4095], F32, kind="ExternalInput").ap()
    w1_d = nc.dram_tensor("w1", [8, D, 4096], F32, kind="ExternalInput").ap()
    w2_d = nc.dram_tensor("w2", [8, 4096, D], F32, kind="ExternalInput").ap()
    wg_d = nc.dram_tensor("wg", [D, 8], F32, kind="ExternalInput").ap()
    bg_d = nc.dram_tensor("bg", [8], F32, kind="ExternalInput").ap()
    out_d = nc.dram_tensor("out_fm", [D, T], F32, kind="ExternalOutput").ap()

    q0_dram = nc.dram_tensor("q0_scr", [2048, T], F32).ap()
    q1_dram = nc.dram_tensor("q1_scr", [D, T], F32).ap()
    xln_dram = nc.dram_tensor("xln_scr", [D, T], F32).ap()
    x1_dram = nc.dram_tensor("x1_scr", [D, T], F32).ap()
    x2_dram = nc.dram_tensor("x2_scr", [D, T], F32).ap()
    x3_dram = nc.dram_tensor("x3_scr", [D, T], F32).ap()
    ctx0_dram = nc.dram_tensor("ctx0_scr", [2048, T], F32).ap()
    ctx1_dram = nc.dram_tensor("ctx1_scr", [D, T], F32).ap()
    agctx = nc.dram_tensor("agctx", [8 * D, T], F32).ap()
    bm1_dram = nc.dram_tensor("bm1_scr", [128, BW], F32).ap()
    bdev_dram = nc.dram_tensor("bdev_scr", [4095], F32).ap()

    bm0 = nc.inline_tensor(_bandmat_l0(), name="bm0").ap()
    selmat = np.zeros((8, 8 * 128), np.float32)
    for e in range(8):
        selmat[e, e * 128:(e + 1) * 128] = 1.0
    sel_c = nc.inline_tensor(selmat, name="selc").ap()
    id8_c = nc.inline_tensor(np.eye(8, dtype=np.float32), name="id8c").ap()
    id128_c = nc.inline_tensor(np.eye(128, dtype=np.float32), name="id128c").ap()
    ucum_np = np.triu(np.ones((8, 8), np.float32), 1).T
    ucum_c = nc.inline_tensor(np.ascontiguousarray(ucum_np.T), name="ucumc").ap()
    slopes8 = [float(s) * 8.0 for s in _alibi_slopes32()]

    dump_aps = {}
    for name, shape in [("d_xln", [D, T]), ("d_k", [D, T]), ("d_v", [128, 16 * 1040]),
                        ("d_q0", [2048, T]), ("d_ctx0", [2048, T]), ("d_x1", [D, T]),
                        ("d_x2", [D, T]), ("d_ctx1", [D, T]), ("d_a2a", [2 * D, T]),
                        ("d_x3", [D, T]), ("d_bm1", [128, BW]), ("d_q1", [D, T]),
                        ("d_mask", [8, T]), ("d_xln2", [D, T]), ("d_scbq", [128, 16]), ("d_csrc", [128, 8 * T]), ("d_gdbg", [128, 32])]:
        if name in dumps:
            dump_aps[name] = nc.dram_tensor(name, shape, F32, kind="ExternalOutput").ap()

    onesc = nc.alloc_sbuf_tensor("onesc", [128, 1], F32).ap()
    ones1 = nc.alloc_sbuf_tensor("ones1", [1, 128], F32).ap()
    scb_sb = nc.alloc_sbuf_tensor("scb_sb", [128, 8], F32).ap()
    cbias_sb = nc.alloc_sbuf_tensor("cbias_sb", [128, 8], F32).ap()
    bq_sb = nc.alloc_sbuf_tensor("bq_sb", [128, 8], F32).ap()
    bk_sb = nc.alloc_sbuf_tensor("bk_sb", [128, 8], F32).ap()
    bo_sb = nc.alloc_sbuf_tensor("bo_sb", [128, 8], F32).ap()
    bg_sb = nc.alloc_sbuf_tensor("bg_sb", [8, 1], F32).ap()
    r127 = nc.alloc_sbuf_tensor("r127", [128, 1], F32).ap()
    r127b = nc.alloc_sbuf_tensor("r127b", [128, 1], F32).ap()
    scbq_sb = nc.alloc_sbuf_tensor("scbq_sb", [128, 8], F32).ap()
    onespad_c = nc.alloc_sbuf_tensor("onespad", [128, 64], F32).ap()

    sems = {}

    def C(name):
        if name not in sems:
            sems[name] = Cnt(nc.alloc_semaphore(name))
        return Rel(sems[name])

    def bar():
        nc.all_engine_barrier()

    def w(eng, cnt):
        if cnt.val > 0:
            eng.wait_ge(cnt.sem, cnt.val)

    def dump_rows(name, dram_src, nrow):
        if name not in dump_aps:
            return
        sD = C("dmp")
        with nc.Block() as blk:
            @blk.scalar
            def _(sc):
                for r in range(nrow // 128):
                    sD.inc(sc.dma_start(dump_aps[name][r * 128:(r + 1) * 128, :],
                                        dram_src[r * 128:(r + 1) * 128, :]), 16)
                w(sc, sD)
        bar()

    sP0 = C("p0ld")
    sP0g = C("p0g")
    with nc.Block() as blk:
        @blk.sync
        def _(sy):
            for dst, src in [(scb_sb, scb_d), (cbias_sb, cbias_d), (bq_sb, bq_d),
                             (bk_sb, bk_d), (bo_sb, bo_d)]:
                for dc in range(DC):
                    sP0.inc(sy.dma_start(dst[:, dc:dc + 1],
                                         src[dc * 128:(dc + 1) * 128]), 16)
            sP0.inc(sy.dma_start(bg_sb[:, :], bg_d[:]), 16)
            w(sy, sP0)

        @blk.gpsimd
        def _(g):
            g.memset(onesc[:, :], 1.0)
            g.memset(ones1[:, :], 1.0)
            g.memset(onespad_c[:, :], 0.0)
            g.memset(onespad_c[0:1, :], 1.0)
            sP0g.inc(g.memset(r127[:, :], 127.0))

        @blk.vector
        def _(v):
            sP0.w(v, sP0.val)
            sP0g.w(v, 1)
            sP0g.inc(v.reciprocal(r127b[:, :], r127[:, :]))
            sP0g.w(v, 2)
            v.tensor_single_scalar(scbq_sb[:, :], scb_sb[:, :], r127b[:, 0:1], ALU.mult)
    bar()

    def dump_smalls():
        if "d_scbq" not in dump_aps:
            return
        sDx = C("dmpg")
        with nc.Block() as blk:
            @blk.gpsimd
            def _(g):
                sDx.inc(g.dma_start(dump_aps["d_scbq"][:, 0:8], scbq_sb[:, :]), 16)
                sDx.inc(g.dma_start(dump_aps["d_scbq"][:, 8:12], cbias_sb[:, 0:4]), 16)
                sDx.inc(g.dma_start(dump_aps["d_scbq"][:, 12:13], r127[:, :],
                                    allow_slow_non_contiguous=True), 16)
                sDx.inc(g.dma_start(dump_aps["d_scbq"][:, 13:14], r127b[:, :],
                                    allow_slow_non_contiguous=True), 16)
                sDx.inc(g.dma_start(dump_aps["d_scbq"][:, 14:15], scb_sb[:, 0:1],
                                    allow_slow_non_contiguous=True), 16)
                w(g, sDx)
        bar()

    lnid = [0]

    def emit_ln_stream(src_dram, dst_dram):
        lnid[0] += 1
        W_ = 512
        ntt = 4
        tag = "ln"
        utag = f"ln{lnid[0]}"
        sXL, sSQ, sSUM, sVA, sRT, sST2, sBC, sAPP = (C(tag + s) for s in
            ("xl", "sq", "sum", "va", "rt", "st2", "bc", "app"))
        sOSs = [C(tag + "os0"), C(tag + "os1")]
        with (
            nc.psum_tensor(utag + "_s1", [1, W_], F32) as ps1_g,
            nc.psum_tensor(utag + "_s2", [1, W_], F32) as ps2_g,
            nc.psum_tensor(utag + "_a", [128, W_], F32) as psa_g,
            nc.psum_tensor(utag + "_b", [128, W_], F32) as psb_g,
            nc.sbuf_tensor(utag + "_xh", [128, DC * W_], F32) as xh_g,
            nc.sbuf_tensor(utag + "_sq", [128, DC * W_], F32) as sq_g,
            nc.sbuf_tensor(utag + "_ob", [128, 2 * W_], F32) as ob_g,
            nc.sbuf_tensor(utag + "_mean", [1, W_], F32) as mean_g,
            nc.sbuf_tensor(utag + "_varp", [1, W_], F32) as varp_g,
            nc.sbuf_tensor(utag + "_t1", [1, W_], F32) as t1_g,
            nc.sbuf_tensor(utag + "_inv", [1, W_], F32) as inv_g,
        ):
            ps1, ps2, psa, psb = ps1_g.ap(), ps2_g.ap(), psa_g.ap(), psb_g.ap()
            xh, sq, ob = xh_g.ap(), sq_g.ap(), ob_g.ap()
            mean, varp, t1, inv = mean_g.ap(), varp_g.ap(), t1_g.ap(), inv_g.ap()
            with nc.Block() as blk:
                @blk.sync
                def _(sy):
                    for tt in range(ntt):
                        if tt >= 1:
                            sAPP.w(sy, tt * DC)
                        sXL.inc(sy.dma_start(
                            bass.AP(xh.tensor, xh.offset,
                                    [[DC * W_, 128], [W_, DC], [1, W_]]),
                            bass.AP(src_dram.tensor,
                                    src_dram.offset + tt * W_,
                                    [[T, 128], [128 * T, DC], [1, W_]])), 16)

                @blk.scalar
                def _(sc):
                    for tt in range(ntt):
                        if tt >= 1:
                            sSUM.w(sc, tt)
                        sXL.w(sc, 16 * (tt + 1))
                        for dc in range(DC):
                            sSQ.inc(sc.activation(sq[:, dc * W_:(dc + 1) * W_],
                                                  xh[:, dc * W_:(dc + 1) * W_],
                                                  AF.Square))
                        sVA.w(sc, tt + 1)
                        sRT.inc(sc.activation(t1, varp, AF.Sqrt))

                @blk.tensor
                def _(t):
                    for tt in range(ntt):
                        if tt >= 1:
                            sVA.w(t, tt)
                        sXL.w(t, 16 * (tt + 1))
                        ins = None
                        for dc in range(DC):
                            i = tt * DC + dc
                            t.matmul(ps1[:, :], onesc[:, :],
                                     xh[:, dc * W_:(dc + 1) * W_],
                                     start=(dc == 0), stop=(dc == DC - 1))
                            sSQ.w(t, i + 1)
                            ins = t.matmul(ps2[:, :], onesc[:, :],
                                           sq[:, dc * W_:(dc + 1) * W_],
                                           start=(dc == 0), stop=(dc == DC - 1))
                        sSUM.inc(ins)
                        sST2.w(t, tt + 1)
                        if tt >= 1:
                            sAPP.w(t, tt * DC)
                        t.matmul(psa[:, :], ones1[:, :], inv, start=True, stop=True)
                        sBC.inc(t.matmul(psb[:, :], ones1[:, :], mean,
                                         start=True, stop=True))

                @blk.vector
                def _(v):
                    for tt in range(ntt):
                        sSUM.w(v, tt + 1)
                        v.tensor_single_scalar(mean, ps1[0:1, :], 1.0 / 1024.0, ALU.mult)
                        v.tensor_single_scalar(varp, ps2[0:1, :], 1.0 / 1024.0, ALU.mult)
                        v.tensor_tensor(t1, mean, mean, ALU.mult)
                        v.tensor_tensor(varp, varp, t1, ALU.subtract)
                        sVA.inc(v.tensor_single_scalar(varp, varp, EPS, ALU.add))
                        sRT.w(v, tt + 1)
                        v.reciprocal(inv, t1)
                        for _i in range(2):
                            v.tensor_tensor(t1, inv, inv, ALU.mult)
                            v.tensor_tensor(t1, t1, varp, ALU.mult)
                            v.tensor_scalar(t1, t1, -0.5, 1.5, ALU.mult, ALU.add)
                            v.tensor_tensor(inv, inv, t1, ALU.mult)
                        sST2.inc(v.tensor_tensor(mean, mean, inv, ALU.mult))
                        sBC.w(v, tt + 1)
                        for dc in range(DC):
                            i = tt * DC + dc
                            if i >= 2:
                                sOSs[i % 2].w(v, 16 * (i // 2))
                            v.tensor_tensor(ob[:, (i % 2) * W_:(i % 2 + 1) * W_],
                                            xh[:, dc * W_:(dc + 1) * W_],
                                            psa[:, :], ALU.mult)
                            sAPP.inc(v.tensor_tensor(
                                ob[:, (i % 2) * W_:(i % 2 + 1) * W_],
                                ob[:, (i % 2) * W_:(i % 2 + 1) * W_],
                                psb[:, :], ALU.subtract))

                @blk.gpsimd
                def _(g):
                    for tt in range(ntt):
                        for dc in range(DC):
                            i = tt * DC + dc
                            sAPP.w(g, i + 1)
                            sOSs[i % 2].inc(g.dma_start(
                                bass.AP(dst_dram.tensor,
                                        dst_dram.offset + dc * 128 * T + tt * W_,
                                        [[T, 128], [1, W_]]),
                                ob[:, (i % 2) * W_:(i % 2 + 1) * W_]), 16)
                    w(g, sOSs[0])
                    w(g, sOSs[1])
            bar()

    vpid = [0]

    def emit_v(vsb, wv_src, bias_bc):
        vpid[0] += 1
        tag = "vp"
        utag = f"vp{vpid[0]}"
        sWL, sXT, sM, sE, sZ = (C(tag + s) for s in ("wl", "xt", "mm", "ev", "z"))
        with (
            nc.sbuf_tensor(utag + "_wst", [128, 32 * 256], F32) as wst_g,
            nc.sbuf_tensor(utag + "_xt", [128, DC * 512], F32) as xt_g,
            nc.psum_tensor(utag + "_ps0", [128, 256], F32) as ps0_g,
            nc.psum_tensor(utag + "_ps1", [128, 256], F32) as ps1_g,
        ):
            wst = wst_g.ap()
            xt = xt_g.ap()
            psv = [ps0_g.ap(), ps1_g.ap()]
            with nc.Block() as blk:
                @blk.sync
                def _(sy):
                    for sp in range(4):
                        for dc in range(DC):
                            sWL.inc(sy.dma_start(
                                wst[:, (sp * DC + dc) * 256:(sp * DC + dc + 1) * 256],
                                wv_src(sp, dc)), 16)
                    for tt in range(4):
                        if tt >= 1:
                            sM.w(sy, tt * 16)
                        sXT.inc(sy.dma_start(
                            bass.AP(xt.tensor, xt.offset,
                                    [[DC * 512, 128], [512, DC], [1, 512]]),
                            bass.AP(xln_dram.tensor,
                                    xln_dram.offset + tt * 512,
                                    [[T, 128], [128 * T, DC], [1, 512]])), 16)

                @blk.gpsimd
                def _(g):
                    sZ.inc(g.memset(vsb[:, :], 1.0))

                @blk.tensor
                def _(t):
                    w(t, sWL)
                    j = 0
                    for tt in range(4):
                        sXT.w(t, 16 * (tt + 1))
                        for ksub in range(4):
                            for sp in range(4):
                                if j >= 2:
                                    sE.w(t, j - 1)
                                ins = None
                                for dc in range(DC):
                                    ins = t.matmul(
                                        psv[j % 2][:, :],
                                        xt[:, dc * 512 + ksub * 128:
                                           dc * 512 + (ksub + 1) * 128],
                                        wst[:, (sp * DC + dc) * 256:
                                            (sp * DC + dc + 1) * 256],
                                        start=(dc == 0), stop=(dc == DC - 1))
                                sM.inc(ins)
                                j += 1

                @blk.vector
                def _(v):
                    sZ.w(v, 1)
                    j = 0
                    for tt in range(4):
                        for ksub in range(4):
                            kt = tt * 4 + ksub
                            for sp in range(4):
                                sM.w(v, j + 1)
                                dst = bass.AP(vsb.tensor,
                                              vsb.offset + kt * 1040 + sp * 260,
                                              [[16640, 128], [65, 4], [1, 64]])
                                if bias_bc is None:
                                    sE.inc(v.tensor_copy(dst, psv[j % 2][:, :]))
                                else:
                                    sE.inc(v.tensor_tensor(
                                        dst, psv[j % 2][:, :],
                                        bias_bc[:, sp * 256:(sp + 1) * 256], ALU.add))
                                j += 1
            bar()

    qkid = [0]

    def emit_qk(ksb, chunks, wgroup_src, bias_for, q_dram_ap):
        qkid[0] += 1
        tag = "qk"
        utag = f"qk{qkid[0]}"
        sXT, sM, sE = C(tag + "xt"), C(tag + "mm"), C(tag + "ev")
        sWs = [C(tag + "wl0"), C(tag + "wl1")]
        sQs = [C(tag + "qs0"), C(tag + "qs1")]
        nch = len(chunks)
        with (
            nc.sbuf_tensor(utag + "_wst", [128, 2 * 8 * 128], F32) as wst_g,
            nc.sbuf_tensor(utag + "_xt", [128, DC * 512], F32) as xt_g,
            nc.sbuf_tensor(utag + "_qst", [128, 2 * 512], F32) as qst_g,
            nc.psum_tensor(utag + "_ps0", [128, 512], F32) as ps0_g,
            nc.psum_tensor(utag + "_ps1", [128, 512], F32) as ps1_g,
        ):
            wst, xt, qst = wst_g.ap(), xt_g.ap(), qst_g.ap()
            psq = [ps0_g.ap(), ps1_g.ap()]
            with nc.Block() as blk:
                @blk.sync
                def _(sy):
                    for tt in range(4):
                        if tt >= 1:
                            sM.w(sy, tt * nch)
                        sXT.inc(sy.dma_start(
                            bass.AP(xt.tensor, xt.offset,
                                    [[DC * 512, 128], [512, DC], [1, 512]]),
                            bass.AP(xln_dram.tensor,
                                    xln_dram.offset + tt * 512,
                                    [[T, 128], [128 * T, DC], [1, 512]])), 16)
                        for ci, (mch, kind) in enumerate(chunks):
                            j = tt * nch + ci
                            if j >= 2:
                                sM.w(sy, j - 1)
                            sWs[j % 2].inc(sy.dma_start(
                                bass.AP(wst.tensor,
                                        wst.offset + (j % 2) * 1024,
                                        [[2 * 1024, 128], [128, 8], [1, 128]]),
                                wgroup_src(mch, kind)), 16)

                @blk.tensor
                def _(t):
                    for tt in range(4):
                        sXT.w(t, 16 * (tt + 1))
                        for ci, (mch, kind) in enumerate(chunks):
                            j = tt * nch + ci
                            if j >= 2:
                                sE.w(t, j - 1)
                            sWs[j % 2].w(t, 16 * (j // 2 + 1))
                            ins = None
                            for dc in range(DC):
                                ins = t.matmul(
                                    psq[j % 2][:, :],
                                    wst[:, (j % 2) * 1024 + dc * 128:
                                        (j % 2) * 1024 + (dc + 1) * 128],
                                    xt[:, dc * 512:(dc + 1) * 512],
                                    start=(dc == 0), stop=(dc == DC - 1))
                            sM.inc(ins)

                @blk.scalar
                def _(sc):
                    jq = 0
                    for tt in range(4):
                        for ci, (mch, kind) in enumerate(chunks):
                            j = tt * nch + ci
                            sM.w(sc, j + 1)
                            bias = bias_for(mch, kind)
                            if kind == "k":
                                dst = ksb[:, mch * T + tt * 512: mch * T + (tt + 1) * 512]
                            else:
                                if jq >= 2:
                                    sQs[jq % 2].w(sc, 16 * (jq // 2))
                                dst = qst[:, (jq % 2) * 512:(jq % 2 + 1) * 512]
                            if bias is None:
                                sE.inc(sc.copy(dst, psq[j % 2][:, :]))
                            else:
                                sE.inc(sc.activation(dst, psq[j % 2][:, :],
                                                     AF.Identity, bias=bias))
                            if kind == "q":
                                jq += 1

                @blk.gpsimd
                def _(g):
                    jq = 0
                    for tt in range(4):
                        for ci, (mch, kind) in enumerate(chunks):
                            j = tt * nch + ci
                            if kind != "q":
                                continue
                            sE.w(g, j + 1)
                            sQs[jq % 2].inc(g.dma_start(
                                q_dram_ap[mch * 128:(mch + 1) * 128,
                                          tt * 512:(tt + 1) * 512],
                                qst[:, (jq % 2) * 512:(jq % 2 + 1) * 512]), 16)
                            jq += 1
                    w(g, sQs[0])
                    w(g, sQs[1])
            bar()

    atid = [0]

    def emit_attention(layer, ksb, vsb, q_dram_ap, band_dram, ctxl_dram):
        atid[0] += 1
        tag = "at"
        utag = f"at{atid[0]}"
        NH = 32 if layer == 0 else 16
        LAG = 3
        sBD, sSM, sBA, sEX, sPV, sRC, sBC2, sCX, sCP = (
            C(tag + s) for s in
            ("bd", "sm", "ba", "ex", "pv", "rc", "bc", "cx", "cp"))
        sQLs = [C(tag + "ql0"), C(tag + "ql1"), C(tag + "ql2"), C(tag + "ql3")]
        sCSs = [C(tag + "cs0"), C(tag + "cs1")]
        sZP = C(tag + "zp")
        npair = [(qt + 1) if layer == 0 else 8 for qt in range(NQT)]
        pairs = []
        cum = {}
        tot = 0
        for qt in range(NQT):
            for h in range(NH):
                gi = qt * NH + h
                cum[gi] = tot
                for kp in range(npair[qt]):
                    pairs.append((tot + kp, gi, qt, h, kp))
                tot += npair[qt]
        n_items = tot
        bd_cum = [0]
        for qt in range(NQT):
            bd_cum.append(bd_cum[-1] + 2 * npair[qt])
        jlast = {}
        for (i, gi, qt, h, kp) in pairs:
            if kp == npair[qt] - 1:
                jlast[gi] = i
        NHEADS = NH * NQT
        bcast_after_pv = {}
        recip_after = {}
        evac_after = {}
        for g in range(NHEADS):
            bcast_after_pv.setdefault(min(jlast[g] + 1, n_items - 1), []).append(g)
            recip_after.setdefault(min(jlast[g] + LAG, n_items - 1), []).append(g)
            evac_after.setdefault(min(jlast[g] + LAG + 1, n_items - 1), []).append(g)

        def kv_of(h):
            g = h // 2 if layer == 0 else h
            return g // 2, (g % 2) * 64, 65 * g

        with (
            nc.sbuf_tensor(utag + "_band", [128, 16 * QT], F32) as band_g,
            nc.sbuf_tensor(utag + "_q", [128, 4 * QT], F32) as q_g,
            nc.sbuf_tensor(utag + "_p", [128, 4 * 512], F32) as p_g,
            nc.sbuf_tensor(utag + "_rin", [128, 2 * QT], F32) as rin_g,
            nc.sbuf_tensor(utag + "_cst", [64, 2 * QT], F32) as cst_g,
            nc.sbuf_tensor(utag + "_rbs", [64, 2 * QT], F32) as rbs_g,
            nc.psum_tensor(utag + "_s0", [128, 512], F32) as s0_g,
            nc.psum_tensor(utag + "_s1", [128, 512], F32) as s1_g,
            nc.psum_tensor(utag + "_s2", [128, 512], F32) as s2_g,
            nc.psum_tensor(utag + "_s3", [128, 512], F32) as s3_g,
            nc.psum_tensor(utag + "_c0", [65, QT], F32) as c0_g,
            nc.psum_tensor(utag + "_c1", [65, QT], F32) as c1_g,
            nc.psum_tensor(utag + "_rb", [64, 2 * QT], F32) as rb_g,
        ):
            band = band_g.ap()
            qsb = q_g.ap()
            psb = p_g.ap()
            rin = rin_g.ap()
            cst = cst_g.ap()
            rbs = rbs_g.ap()
            spsum = [s0_g.ap(), s1_g.ap(), s2_g.ap(), s3_g.ap()]
            cpsum = [c0_g.ap(), c1_g.ap()]
            rbps = rb_g.ap()

            with nc.Block() as blk:
                @blk.sync
                def _(sy):
                    sZP.w(sy, 1)
                    for qt in range(NQT):
                        if qt >= 1:
                            sBA.w(sy, NH * (bd_cum[qt] // 2))
                        for kc in range(2 * npair[qt]):
                            c_off = qt * QT - kc * 128 + BC0
                            sBD.inc(sy.dma_start(
                                band[:, kc * QT:(kc + 1) * QT],
                                bass.AP(band_dram.tensor, band_dram.offset + c_off,
                                        [[BW, 128], [1, QT]])), 16)
                        for h in range(NH):
                            gi = qt * NH + h
                            kvoff_h = kv_of(h)[1]
                            if gi >= 4:
                                sSM.w(sy, cum[gi - 3])
                            sQLs[gi % 4].inc(sy.dma_start(
                                bass.AP(qsb.tensor,
                                        qsb.offset + kvoff_h * (4 * QT)
                                        + (gi % 4) * QT,
                                        [[4 * QT, 64], [1, QT]]),
                                q_dram_ap[h * 64:(h + 1) * 64,
                                          qt * QT:(qt + 1) * QT]), 16)

                @blk.tensor
                def _(t):
                    def emit_scores(pr):
                        i, gi, qt, h, kp = pr
                        kvc, kvoff, vcol = kv_of(h)
                        if kp == 0:
                            sQLs[gi % 4].w(t, 16 * (gi // 4 + 1))
                        if i >= 4:
                            sEX.w(t, i - 3)
                        ins = None
                        for half in range(2):
                            kc = kp * 2 + half
                            lhs = bass.AP(ksb.tensor,
                                          ksb.offset + kvc * T + kc * 128,
                                          [[DC * T, 128], [1, 128]])
                            rhsq = bass.AP(qsb.tensor,
                                           qsb.offset + (gi % 4) * QT,
                                           [[4 * QT, 128], [1, QT]])
                            ins = t.matmul(
                                spsum[i % 4][:, half * QT:(half + 1) * QT], lhs,
                                rhsq, start=True, stop=True)
                        sSM.inc(ins)

                    def emit_pv(pr):
                        i, gi, qt, h, kp = pr
                        kvc, kvoff, vcol = kv_of(h)
                        sEX.w(t, i + 1)
                        if kp == 0 and gi >= 2:
                            sCX.w(t, gi - 1)
                        ins = None
                        for half in range(2):
                            kc = kp * 2 + half
                            lhs = bass.AP(vsb.tensor, vsb.offset + kc * 1040 + vcol,
                                          [[16640, 128], [1, 65]])
                            ins = t.matmul(
                                cpsum[gi % 2][:, :], lhs,
                                psb[:, (i % 4) * 512 + half * QT:
                                    (i % 4) * 512 + (half + 1) * QT],
                                start=(kp == 0 and half == 0),
                                stop=(kp == npair[qt] - 1 and half == 1))
                        sPV.inc(ins)
                        for g2 in bcast_after_pv.get(i, []):
                            sRC.w(t, g2 + 1)
                            if g2 >= 2:
                                sCP.w(t, g2 - 1)
                            sBC2.inc(t.matmul(
                                rbps[:, (g2 % 2) * QT:(g2 % 2 + 1) * QT],
                                onespad_c[:, 0:64],
                                rin[:, (g2 % 2) * QT:(g2 % 2 + 1) * QT],
                                start=True, stop=True))

                    for idx, pr in enumerate(pairs):
                        emit_scores(pr)
                        if idx >= LAG:
                            emit_pv(pairs[idx - LAG])
                    for j in range(max(n_items - LAG, 0), n_items):
                        emit_pv(pairs[j])

                @blk.vector
                def _(v):
                    def vhead_recip(g):
                        sPV.w(v, jlast[g] + 1)
                        if g >= 2:
                            sBC2.w(v, g - 1)
                        sRC.inc(v.reciprocal(
                            rin[0:1, (g % 2) * QT:(g % 2 + 1) * QT],
                            cpsum[g % 2][64:65, :]))

                    def vhead_evac(g):
                        sBC2.w(v, g + 1)
                        if g >= 2:
                            sCSs[g % 2].w(v, 16 * (g // 2))
                        sCP.inc(v.tensor_copy(
                            rbs[:, (g % 2) * QT:(g % 2 + 1) * QT],
                            rbps[:, (g % 2) * QT:(g % 2 + 1) * QT]))
                        sCX.inc(v.tensor_tensor(
                            cst[:, (g % 2) * QT:(g % 2 + 1) * QT],
                            cpsum[g % 2][0:64, :],
                            rbs[:, (g % 2) * QT:(g % 2 + 1) * QT], ALU.mult))

                    v.memset(qsb[:, :], 0.0)
                    sZP.inc(v.memset(rin[:, :], 0.0))
                    for idx, pr in enumerate(pairs):
                        i, gi, qt, h, kp = pr
                        if kp == 0 and h == 0:
                            sBD.w(v, 16 * bd_cum[qt + 1])
                        sSM.w(v, i + 1)
                        sps = spsum[i % 4]
                        if layer == 0:
                            ins = v.scalar_tensor_tensor(
                                sps[:, :], band[:, kp * 2 * QT:(kp * 2 + 2) * QT],
                                slopes8[h], sps[:, :], ALU.mult, ALU.add)
                        else:
                            ins = v.tensor_tensor(
                                sps[:, :], sps[:, :],
                                band[:, kp * 2 * QT:(kp * 2 + 2) * QT], ALU.add)
                        sBA.inc(ins)
                        for g2 in recip_after.get(i, []):
                            vhead_recip(g2)
                        for g2 in evac_after.get(i, []):
                            vhead_evac(g2)

                @blk.scalar
                def _(sc):
                    for idx, pr in enumerate(pairs):
                        i, gi, qt, h, kp = pr
                        sBA.w(sc, i + 1)
                        if i >= 4:
                            sPV.w(sc, i - 3)
                        sEX.inc(sc.activation(
                            psb[:, (i % 4) * 512:(i % 4 + 1) * 512],
                            spsum[i % 4][:, :], AF.Exp, scale=0.125))

                @blk.gpsimd
                def _(g):
                    for qt in range(NQT):
                        for h in range(NH):
                            gi = qt * NH + h
                            sCX.w(g, gi + 1)
                            sCSs[gi % 2].inc(g.dma_start(
                                ctxl_dram[h * 64:(h + 1) * 64, qt * QT:(qt + 1) * QT],
                                cst[:, (gi % 2) * QT:(gi % 2 + 1) * QT]), 16)
                    w(g, sCSs[0])
                    w(g, sCSs[1])
            bar()

    def emit_wout0():
        sCL, sM, sE = C("w0cl"), C("w0mm"), C("w0ev")
        sWs = [C("w0wl0"), C("w0wl1")]
        sXs = [C("w0xl0"), C("w0xl1")]
        sOs = [C("w0os0"), C("w0os1")]
        with (
            nc.sbuf_tensor("w0_ctx", [128, 16 * 512], F32) as ctx_g,
            nc.sbuf_tensor("w0_wst", [128, 2 * 16 * 128], F32) as wst_g,
            nc.sbuf_tensor("w0_xr", [128, 2 * 512], F32) as xr_g,
            nc.sbuf_tensor("w0_ob", [128, 2 * 512], F32) as ob_g,
            nc.psum_tensor("w0_ps0", [128, 512], F32) as p0_g,
            nc.psum_tensor("w0_ps1", [128, 512], F32) as p1_g,
        ):
            ctxt = ctx_g.ap()
            wst = wst_g.ap()
            xr = xr_g.ap()
            ob = ob_g.ap()
            pj = [p0_g.ap(), p1_g.ap()]
            with nc.Block() as blk:
                @blk.sync
                def _(sy):
                    for tt in range(4):
                        if tt >= 1:
                            sM.w(sy, tt * 8)
                        sCL.inc(sy.dma_start(
                            bass.AP(ctxt.tensor, ctxt.offset,
                                    [[16 * 512, 128], [512, 16], [1, 512]]),
                            bass.AP(ctx0_dram.tensor,
                                    ctx0_dram.offset + tt * 512,
                                    [[T, 128], [128 * T, 16], [1, 512]])), 16)
                        for dc in range(DC):
                            j = tt * 8 + dc
                            if j >= 2:
                                sE.w(sy, j - 1)
                            sXs[j % 2].inc(sy.dma_start(
                                xr[:, (j % 2) * 512:(j % 2 + 1) * 512],
                                x_in[dc * 128:(dc + 1) * 128,
                                     tt * 512:(tt + 1) * 512]), 16)
                            if j >= 2:
                                sM.w(sy, j - 1)
                            sWs[j % 2].inc(sy.dma_start(
                                bass.AP(wst.tensor,
                                        wst.offset + (j % 2) * 2048,
                                        [[2 * 2048, 128], [128, 16], [1, 128]]),
                                bass.AP(w_out.tensor, w_out.offset + dc * 128,
                                        [[D, 128], [128 * D, 16], [1, 128]])), 16)

                @blk.tensor
                def _(t):
                    for tt in range(4):
                        sCL.w(t, 16 * (tt + 1))
                        for dc in range(DC):
                            j = tt * 8 + dc
                            if j >= 2:
                                t_ = sE.w(t, j - 1)
                            sWs[j % 2].w(t, 16 * (j // 2 + 1))
                            ins = None
                            for pr in range(16):
                                ins = t.matmul(pj[j % 2][:, :],
                                               wst[:, (j % 2) * 2048 + pr * 128:
                                                   (j % 2) * 2048 + (pr + 1) * 128],
                                               ctxt[:, pr * 512:(pr + 1) * 512],
                                               start=(pr == 0), stop=(pr == 15))
                            sM.inc(ins)

                @blk.vector
                def _(v):
                    for tt in range(4):
                        for dc in range(DC):
                            j = tt * 8 + dc
                            sM.w(v, j + 1)
                            sXs[j % 2].w(v, 16 * (j // 2 + 1))
                            if j >= 2:
                                sOs[j % 2].w(v, 16 * (j // 2))
                            sE.inc(v.tensor_tensor(
                                ob[:, (j % 2) * 512:(j % 2 + 1) * 512],
                                pj[j % 2][:, :],
                                xr[:, (j % 2) * 512:(j % 2 + 1) * 512], ALU.add))

                @blk.gpsimd
                def _(g):
                    for tt in range(4):
                        for dc in range(DC):
                            j = tt * 8 + dc
                            sE.w(g, j + 1)
                            sOs[j % 2].inc(g.dma_start(
                                bass.AP(x1_dram.tensor,
                                        x1_dram.offset + dc * 128 * T + tt * 512,
                                        [[T, 128], [1, 512]]),
                                ob[:, (j % 2) * 512:(j % 2 + 1) * 512]), 16)
                    w(g, sOs[0])
                    w(g, sOs[1])
            bar()

    def emit_cb():
        sWL, sXT, sM, sE = C("cbwl"), C("cbxt"), C("cbmm"), C("cbev")
        sXs = [C("cbxr0"), C("cbxr1")]
        sOs = [C("cbos0"), C("cbos1")]
        with (
            nc.sbuf_tensor("cbt_sb", [128, DC * 1024], F32) as cbt_g,
            nc.sbuf_tensor("cb_xt", [128, DC * 512], F32) as xt_g,
            nc.sbuf_tensor("cb_xr", [128, 2 * 512], F32) as xr_g,
            nc.sbuf_tensor("cb_ob", [128, 2 * 512], F32) as ob_g,
            nc.sbuf_tensor("cb_obt", [128, 2 * 512], F32) as obt_g,
            nc.psum_tensor("cb_ps0", [128, 512], F32) as cps0,
            nc.psum_tensor("cb_ps1", [128, 512], F32) as cps1,
        ):
            cbtsb = cbt_g.ap()
            xt = xt_g.ap()
            xr = xr_g.ap()
            ob = ob_g.ap()
            obt = obt_g.ap()
            cps = [cps0.ap(), cps1.ap()]
            with nc.Block() as blk:
                @blk.sync
                def _(sy):
                    for dc in range(DC):
                        srcw = bass.AP(cbt.tensor, cbt.offset + dc * 128 * D,
                                       [[D, 128], [1, D]])
                        sWL.inc(sy.dma_start(cbtsb[:, dc * 1024:(dc + 1) * 1024],
                                             srcw), 16)
                    for tt in range(4):
                        if tt >= 1:
                            sM.w(sy, tt * 8)
                        sXT.inc(sy.dma_start(
                            bass.AP(xt.tensor, xt.offset,
                                    [[DC * 512, 128], [512, DC], [1, 512]]),
                            bass.AP(xln_dram.tensor,
                                    xln_dram.offset + tt * 512,
                                    [[T, 128], [128 * T, DC], [1, 512]])), 16)
                        for nc_ in range(8):
                            j = tt * 8 + nc_
                            if j >= 2:
                                sE.w(sy, j - 1)
                            sXs[j % 2].inc(sy.dma_start(
                                xr[:, (j % 2) * 512:(j % 2 + 1) * 512],
                                bass.AP(x1_dram.tensor,
                                        x1_dram.offset + nc_ * 128 * T + tt * 512,
                                        [[T, 128], [1, 512]])), 16)

                @blk.tensor
                def _(t):
                    w(t, sWL)
                    for tt in range(4):
                        sXT.w(t, 16 * (tt + 1))
                        for nc_ in range(8):
                            j = tt * 8 + nc_
                            if j >= 2:
                                sE.w(t, j - 1)
                            ins = None
                            for dc in range(DC):
                                ins = t.matmul(
                                    cps[j % 2][:, :],
                                    cbtsb[:, dc * 1024 + nc_ * 128:
                                          dc * 1024 + (nc_ + 1) * 128],
                                    xt[:, dc * 512:(dc + 1) * 512],
                                    start=(dc == 0), stop=(dc == DC - 1))
                            sM.inc(ins)

                @blk.vector
                def _(v):
                    for tt in range(4):
                        for nc_ in range(8):
                            j = tt * 8 + nc_
                            sM.w(v, j + 1)
                            sXs[j % 2].w(v, 16 * (j // 2 + 1))
                            if j >= 2:
                                sOs[j % 2].w(v, 16 * (j // 2))
                            v.tensor_copy(obt[:, (j % 2) * 512:(j % 2 + 1) * 512],
                                          cps[j % 2][:, :])
                            v.tensor_scalar(ob[:, (j % 2) * 512:(j % 2 + 1) * 512],
                                            obt[:, (j % 2) * 512:(j % 2 + 1) * 512],
                                            scbq_sb[:, nc_:nc_ + 1],
                                            cbias_sb[:, nc_:nc_ + 1],
                                            ALU.mult, ALU.add)
                            sE.inc(v.tensor_tensor(
                                ob[:, (j % 2) * 512:(j % 2 + 1) * 512],
                                ob[:, (j % 2) * 512:(j % 2 + 1) * 512],
                                xr[:, (j % 2) * 512:(j % 2 + 1) * 512], ALU.add))

                @blk.gpsimd
                def _(g):
                    for tt in range(4):
                        for nc_ in range(8):
                            j = tt * 8 + nc_
                            sE.w(g, j + 1)
                            sOs[j % 2].inc(g.dma_start(
                                bass.AP(x2_dram.tensor,
                                        x2_dram.offset + nc_ * 128 * T + tt * 512,
                                        [[T, 128], [1, 512]]),
                                ob[:, (j % 2) * 512:(j % 2 + 1) * 512]), 16)
                    w(g, sOs[0])
                    w(g, sOs[1])
            bar()

    def emit_bvbc(bvbc):
        sB = C("bvb")
        with (
            nc.sbuf_tensor("bvrow", [1, D], F32) as bvr_g,
            nc.psum_tensor("bv_ps", [128, 512], F32) as bvps_g,
        ):
            bvr = bvr_g.ap()
            bvps = bvps_g.ap()
            with nc.Block() as blk:
                @blk.sync
                def _(sy):
                    sB.inc(sy.dma_start(bvr[:, :], bv_d[:]), 16)

                @blk.tensor
                def _(t):
                    sB.w(t, 16)
                    sB.inc(t.matmul(bvps[:, :], ones1[:, :], bvr[:, 0:512],
                                    start=True, stop=True))
                    sB.w(t, 18)
                    sB.inc(t.matmul(bvps[:, :], ones1[:, :], bvr[:, 512:1024],
                                    start=True, stop=True))

                @blk.vector
                def _(v):
                    sB.w(v, 17)
                    sB.inc(v.tensor_copy(bvbc[:, 0:512], bvps[:, :]))
                    sB.w(v, 19)
                    v.tensor_copy(bvbc[:, 512:1024], bvps[:, :])
            bar()

    def emit_band1():
        sBP = C("bprol")
        sBPg = C("bprolg")
        with (
            nc.sbuf_tensor("brsb", [1, 4095], F32) as brsb_g,
            nc.sbuf_tensor("etsb", [1, 2], F32) as etsb_g,
        ):
            brsb = brsb_g.ap()
            etsb = etsb_g.ap()
            with nc.Block() as blk:
                @blk.sync
                def _(sy):
                    sBP.inc(sy.dma_start(brsb[:, :], bandrev1[:]), 16)
                    sBP.inc(sy.dma_start(etsb[:, 0:1], etg[:, :]), 16)
                    sBP.inc(sy.dma_start(etsb[:, 1:2], isg[:, :]), 16)

                @blk.scalar
                def _(sc):
                    sBP.w(sc, 48)
                    sBP.inc(sc.activation(etsb[:, 0:1], etsb[:, 0:1], AF.Sigmoid))

                @blk.vector
                def _(v):
                    sBP.w(v, 49)
                    v.tensor_single_scalar(etsb[:, 0:1], etsb[:, 0:1], 8.0, ALU.mult)
                    sBP.inc(v.tensor_single_scalar(etsb[:, 1:2], etsb[:, 1:2], 8.0,
                                                   ALU.mult))
                    sBP.w(v, 50)
                    sBP.inc(v.tensor_scalar(brsb[:, :], brsb[:, :], etsb[:, 0:1],
                                            etsb[:, 1:2], ALU.mult, ALU.add))

                @blk.gpsimd
                def _(g):
                    sBP.w(g, 51)
                    sBPg.inc(g.dma_start(bdev_dram[:], brsb[:, :]), 16)
                    sBPg.w(g, 16)
                    for p in range(128):
                        sBPg.inc(g.dma_start(
                            bm1_dram[p:p + 1, :],
                            bass.AP(bdev_dram.tensor, bdev_dram.offset + 127 - p,
                                    [[1, 1], [1, BW]])), 16)
                    w(g, sBPg)
            bar()

    def emit_scramble_proj():
        sCL, sM, sE = C("p12cl"), C("p12mm"), C("p12ev")
        sCLF = C("p12clf")
        sWs = [C("p12wl0"), C("p12wl1")]
        sXs = [C("p12xl0"), C("p12xl1")]
        sSs = [C("p12st0"), C("p12st1")]
        with (
            nc.sbuf_tensor("csrc", [128, 8 * T], F32) as csrc_g,
            nc.sbuf_tensor("wost", [128, 2 * 2048], F32) as wost_g,
            nc.sbuf_tensor("x2st", [128, 2 * T], F32) as x2st_g,
            nc.sbuf_tensor("x3st", [128, 2 * T], F32) as x3st_g,
            nc.psum_tensor("p12_ps", [128, 2048], F32) as ps_g,
        ):
            csrc = csrc_g.ap()
            wost = wost_g.ap()
            x2st = x2st_g.ap()
            x3st = x3st_g.ap()
            ps = ps_g.ap()
            with nc.Block() as blk:
                @blk.sync
                def _(sy):
                    for di_, dc in enumerate([1, 2, 3, 4, 5, 6, 7, 0]):
                        if di_ >= 2:
                            sM.w(sy, di_ - 1)
                        for uh in range(2):
                            srcw = bass.AP(woT.tensor, woT.offset + dc * 128,
                                           [[D, 64], [64 * D, 16], [1, 128]])
                            sWs[di_ % 2].inc(sy.dma_start(
                                bass.AP(wost.tensor,
                                        wost.offset + uh * 64 * (2 * 2048)
                                        + (di_ % 2) * 2048,
                                        [[2 * 2048, 64], [128, 16], [1, 128]]),
                                srcw), 16)
                        if di_ >= 2:
                            sE.w(sy, di_ - 1)
                        sXs[di_ % 2].inc(sy.dma_start(
                            x2st[:, (di_ % 2) * T:(di_ % 2 + 1) * T],
                            x2_dram[dc * 128:(dc + 1) * 128, :]), 16)

                @blk.tensor
                def _(t):
                    w(t, sCL)
                    sCLF.w(t, 16)
                    for di_, dc in enumerate([1, 2, 3, 4, 5, 6, 7, 0]):
                        sWs[di_ % 2].w(t, 32 * (di_ // 2 + 1))
                        if di_ >= 1:
                            sE.w(t, di_)
                        ins = None
                        for s in range(16):
                            for u in range(2):
                                lhs = bass.AP(wost.tensor,
                                              wost.offset + u * 64 * (2 * 2048)
                                              + (di_ % 2) * 2048 + s * 128,
                                              [[2 * 2048, 64], [1, 128]])
                                for m in range(8):
                                    k = u * 8 + m
                                    rhs = bass.AP(csrc.tensor,
                                                  csrc.offset + m * T
                                                  + (u * 64) * (8 * T) + s,
                                                  [[8 * T, 64], [16, 128]])
                                    ins = t.matmul(
                                        ps[:, k * 128:(k + 1) * 128], lhs, rhs,
                                        start=(s == 0 and k % 4 == 0),
                                        stop=(s == 15))
                        sM.inc(ins)

                @blk.vector
                def _(v):
                    for di_, dc in enumerate([1, 2, 3, 4, 5, 6, 7, 0]):
                        sM.w(v, di_ + 1)
                        sXs[di_ % 2].w(v, 16 * (di_ // 2 + 1))
                        if di_ >= 2:
                            sSs[di_ % 2].w(v, 16 * (di_ // 2))
                        sE.inc(v.scalar_tensor_tensor(
                            x3st[:, (di_ % 2) * T:(di_ % 2 + 1) * T],
                            ps[:, :], bo_sb[:, dc:dc + 1],
                            x2st[:, (di_ % 2) * T:(di_ % 2 + 1) * T],
                            ALU.add, ALU.add))

                @blk.gpsimd
                def _(g):
                    with g.register("p12r") as r12:
                        pid = g.partition_id()
                        for m in range(8):
                            g.reg_mul(r12, pid, 128 * T)
                            g.reg_add(r12, r12, m * D * T)
                            sCL.inc(g.dma_start(
                                csrc[:, m * T:(m + 1) * T],
                                bass.AP(agctx.tensor, r12,
                                        [[T, 128], [1, T]])), 16)
                    sCL.w(g, 128)
                    sCLF.inc(g.dma_start(x3st[:, 0:4], x2st[:, 0:4]), 16)
                    if "d_csrc" in dump_aps:
                        sDc = C("dmpg")
                        sDc.inc(g.dma_start(dump_aps["d_csrc"][:, :], csrc[:, :]), 16)
                        w(g, sDc)
                    for di_, dc in enumerate([1, 2, 3, 4, 5, 6, 7, 0]):
                        sE.w(g, di_ + 1)
                        sSs[di_ % 2].inc(g.dma_start(
                            x3_dram[dc * 128:(dc + 1) * 128, :],
                            x3st[:, (di_ % 2) * T:(di_ % 2 + 1) * T]), 16)
                    w(g, sSs[0])
                    w(g, sSs[1])
            bar()

    def emit_moe():
        TT = 512
        (sXL, sLSQ, sLSM, sLVA, sLRT, sLST, sLBC, sLAP) = (
            C("m" + s) for s in ("xl", "lsq", "lsm", "lva", "lrt", "lst", "lbc", "lap"))
        (sGM, sGB, sTR, sMK, sT2, sMF, sMB, sXE, sCU, sFW) = (
            C("m" + s) for s in
            ("gm", "gb", "tr", "mk", "t2", "mf", "mb", "xe", "cu", "fw"))
        (sHM, sGE, sW2T, sYE, sEV) = (
            C("m" + s) for s in ("hm", "ge", "w2t", "ye", "ev"))
        sW1s = [C("mw1a"), C("mw1b")]
        sW2s = [C("mw2a"), C("mw2b")]
        sOSs2 = [C("mosa"), C("mosb")]
        import contextlib
        with contextlib.ExitStack() as _st:
            x3_g = _st.enter_context(nc.sbuf_tensor("m_x3", [128, DC * TT], F32))
            x3ln_g = _st.enter_context(nc.sbuf_tensor("m_x3ln", [128, DC * TT], F32))
            xe_g = _st.enter_context(nc.sbuf_tensor("m_xe", [128, DC * TT], F32))
            h_g = _st.enter_context(nc.sbuf_tensor("m_h", [128, 32 * TT], F32))
            ysb_g = _st.enter_context(nc.sbuf_tensor("m_ysb", [128, DC * TT], F32))
            w1st_g = _st.enter_context(nc.sbuf_tensor("m_w1st", [128, 2 * 1024], F32))
            w2st_g = _st.enter_context(nc.sbuf_tensor("m_w2st", [128, 2 * 1024], F32))
            lg_g = _st.enter_context(nc.sbuf_tensor("m_lg", [8, TT], F32))
            mtk_g = _st.enter_context(nc.sbuf_tensor("m_mtk", [128, 8], F32))
            mtk2_g = _st.enter_context(nc.sbuf_tensor("m_mtk2", [128, 8], F32))
            gcap_g = _st.enter_context(nc.sbuf_tensor("m_gcap", [128, 32], F32))
            run_g = _st.enter_context(nc.sbuf_tensor("m_run", [128, 2], F32))
            mfm_g = _st.enter_context(nc.sbuf_tensor("m_mfm", [8, TT], F32))
            mbc_g = _st.enter_context(nc.sbuf_tensor("m_mbc", [128, TT], F32))
            ost_g = _st.enter_context(nc.sbuf_tensor("m_ost", [128, 2 * TT], F32))
            msq_g = _st.enter_context(nc.sbuf_tensor("m_sq", [128, DC * TT], F32))
            mmean_g = _st.enter_context(nc.sbuf_tensor("m_mean", [1, TT], F32))
            mvarp_g = _st.enter_context(nc.sbuf_tensor("m_varp", [1, TT], F32))
            mt1_g = _st.enter_context(nc.sbuf_tensor("m_t1", [1, TT], F32))
            minv_g = _st.enter_context(nc.sbuf_tensor("m_inv", [1, TT], F32))
            selsb_g = _st.enter_context(nc.sbuf_tensor("m_selsb", [8, 8 * 128], F32))
            id8_g = _st.enter_context(nc.sbuf_tensor("m_id8", [8, 8], F32))
            id128_g = _st.enter_context(nc.sbuf_tensor("m_id128", [128, 128], F32))
            wg_g = _st.enter_context(nc.sbuf_tensor("m_wg", [128, 64], F32))
            ucum_g = _st.enter_context(nc.sbuf_tensor("m_ucum", [8, 8], F32))
            ls_g = _st.enter_context(nc.psum_tensor("m_ls12", [33, TT], F32))
            gbc_g = _st.enter_context(nc.psum_tensor("m_gbc", [128, 512], F32))
            hp0_g = _st.enter_context(nc.psum_tensor("m_hps0", [128, 512], F32))
            hp1_g = _st.enter_context(nc.psum_tensor("m_hps1", [128, 512], F32))
            y0_g = _st.enter_context(nc.psum_tensor("m_y0", [128, 512], F32))
            y1_g = _st.enter_context(nc.psum_tensor("m_y1", [128, 512], F32))
            y2_g = _st.enter_context(nc.psum_tensor("m_y2", [128, 512], F32))
            y3_g = _st.enter_context(nc.psum_tensor("m_y3", [128, 512], F32))
            x3sb = x3_g.ap()
            x3ln = x3ln_g.ap()
            xesb = xe_g.ap()
            hsb = h_g.ap()
            ysb = ysb_g.ap()
            w1st = w1st_g.ap()
            w2st = w2st_g.ap()
            lg = lg_g.ap()
            mtk = mtk_g.ap()
            mtk2 = mtk2_g.ap()
            gcap = gcap_g.ap()
            runb = run_g.ap()
            mfm = mfm_g.ap()
            mbcsb = mbc_g.ap()
            ost = ost_g.ap()
            msq = msq_g.ap()
            mean, varp, t1v, inv = (mmean_g.ap(), mvarp_g.ap(), mt1_g.ap(),
                                    minv_g.ap())
            selsb = selsb_g.ap()
            id8 = id8_g.ap()
            id128 = id128_g.ap()
            wg_sb = wg_g.ap()
            ucum = ucum_g.ap()
            ls12 = ls_g.ap()
            gbc = gbc_g.ap()
            hps = [hp0_g.ap(), hp1_g.ap()]
            ypsl = [y0_g.ap(), y1_g.ap(), y2_g.ap(), y3_g.ap()]

            prol = C("mprol")
            with nc.Block() as blk:
                @blk.sync
                def _(sy):
                    prol.inc(sy.dma_start(selsb[:, :], sel_c[:, :]), 16)
                    prol.inc(sy.dma_start(id8[:, :], id8_c[:, :]), 16)
                    prol.inc(sy.dma_start(id128[:, :], id128_c[:, :]), 16)
                    prol.inc(sy.dma_start(ucum[:, :], ucum_c[:, :]), 16)
                    for dc in range(DC):
                        prol.inc(sy.dma_start(wg_sb[:, dc * 8:(dc + 1) * 8],
                                              wg_d[dc * 128:(dc + 1) * 128, :]), 16)
                    w(sy, prol)
            bar()

            for tt in range(4):
                with nc.Block() as blk:
                    @blk.sync
                    def _(sy):
                        if tt >= 1:
                            sEV.w(sy, tt * 8)
                            sXE.w(sy, tt * 8)
                        sXL.inc(sy.dma_start(
                            bass.AP(x3sb.tensor, x3sb.offset,
                                    [[DC * TT, 128], [TT, DC], [1, TT]]),
                            bass.AP(x3_dram.tensor,
                                    x3_dram.offset + tt * TT,
                                    [[T, 128], [128 * T, DC], [1, TT]])), 16)

                    @blk.scalar
                    def _(sc):
                        sXL.w(sc, 16 * (tt + 1))
                        for dc in range(DC):
                            i = tt * DC + dc
                            sLSQ.inc(sc.activation(
                                msq[:, dc * TT:(dc + 1) * TT],
                                x3sb[:, dc * TT:(dc + 1) * TT], AF.Square))
                        sLVA.w(sc, tt + 1)
                        sLRT.inc(sc.activation(t1v, varp, AF.Sqrt))

                    @blk.tensor
                    def _(t):
                        sXL.w(t, 16 * (tt + 1))
                        for dc in range(DC):
                            i = tt * DC + dc
                            t.matmul(ls12[0:1, :], onesc[:, :],
                                     x3sb[:, dc * TT:(dc + 1) * TT],
                                     start=(dc == 0), stop=(dc == DC - 1))
                            sLSQ.w(t, i + 1)
                            ins = t.matmul(ls12[32:33, :], onesc[:, :],
                                           msq[:, dc * TT:(dc + 1) * TT],
                                           start=(dc == 0), stop=(dc == DC - 1))
                        sLSM.inc(ins)
                        sLST.w(t, tt + 1)
                        t.matmul(gbc[:, :], ones1[:, :], inv, start=True, stop=True)
                        sLBC.inc(t.matmul(hps[0][:, :], ones1[:, :], mean,
                                          start=True, stop=True))

                    @blk.vector
                    def _(v):
                        sLSM.w(v, tt + 1)
                        v.tensor_single_scalar(mean, ls12[0:1, :], 1.0 / 1024.0,
                                               ALU.mult)
                        v.tensor_single_scalar(varp, ls12[32:33, :], 1.0 / 1024.0,
                                               ALU.mult)
                        v.tensor_tensor(t1v, mean, mean, ALU.mult)
                        v.tensor_tensor(varp, varp, t1v, ALU.subtract)
                        sLVA.inc(v.tensor_single_scalar(varp, varp, EPS, ALU.add))
                        sLRT.w(v, tt + 1)
                        v.reciprocal(inv, t1v)
                        for _i in range(2):
                            v.tensor_tensor(t1v, inv, inv, ALU.mult)
                            v.tensor_tensor(t1v, t1v, varp, ALU.mult)
                            v.tensor_scalar(t1v, t1v, -0.5, 1.5, ALU.mult, ALU.add)
                            v.tensor_tensor(inv, inv, t1v, ALU.mult)
                        sLST.inc(v.tensor_tensor(mean, mean, inv, ALU.mult))
                        sLBC.w(v, tt + 1)
                        ins = None
                        for dc in range(DC):
                            sl = slice(dc * TT, (dc + 1) * TT)
                            v.tensor_tensor(x3ln[:, sl], x3sb[:, sl],
                                            gbc[:, :], ALU.mult)
                            ins = v.tensor_tensor(x3ln[:, sl], x3ln[:, sl],
                                                  hps[0][:, :], ALU.subtract)
                        sLAP.inc(ins)

                    @blk.gpsimd
                    def _(g):
                        g.memset(ysb[:, :], 0.0)
                bar()
                gm0, gb0, tr0, mk0, t20, mf0 = (sGM.rval, sGB.rval, sTR.rval,
                                                sMK.rval, sT2.rval, sMF.rval)
                cu0, fw0 = sCU.rval, sFW.rval
                with nc.Block() as blk:
                    @blk.tensor
                    def _(t):
                        ins = None
                        for dc in range(DC):
                            ins = t.matmul(gbc[0:8, :], wg_sb[:, dc * 8:(dc + 1) * 8],
                                           x3ln[:, dc * TT:(dc + 1) * TT],
                                           start=(dc == 0), stop=(dc == DC - 1))
                        sGM.inc(ins)
                        for q in range(4):
                            sGB.w(t, gb0 + 1)
                            if q >= 1:
                                sFW.w(t, fw0 + q)
                                sMK.w(t, mk0 + 3 * q)
                            sTR.inc(t.transpose(gbc[0:128, 0:8],
                                                lg[:, q * 128:(q + 1) * 128],
                                                id8[:, :]))
                            sMK.w(t, mk0 + 3 * q + 3)
                            sT2.inc(t.transpose(gbc[0:8, 64:192],
                                                mtk[:, :], id128[:, :]))
                            sMF.w(t, mf0 + q + 1)
                            sCU.inc(t.matmul(gbc[0:8, 192:320], ucum[:, :],
                                             mfm[:, q * 128:(q + 1) * 128],
                                             start=True, stop=True))

                    @blk.scalar
                    def _(sc):
                        sGM.w(sc, gm0 + 1)
                        sGB.inc(sc.activation(lg[:, :], gbc[0:8, :], AF.Identity,
                                              bias=bg_sb[:, :]))
                        for q in range(4):
                            sT2.w(sc, t20 + q + 1)
                            sMF.inc(sc.copy(mfm[:, q * 128:(q + 1) * 128],
                                            gbc[0:8, 64:192]))

                    @blk.vector
                    def _(v):
                        for q in range(4):
                            sTR.w(v, tr0 + q + 1)
                            sMK.inc(v.tensor_copy(mtk2[:, :], gbc[0:128, 0:8]))
                            sMK.w(v, mk0 + 3 * q + 1)
                            sMK.inc(v.reduce_max(runb[:, 0:1], mtk2[:, :],
                                                 mybir.AxisListType.X))
                            sMK.w(v, mk0 + 3 * q + 2)
                            sMK.inc(v.tensor_single_scalar(mtk[:, :], mtk2[:, :],
                                                           runb[:, 0:1], ALU.is_ge))
                            sCU.w(v, cu0 + q + 1)
                            sFW.inc(v.scalar_tensor_tensor(
                                mfm[:, q * 128:(q + 1) * 128],
                                gbc[0:8, 192:320], 0.0,
                                mfm[:, q * 128:(q + 1) * 128],
                                ALU.is_equal, ALU.mult))
                            if "d_gdbg" in dump_aps and tt == 0 and q == 0:
                                v.tensor_copy(gcap[:, 0:8], mtk2[:, :])
                                v.tensor_copy(gcap[:, 8:9], runb[:, 0:1])
                                v.tensor_copy(gcap[:, 9:17], mtk[:, :])
                bar()
                if "d_mask" in dump_aps:
                    sD = C(f"dmask{tt}")
                    with nc.Block() as blk:
                        @blk.gpsimd
                        def _(g):
                            sD.inc(g.dma_start(
                                dump_aps["d_mask"][:, tt * TT:(tt + 1) * TT],
                                mfm[:, :]), 16)
                            w(g, sD)
                    bar()
                hm0, ge0, mb0, xe0 = sHM.rval, sGE.rval, sMB.rval, sXE.val
                w2t0, ye0 = sW2T.rval, sYE.rval
                w1s0 = [sW1s[0].rval, sW1s[1].rval]
                w2s0 = [sW2s[0].rval, sW2s[1].rval]
                with nc.Block() as blk:
                    @blk.sync
                    def _(sy):
                        for e in range(8):
                            for hc in range(32):
                                i = e * 32 + hc
                                if i >= 2:
                                    sHM.w(sy, hm0 + i - 1)
                                src = bass.AP(
                                    w1_d.tensor,
                                    w1_d.offset + e * D * 4096 + hc * 128,
                                    [[4096, 128], [4096 * 128, 8], [1, 128]])
                                sW1s[i % 2].inc(sy.dma_start(
                                    bass.AP(w1st.tensor,
                                            w1st.offset + (i % 2) * 1024,
                                            [[2 * 1024, 128], [128, 8], [1, 128]]),
                                    src), 16)

                    @blk.scalar
                    def _(sc):
                        for e in range(8):
                            for hc in range(32):
                                i = e * 32 + hc
                                sHM.w(sc, hm0 + i + 1)
                                sGE.inc(sc.activation(
                                    hsb[:, hc * TT:(hc + 1) * TT],
                                    hps[i % 2][:, :], AF.Gelu_apprx_tanh))

                    @blk.gpsimd
                    def _(g):
                        for e in range(8):
                            for half in range(2):
                                for hc in range(32):
                                    ii = (2 * e + half) * 32 + hc
                                    if ii >= 2:
                                        sW2T.w(g, w2t0 + ii - 1)
                                    src = bass.AP(
                                        w2_d.tensor,
                                        w2_d.offset + (e * 4096 + hc * 128) * D
                                        + half * 512,
                                        [[D, 128], [1, 512]])
                                    sW2s[ii % 2].inc(g.dma_start(
                                        w2st[:, (ii % 2) * 512:(ii % 2 + 1) * 512],
                                        src), 16)

                    @blk.tensor
                    def _(t):
                        for e in range(8):
                            sFW.w(t, fw0 + 4)
                            sLAP.w(t, tt + 1)
                            if e >= 1:
                                sXE.w(t, xe0 + e)
                            sMB.inc(t.matmul(gbc[0:128, :],
                                             selsb[:, e * 128:(e + 1) * 128],
                                             mfm[:, :], start=True, stop=True))
                            for hc in range(32):
                                i = e * 32 + hc
                                sW1s[i % 2].w(t, w1s0[i % 2] + 16 * (i // 2 + 1))
                                sXE.w(t, xe0 + e + 1)
                                if i >= 2:
                                    sGE.w(t, ge0 + i - 1)
                                ins = None
                                for dc in range(DC):
                                    ins = t.matmul(
                                        hps[i % 2][:, :],
                                        w1st[:, (i % 2) * 1024 + dc * 128:
                                             (i % 2) * 1024 + (dc + 1) * 128],
                                        xesb[:, dc * TT:(dc + 1) * TT],
                                        start=(dc == 0), stop=(dc == DC - 1))
                                sHM.inc(ins)
                            for half in range(2):
                                if 2 * e + half >= 1:
                                    sYE.w(t, ye0 + 2 * e + half)
                                for hc in range(32):
                                    ii = (2 * e + half) * 32 + hc
                                    sW2s[ii % 2].w(t, w2s0[ii % 2] + 16 * (ii // 2 + 1))
                                    sGE.w(t, ge0 + e * 32 + hc + 1)
                                    ins = None
                                    for di in range(4):
                                        dcw = half * 4 + di
                                        ins = t.matmul(
                                            ypsl[di][:, :],
                                            w2st[:, (ii % 2) * 512 + di * 128:
                                                 (ii % 2) * 512 + (di + 1) * 128],
                                            hsb[:, hc * TT:(hc + 1) * TT],
                                            start=(hc == 0), stop=(hc == 31))
                                    sW2T.inc(ins)

                    @blk.vector
                    def _(v):
                        for e in range(8):
                            sMB.w(v, mb0 + e + 1)
                            v.tensor_copy(mbcsb[:, :], gbc[0:128, :])
                            ins = None
                            for dc in range(DC):
                                ins = v.tensor_tensor(
                                    xesb[:, dc * TT:(dc + 1) * TT],
                                    x3ln[:, dc * TT:(dc + 1) * TT],
                                    mbcsb[:, :], ALU.mult)
                            sXE.inc(ins)
                            for half in range(2):
                                sW2T.w(v, w2t0 + 32 * (2 * e + half + 1))
                                ins = None
                                for di in range(4):
                                    dcw = half * 4 + di
                                    ins = v.tensor_tensor(
                                        ysb[:, dcw * TT:(dcw + 1) * TT],
                                        ysb[:, dcw * TT:(dcw + 1) * TT],
                                        ypsl[di][:, :], ALU.add)
                                sYE.inc(ins)
                bar()
                with nc.Block() as blk:
                    @blk.vector
                    def _(v):
                        for dc in range(DC):
                            j = tt * 8 + dc
                            if j >= 2:
                                sOSs2[j % 2].w(v, 16 * (j // 2))
                            sEV.inc(v.tensor_tensor(
                                ost[:, (j % 2) * TT:(j % 2 + 1) * TT],
                                ysb[:, dc * TT:(dc + 1) * TT],
                                x3sb[:, dc * TT:(dc + 1) * TT], ALU.add))

                    @blk.gpsimd
                    def _(g):
                        for dc in range(DC):
                            j = tt * 8 + dc
                            sEV.w(g, j + 1)
                            sOSs2[j % 2].inc(g.dma_start(
                                bass.AP(out_d.tensor,
                                        out_d.offset + dc * 128 * T + tt * TT,
                                        [[T, 128], [1, TT]]),
                                ost[:, (j % 2) * TT:(j % 2 + 1) * TT]), 16)
                bar()
            if "d_gdbg" in dump_aps:
                sDg = C("dmpg")
                with nc.Block() as blk:
                    @blk.gpsimd
                    def _(g):
                        sDg.inc(g.dma_start(dump_aps["d_gdbg"][:, :], gcap[:, :]), 16)
                        w(g, sDg)
                bar()
            with nc.Block() as blk:
                @blk.gpsimd
                def _(g):
                    w(g, sOSs2[0])
                    w(g, sOSs2[1])
            bar()

    if "sm0" in dumps:
        dump_smalls()
    emit_ln_stream(x_in, xln_dram)
    if "sm1" in dumps:
        dump_smalls()
    dump_rows("d_xln", xln_dram, D)
    with (
        nc.sbuf_tensor("vsb0", [128, 16 * 1040], F32) as vsb0_g,
        nc.sbuf_tensor("ksb0", [128, DC * T], F32) as ksb0_g,
    ):
        vsb0, ksb0 = vsb0_g.ap(), ksb0_g.ap()
        if upto >= 2:
            emit_v(vsb0,
                   lambda sp, dc: w_qkv[dc * 128:(dc + 1) * 128,
                                        3072 + sp * 256:3072 + (sp + 1) * 256],
                   None)
            chunks0 = [(m, "q") for m in range(16)] + [(m, "k") for m in range(8)]

            def wsrc0(mch, kind):
                c0 = mch * 128 if kind == "q" else 2048 + mch * 128
                return bass.AP(w_qkv.tensor, w_qkv.offset + c0,
                               [[4096, 128], [128 * 4096, 8], [1, 128]])

            emit_qk(ksb0, chunks0, wsrc0, lambda mch, kind: None, q0_dram)
            if "sm2" in dumps:
                dump_smalls()
            if "d_v" in dump_aps:
                sD = C("dmpg")
                with nc.Block() as blk:
                    @blk.gpsimd
                    def _(g):
                        sD.inc(g.dma_start(dump_aps["d_v"][:, :], vsb0[:, :]), 16)
                        w(g, sD)
                bar()
            if "d_k" in dump_aps:
                sD = C("dmpg")
                with nc.Block() as blk:
                    @blk.gpsimd
                    def _(g):
                        for dc in range(DC):
                            sD.inc(g.dma_start(
                                dump_aps["d_k"][dc * 128:(dc + 1) * 128, :],
                                ksb0[:, dc * T:(dc + 1) * T]), 16)
                        w(g, sD)
                bar()
            dump_rows("d_q0", q0_dram, 2048)
        if upto >= 3:
            emit_attention(0, ksb0, vsb0, q0_dram, bm0, ctx0_dram)
            if "sm3" in dumps:
                dump_smalls()
            dump_rows("d_ctx0", ctx0_dram, 2048)
    if upto >= 4:
        emit_wout0()
        if "sm4" in dumps:
            dump_smalls()
        dump_rows("d_x1", x1_dram, D)
    if upto >= 5:
        emit_ln_stream(x1_dram, xln_dram)
        dump_rows("d_xln2", xln_dram, D)
        emit_cb()
        dump_rows("d_x2", x2_dram, D)
    if upto >= 6:
        emit_ln_stream(x2_dram, xln_dram)
        with (
            nc.sbuf_tensor("vsb1", [128, 16 * 1040], F32) as vsb1_g,
            nc.sbuf_tensor("ksb1", [128, DC * T], F32) as ksb1_g,
            nc.sbuf_tensor("bvbc", [128, D], F32) as bvbc_g,
        ):
            vsb1, ksb1, bvbc = vsb1_g.ap(), ksb1_g.ap(), bvbc_g.ap()
            emit_bvbc(bvbc)
            emit_v(vsb1,
                   lambda sp, dc: wvT[dc * 128:(dc + 1) * 128,
                                      sp * 256:(sp + 1) * 256],
                   bvbc)
            chunks1 = [(m, "q") for m in range(8)] + [(m, "k") for m in range(8)]

            def wsrc1(mch, kind):
                wsrc = wqT if kind == "q" else wkT
                return bass.AP(wsrc.tensor, wsrc.offset + mch * 128,
                               [[D, 128], [128 * D, 8], [1, 128]])

            def bias1(mch, kind):
                return (bq_sb if kind == "q" else bk_sb)[:, mch:mch + 1]

            emit_qk(ksb1, chunks1, wsrc1, bias1, q1_dram)
            dump_rows("d_q1", q1_dram, D)
            emit_band1()
            dump_rows("d_bm1", bm1_dram, 128)
            if upto >= 7:
                emit_attention(1, ksb1, vsb1, q1_dram, bm1_dram, ctx1_dram)
                dump_rows("d_ctx1", ctx1_dram, D)
    if upto >= 8:
        sA2 = C("a2a")
        with nc.Block() as blk:
            @blk.gpsimd
            def _(g):
                ins = g.collective_compute(
                    "AllGather", ALU.bypass, replica_groups=[list(range(8))],
                    ins=[ctx1_dram[:, :]], outs=[agctx[:, :]])
                sA2.inc(ins, 1)
                sA2.w(g, 1)
        bar()
        nc.all_core_barrier()
        bar()
        dump_rows("d_a2a", agctx, 2 * D)
    if upto >= 9:
        emit_scramble_proj()
        dump_rows("d_x3", x3_dram, D)
    if upto >= 10:
        emit_moe()
    bar()
    return nc


def _host_inputs(inputs):
    f32 = np.float32
    c = np.ascontiguousarray
    x = np.asarray(inputs["L0_attn_inputs"], f32)
    tb = np.asarray(inputs["L1_attn_time_bias"], f32).reshape(500)
    idx = np.clip(np.arange(4095) - 2047 + 250, 0, 499)
    band1 = tb[idx]
    band1_rev = c(band1[::-1])
    common = {
        "w_qkv": c(np.asarray(inputs["L0_attn_W_qkv"], f32)),
        "w_out": c(np.asarray(inputs["L0_attn_W_out"], f32)),
        "cbt": c(np.asarray(inputs["L0_ffn_CB"], f32).T),
        "scb": c(np.asarray(inputs["L0_ffn_SCB"], f32)),
        "cbias": c(np.asarray(inputs["L0_ffn_bias"], f32)),
        "wqT": c(np.asarray(inputs["L1_attn_Wq"], f32).T),
        "wkT": c(np.asarray(inputs["L1_attn_Wk"], f32).T),
        "wvT": c(np.asarray(inputs["L1_attn_Wv"], f32).T),
        "woT": c(np.asarray(inputs["L1_attn_Wo"], f32).T),
        "bq": c(np.asarray(inputs["L1_attn_bq"], f32)),
        "bk": c(np.asarray(inputs["L1_attn_bk"], f32)),
        "bv": c(np.asarray(inputs["L1_attn_bv"], f32)),
        "bo": c(np.asarray(inputs["L1_attn_bo"], f32)),
        "etg": c(np.asarray(inputs["L1_attn_et_gate"], f32).reshape(1, 1)),
        "isg": c(np.asarray(inputs["L1_attn_is_gate"], f32).reshape(1, 1)),
        "bandrev1": band1_rev,
        "w1": c(np.asarray(inputs["L1_ffn_W1"], f32)),
        "w2": c(np.asarray(inputs["L1_ffn_W2"], f32)),
        "wg": c(np.asarray(inputs["L1_ffn_Wg"], f32)),
        "bg": c(np.asarray(inputs["L1_ffn_bg"], f32)),
    }
    in_maps = []
    for j in range(8):
        m = dict(common)
        m["x_fm"] = c(x[j].T)
        in_maps.append(m)
    return in_maps


def kernel(**inputs):
    in_maps = _host_inputs(inputs)
    nc = build()
    res = run_bass_kernel_spmd(nc, in_maps, core_ids=list(range(8)))
    out = np.empty((8, T, D), np.float32)
    for j in range(8):
        out[j] = np.asarray(res.results[j]["out_fm"]).T
    return out


# version 04
# speedup vs baseline: 1.2397x; 1.0559x over previous
import math
import numpy as np
import concourse.bass as bass
import concourse.mybir as mybir
from concourse.bass_utils import run_bass_kernel_spmd

F32 = mybir.dt.float32
AF = mybir.ActivationFunctionType
ALU = mybir.AluOpType

D = 1024
T = 2048
DC = 8
QT = 256
NQT = T // QT
BW = 3968
BC0 = 1920
NEG = -1.0e30
EPS = 1e-5


def _alibi_slopes32():
    base = 2.0 ** (-(2.0 ** (-(math.log2(32) - 3))))
    powers = np.arange(1, 33, dtype=np.float32)
    return (np.float32(base) ** powers).astype(np.float32)


def _bandmat_l0():
    band = np.arange(4095, dtype=np.float64) - 2047.0
    band[band > 0] = NEG
    p = np.arange(128)[:, None]
    c = np.arange(BW)[None, :]
    idx = p - c + (BW - 1)
    ok = (idx >= 0) & (idx <= 4094)
    return (band[np.clip(idx, 0, 4094)] * ok).astype(np.float32)


class Cnt:
    def __init__(self, sem):
        self.sem = sem
        self.val = 0

    def inc(self, ins, n=1):
        ins.then_inc(self.sem, n)
        self.val += n
        return self.val


class Rel:

    def __init__(self, cnt):
        self.c = cnt
        self.b = cnt.val

    def inc(self, ins, n=1):
        return self.c.inc(ins, n)

    def w(self, eng, v):
        if self.b + v > 0:
            eng.wait_ge(self.c.sem, self.b + v)

    @property
    def val(self):
        return self.c.val

    @property
    def rval(self):
        return self.c.val - self.b

    @property
    def sem(self):
        return self.c.sem


def build(upto=99, dumps=(), race=False):
    nc = bass.Bass("TRN2", target_bir_lowering=False,
                   detect_race_conditions=race, num_devices=8)
    dumps = set(dumps)

    x_in = nc.dram_tensor("x_fm", [D, T], F32, kind="ExternalInput").ap()
    w_qkv = nc.dram_tensor("w_qkv", [D, 4096], F32, kind="ExternalInput").ap()
    w_out = nc.dram_tensor("w_out", [2048, D], F32, kind="ExternalInput").ap()
    cbt = nc.dram_tensor("cbt", [D, D], F32, kind="ExternalInput").ap()
    scb_d = nc.dram_tensor("scb", [D], F32, kind="ExternalInput").ap()
    cbias_d = nc.dram_tensor("cbias", [D], F32, kind="ExternalInput").ap()
    wqT = nc.dram_tensor("wqT", [D, D], F32, kind="ExternalInput").ap()
    wkT = nc.dram_tensor("wkT", [D, D], F32, kind="ExternalInput").ap()
    wvT = nc.dram_tensor("wvT", [D, D], F32, kind="ExternalInput").ap()
    woT = nc.dram_tensor("woT", [D, D], F32, kind="ExternalInput").ap()
    bq_d = nc.dram_tensor("bq", [D], F32, kind="ExternalInput").ap()
    bk_d = nc.dram_tensor("bk", [D], F32, kind="ExternalInput").ap()
    bv_d = nc.dram_tensor("bv", [D], F32, kind="ExternalInput").ap()
    bo_d = nc.dram_tensor("bo", [D], F32, kind="ExternalInput").ap()
    etg = nc.dram_tensor("etg", [1, 1], F32, kind="ExternalInput").ap()
    isg = nc.dram_tensor("isg", [1, 1], F32, kind="ExternalInput").ap()
    bandrev1 = nc.dram_tensor("bandrev1", [4095], F32, kind="ExternalInput").ap()
    w1_d = nc.dram_tensor("w1", [8, D, 4096], F32, kind="ExternalInput").ap()
    w2_d = nc.dram_tensor("w2", [8, 4096, D], F32, kind="ExternalInput").ap()
    wg_d = nc.dram_tensor("wg", [D, 8], F32, kind="ExternalInput").ap()
    bg_d = nc.dram_tensor("bg", [8], F32, kind="ExternalInput").ap()
    out_d = nc.dram_tensor("out_fm", [D, T], F32, kind="ExternalOutput").ap()

    q0_dram = nc.dram_tensor("q0_scr", [2048, T], F32).ap()
    q1_dram = nc.dram_tensor("q1_scr", [D, T], F32).ap()
    xln_dram = nc.dram_tensor("xln_scr", [D, T], F32).ap()
    x1_dram = nc.dram_tensor("x1_scr", [D, T], F32).ap()
    x2_dram = nc.dram_tensor("x2_scr", [D, T], F32).ap()
    x3_dram = nc.dram_tensor("x3_scr", [D, T], F32).ap()
    ctx0_dram = nc.dram_tensor("ctx0_scr", [2048, T], F32).ap()
    ctx1_dram = nc.dram_tensor("ctx1_scr", [D, T], F32).ap()
    agctx = nc.dram_tensor("agctx", [8 * D, T], F32).ap()
    bm1_dram = nc.dram_tensor("bm1_scr", [128, BW], F32).ap()
    bdev_dram = nc.dram_tensor("bdev_scr", [4095], F32).ap()

    bm0 = nc.inline_tensor(_bandmat_l0(), name="bm0").ap()
    selmat = np.zeros((8, 8 * 128), np.float32)
    for e in range(8):
        selmat[e, e * 128:(e + 1) * 128] = 1.0
    sel_c = nc.inline_tensor(selmat, name="selc").ap()
    id8_c = nc.inline_tensor(np.eye(8, dtype=np.float32), name="id8c").ap()
    id128_c = nc.inline_tensor(np.eye(128, dtype=np.float32), name="id128c").ap()
    ucum_np = np.triu(np.ones((8, 8), np.float32), 1).T
    ucum_c = nc.inline_tensor(np.ascontiguousarray(ucum_np.T), name="ucumc").ap()
    slopes8 = [float(s) * 8.0 for s in _alibi_slopes32()]

    dump_aps = {}
    for name, shape in [("d_xln", [D, T]), ("d_k", [D, T]), ("d_v", [128, 16 * 1040]),
                        ("d_q0", [2048, T]), ("d_ctx0", [2048, T]), ("d_x1", [D, T]),
                        ("d_x2", [D, T]), ("d_ctx1", [D, T]), ("d_a2a", [2 * D, T]),
                        ("d_x3", [D, T]), ("d_bm1", [128, BW]), ("d_q1", [D, T]),
                        ("d_mask", [8, T]), ("d_xln2", [D, T]), ("d_scbq", [128, 16]), ("d_csrc", [128, 8 * T]), ("d_gdbg", [128, 32])]:
        if name in dumps:
            dump_aps[name] = nc.dram_tensor(name, shape, F32, kind="ExternalOutput").ap()

    onesc = nc.alloc_sbuf_tensor("onesc", [128, 1], F32).ap()
    ones1 = nc.alloc_sbuf_tensor("ones1", [1, 128], F32).ap()
    scb_sb = nc.alloc_sbuf_tensor("scb_sb", [128, 8], F32).ap()
    cbias_sb = nc.alloc_sbuf_tensor("cbias_sb", [128, 8], F32).ap()
    bq_sb = nc.alloc_sbuf_tensor("bq_sb", [128, 8], F32).ap()
    bk_sb = nc.alloc_sbuf_tensor("bk_sb", [128, 8], F32).ap()
    bo_sb = nc.alloc_sbuf_tensor("bo_sb", [128, 8], F32).ap()
    bg_sb = nc.alloc_sbuf_tensor("bg_sb", [8, 1], F32).ap()
    r127 = nc.alloc_sbuf_tensor("r127", [128, 1], F32).ap()
    r127b = nc.alloc_sbuf_tensor("r127b", [128, 1], F32).ap()
    scbq_sb = nc.alloc_sbuf_tensor("scbq_sb", [128, 8], F32).ap()
    onespad_c = nc.alloc_sbuf_tensor("onespad", [128, 64], F32).ap()

    sems = {}

    def C(name):
        if name not in sems:
            sems[name] = Cnt(nc.alloc_semaphore(name))
        return Rel(sems[name])

    def bar():
        nc.all_engine_barrier()

    def w(eng, cnt):
        if cnt.val > 0:
            eng.wait_ge(cnt.sem, cnt.val)

    def dump_rows(name, dram_src, nrow):
        if name not in dump_aps:
            return
        sD = C("dmp")
        with nc.Block() as blk:
            @blk.scalar
            def _(sc):
                for r in range(nrow // 128):
                    sD.inc(sc.dma_start(dump_aps[name][r * 128:(r + 1) * 128, :],
                                        dram_src[r * 128:(r + 1) * 128, :]), 16)
                w(sc, sD)
        bar()

    sP0 = C("p0ld")
    sP0g = C("p0g")
    with nc.Block() as blk:
        @blk.sync
        def _(sy):
            for dst, src in [(scb_sb, scb_d), (cbias_sb, cbias_d), (bq_sb, bq_d),
                             (bk_sb, bk_d), (bo_sb, bo_d)]:
                for dc in range(DC):
                    sP0.inc(sy.dma_start(dst[:, dc:dc + 1],
                                         src[dc * 128:(dc + 1) * 128]), 16)
            sP0.inc(sy.dma_start(bg_sb[:, :], bg_d[:]), 16)
            w(sy, sP0)

        @blk.gpsimd
        def _(g):
            g.memset(onesc[:, :], 1.0)
            g.memset(ones1[:, :], 1.0)
            g.memset(onespad_c[:, :], 0.0)
            g.memset(onespad_c[0:1, :], 1.0)
            sP0g.inc(g.memset(r127[:, :], 127.0))

        @blk.vector
        def _(v):
            sP0.w(v, sP0.val)
            sP0g.w(v, 1)
            sP0g.inc(v.reciprocal(r127b[:, :], r127[:, :]))
            sP0g.w(v, 2)
            v.tensor_single_scalar(scbq_sb[:, :], scb_sb[:, :], r127b[:, 0:1], ALU.mult)
    bar()

    def dump_smalls():
        if "d_scbq" not in dump_aps:
            return
        sDx = C("dmpg")
        with nc.Block() as blk:
            @blk.gpsimd
            def _(g):
                sDx.inc(g.dma_start(dump_aps["d_scbq"][:, 0:8], scbq_sb[:, :]), 16)
                sDx.inc(g.dma_start(dump_aps["d_scbq"][:, 8:12], cbias_sb[:, 0:4]), 16)
                sDx.inc(g.dma_start(dump_aps["d_scbq"][:, 12:13], r127[:, :],
                                    allow_slow_non_contiguous=True), 16)
                sDx.inc(g.dma_start(dump_aps["d_scbq"][:, 13:14], r127b[:, :],
                                    allow_slow_non_contiguous=True), 16)
                sDx.inc(g.dma_start(dump_aps["d_scbq"][:, 14:15], scb_sb[:, 0:1],
                                    allow_slow_non_contiguous=True), 16)
                w(g, sDx)
        bar()

    lnid = [0]

    def emit_ln_stream(src_dram, dst_dram):
        lnid[0] += 1
        W_ = 512
        ntt = 4
        tag = "ln"
        utag = f"ln{lnid[0]}"
        sXL, sSQ, sSUM, sVA, sRT, sST2, sBC, sAPP = (C(tag + s) for s in
            ("xl", "sq", "sum", "va", "rt", "st2", "bc", "app"))
        sOSs = [C(tag + "os0"), C(tag + "os1")]
        with (
            nc.psum_tensor(utag + "_s1", [1, W_], F32) as ps1_g,
            nc.psum_tensor(utag + "_s2", [1, W_], F32) as ps2_g,
            nc.psum_tensor(utag + "_a", [128, W_], F32) as psa_g,
            nc.psum_tensor(utag + "_b", [128, W_], F32) as psb_g,
            nc.sbuf_tensor(utag + "_xh", [128, DC * W_], F32) as xh_g,
            nc.sbuf_tensor(utag + "_sq", [128, DC * W_], F32) as sq_g,
            nc.sbuf_tensor(utag + "_ob", [128, 2 * W_], F32) as ob_g,
            nc.sbuf_tensor(utag + "_mean", [1, W_], F32) as mean_g,
            nc.sbuf_tensor(utag + "_varp", [1, W_], F32) as varp_g,
            nc.sbuf_tensor(utag + "_t1", [1, W_], F32) as t1_g,
            nc.sbuf_tensor(utag + "_inv", [1, W_], F32) as inv_g,
        ):
            ps1, ps2, psa, psb = ps1_g.ap(), ps2_g.ap(), psa_g.ap(), psb_g.ap()
            xh, sq, ob = xh_g.ap(), sq_g.ap(), ob_g.ap()
            mean, varp, t1, inv = mean_g.ap(), varp_g.ap(), t1_g.ap(), inv_g.ap()
            with nc.Block() as blk:
                @blk.sync
                def _(sy):
                    for tt in range(ntt):
                        if tt >= 1:
                            sAPP.w(sy, tt * DC)
                        sXL.inc(sy.dma_start(
                            bass.AP(xh.tensor, xh.offset,
                                    [[DC * W_, 128], [W_, DC], [1, W_]]),
                            bass.AP(src_dram.tensor,
                                    src_dram.offset + tt * W_,
                                    [[T, 128], [128 * T, DC], [1, W_]])), 16)

                @blk.scalar
                def _(sc):
                    for tt in range(ntt):
                        if tt >= 1:
                            sSUM.w(sc, tt)
                        sXL.w(sc, 16 * (tt + 1))
                        for dc in range(DC):
                            sSQ.inc(sc.activation(sq[:, dc * W_:(dc + 1) * W_],
                                                  xh[:, dc * W_:(dc + 1) * W_],
                                                  AF.Square))
                        sVA.w(sc, tt + 1)
                        sRT.inc(sc.activation(t1, varp, AF.Sqrt))

                @blk.tensor
                def _(t):
                    for tt in range(ntt):
                        if tt >= 1:
                            sVA.w(t, tt)
                        sXL.w(t, 16 * (tt + 1))
                        ins = None
                        for dc in range(DC):
                            i = tt * DC + dc
                            t.matmul(ps1[:, :], onesc[:, :],
                                     xh[:, dc * W_:(dc + 1) * W_],
                                     start=(dc == 0), stop=(dc == DC - 1))
                            sSQ.w(t, i + 1)
                            ins = t.matmul(ps2[:, :], onesc[:, :],
                                           sq[:, dc * W_:(dc + 1) * W_],
                                           start=(dc == 0), stop=(dc == DC - 1))
                        sSUM.inc(ins)
                        sST2.w(t, tt + 1)
                        if tt >= 1:
                            sAPP.w(t, tt * DC)
                        t.matmul(psa[:, :], ones1[:, :], inv, start=True, stop=True)
                        sBC.inc(t.matmul(psb[:, :], ones1[:, :], mean,
                                         start=True, stop=True))

                @blk.vector
                def _(v):
                    for tt in range(ntt):
                        sSUM.w(v, tt + 1)
                        v.tensor_single_scalar(mean, ps1[0:1, :], 1.0 / 1024.0, ALU.mult)
                        v.tensor_single_scalar(varp, ps2[0:1, :], 1.0 / 1024.0, ALU.mult)
                        v.tensor_tensor(t1, mean, mean, ALU.mult)
                        v.tensor_tensor(varp, varp, t1, ALU.subtract)
                        sVA.inc(v.tensor_single_scalar(varp, varp, EPS, ALU.add))
                        sRT.w(v, tt + 1)
                        v.reciprocal(inv, t1)
                        for _i in range(2):
                            v.tensor_tensor(t1, inv, inv, ALU.mult)
                            v.tensor_tensor(t1, t1, varp, ALU.mult)
                            v.tensor_scalar(t1, t1, -0.5, 1.5, ALU.mult, ALU.add)
                            v.tensor_tensor(inv, inv, t1, ALU.mult)
                        sST2.inc(v.tensor_tensor(mean, mean, inv, ALU.mult))
                        sBC.w(v, tt + 1)
                        for dc in range(DC):
                            i = tt * DC + dc
                            if i >= 2:
                                sOSs[i % 2].w(v, 16 * (i // 2))
                            v.tensor_tensor(ob[:, (i % 2) * W_:(i % 2 + 1) * W_],
                                            xh[:, dc * W_:(dc + 1) * W_],
                                            psa[:, :], ALU.mult)
                            sAPP.inc(v.tensor_tensor(
                                ob[:, (i % 2) * W_:(i % 2 + 1) * W_],
                                ob[:, (i % 2) * W_:(i % 2 + 1) * W_],
                                psb[:, :], ALU.subtract))

                @blk.gpsimd
                def _(g):
                    for tt in range(ntt):
                        for dc in range(DC):
                            i = tt * DC + dc
                            sAPP.w(g, i + 1)
                            sOSs[i % 2].inc(g.dma_start(
                                bass.AP(dst_dram.tensor,
                                        dst_dram.offset + dc * 128 * T + tt * W_,
                                        [[T, 128], [1, W_]]),
                                ob[:, (i % 2) * W_:(i % 2 + 1) * W_]), 16)
                    w(g, sOSs[0])
                    w(g, sOSs[1])
            bar()

    vpid = [0]

    def emit_v(vsb, wv_src, bias_bc):
        vpid[0] += 1
        tag = "vp"
        utag = f"vp{vpid[0]}"
        sWL, sXT, sM, sE, sZ = (C(tag + s) for s in ("wl", "xt", "mm", "ev", "z"))
        with (
            nc.sbuf_tensor(utag + "_wst", [128, 32 * 256], F32) as wst_g,
            nc.sbuf_tensor(utag + "_xt", [128, DC * 512], F32) as xt_g,
            nc.psum_tensor(utag + "_ps0", [128, 256], F32) as ps0_g,
            nc.psum_tensor(utag + "_ps1", [128, 256], F32) as ps1_g,
        ):
            wst = wst_g.ap()
            xt = xt_g.ap()
            psv = [ps0_g.ap(), ps1_g.ap()]
            with nc.Block() as blk:
                @blk.sync
                def _(sy):
                    for sp in range(4):
                        for dc in range(DC):
                            sWL.inc(sy.dma_start(
                                wst[:, (sp * DC + dc) * 256:(sp * DC + dc + 1) * 256],
                                wv_src(sp, dc)), 16)
                    for tt in range(4):
                        if tt >= 1:
                            sM.w(sy, tt * 16)
                        sXT.inc(sy.dma_start(
                            bass.AP(xt.tensor, xt.offset,
                                    [[DC * 512, 128], [512, DC], [1, 512]]),
                            bass.AP(xln_dram.tensor,
                                    xln_dram.offset + tt * 512,
                                    [[T, 128], [128 * T, DC], [1, 512]])), 16)

                @blk.gpsimd
                def _(g):
                    sZ.inc(g.memset(vsb[:, :], 1.0))

                @blk.tensor
                def _(t):
                    w(t, sWL)
                    j = 0
                    for tt in range(4):
                        sXT.w(t, 16 * (tt + 1))
                        for ksub in range(4):
                            for sp in range(4):
                                if j >= 2:
                                    sE.w(t, j - 1)
                                ins = None
                                for dc in range(DC):
                                    ins = t.matmul(
                                        psv[j % 2][:, :],
                                        xt[:, dc * 512 + ksub * 128:
                                           dc * 512 + (ksub + 1) * 128],
                                        wst[:, (sp * DC + dc) * 256:
                                            (sp * DC + dc + 1) * 256],
                                        start=(dc == 0), stop=(dc == DC - 1))
                                sM.inc(ins)
                                j += 1

                @blk.vector
                def _(v):
                    sZ.w(v, 1)
                    j = 0
                    for tt in range(4):
                        for ksub in range(4):
                            kt = tt * 4 + ksub
                            for sp in range(4):
                                sM.w(v, j + 1)
                                dst = bass.AP(vsb.tensor,
                                              vsb.offset + kt * 1040 + sp * 260,
                                              [[16640, 128], [65, 4], [1, 64]])
                                if bias_bc is None:
                                    sE.inc(v.tensor_copy(dst, psv[j % 2][:, :]))
                                else:
                                    sE.inc(v.tensor_tensor(
                                        dst, psv[j % 2][:, :],
                                        bias_bc[:, sp * 256:(sp + 1) * 256], ALU.add))
                                j += 1
            bar()

    qkid = [0]

    def emit_qk(ksb, chunks, wgroup_src, bias_for, q_dram_ap):
        qkid[0] += 1
        tag = "qk"
        utag = f"qk{qkid[0]}"
        sXT, sM, sE = C(tag + "xt"), C(tag + "mm"), C(tag + "ev")
        sWs = [C(tag + "wl0"), C(tag + "wl1")]
        sQs = [C(tag + "qs0"), C(tag + "qs1")]
        nch = len(chunks)
        with (
            nc.sbuf_tensor(utag + "_wst", [128, 2 * 8 * 128], F32) as wst_g,
            nc.sbuf_tensor(utag + "_xt", [128, DC * 512], F32) as xt_g,
            nc.sbuf_tensor(utag + "_qst", [128, 2 * 512], F32) as qst_g,
            nc.psum_tensor(utag + "_ps0", [128, 512], F32) as ps0_g,
            nc.psum_tensor(utag + "_ps1", [128, 512], F32) as ps1_g,
        ):
            wst, xt, qst = wst_g.ap(), xt_g.ap(), qst_g.ap()
            psq = [ps0_g.ap(), ps1_g.ap()]
            with nc.Block() as blk:
                @blk.sync
                def _(sy):
                    for tt in range(4):
                        if tt >= 1:
                            sM.w(sy, tt * nch)
                        sXT.inc(sy.dma_start(
                            bass.AP(xt.tensor, xt.offset,
                                    [[DC * 512, 128], [512, DC], [1, 512]]),
                            bass.AP(xln_dram.tensor,
                                    xln_dram.offset + tt * 512,
                                    [[T, 128], [128 * T, DC], [1, 512]])), 16)
                        for ci, (mch, kind) in enumerate(chunks):
                            j = tt * nch + ci
                            if j >= 2:
                                sM.w(sy, j - 1)
                            sWs[j % 2].inc(sy.dma_start(
                                bass.AP(wst.tensor,
                                        wst.offset + (j % 2) * 1024,
                                        [[2 * 1024, 128], [128, 8], [1, 128]]),
                                wgroup_src(mch, kind)), 16)

                @blk.tensor
                def _(t):
                    for tt in range(4):
                        sXT.w(t, 16 * (tt + 1))
                        for ci, (mch, kind) in enumerate(chunks):
                            j = tt * nch + ci
                            if j >= 2:
                                sE.w(t, j - 1)
                            sWs[j % 2].w(t, 16 * (j // 2 + 1))
                            ins = None
                            for dc in range(DC):
                                ins = t.matmul(
                                    psq[j % 2][:, :],
                                    wst[:, (j % 2) * 1024 + dc * 128:
                                        (j % 2) * 1024 + (dc + 1) * 128],
                                    xt[:, dc * 512:(dc + 1) * 512],
                                    start=(dc == 0), stop=(dc == DC - 1))
                            sM.inc(ins)

                @blk.scalar
                def _(sc):
                    jq = 0
                    for tt in range(4):
                        for ci, (mch, kind) in enumerate(chunks):
                            j = tt * nch + ci
                            sM.w(sc, j + 1)
                            bias = bias_for(mch, kind)
                            if kind == "k":
                                dst = ksb[:, mch * T + tt * 512: mch * T + (tt + 1) * 512]
                            else:
                                if jq >= 2:
                                    sQs[jq % 2].w(sc, 16 * (jq // 2))
                                dst = qst[:, (jq % 2) * 512:(jq % 2 + 1) * 512]
                            if bias is None:
                                sE.inc(sc.copy(dst, psq[j % 2][:, :]))
                            else:
                                sE.inc(sc.activation(dst, psq[j % 2][:, :],
                                                     AF.Identity, bias=bias))
                            if kind == "q":
                                jq += 1

                @blk.gpsimd
                def _(g):
                    jq = 0
                    for tt in range(4):
                        for ci, (mch, kind) in enumerate(chunks):
                            j = tt * nch + ci
                            if kind != "q":
                                continue
                            sE.w(g, j + 1)
                            sQs[jq % 2].inc(g.dma_start(
                                q_dram_ap[mch * 128:(mch + 1) * 128,
                                          tt * 512:(tt + 1) * 512],
                                qst[:, (jq % 2) * 512:(jq % 2 + 1) * 512]), 16)
                            jq += 1
                    w(g, sQs[0])
                    w(g, sQs[1])
            bar()

    atid = [0]

    def emit_attention(layer, ksb, vsb, q_dram_ap, band_dram, ctxl_dram):
        atid[0] += 1
        tag = "at"
        utag = f"at{atid[0]}"
        NH = 32 if layer == 0 else 16
        LAG = 3
        sBD, sSM, sBA, sEX, sPV, sRC, sBC2, sCX, sCP = (
            C(tag + s) for s in
            ("bd", "sm", "ba", "ex", "pv", "rc", "bc", "cx", "cp"))
        sQLs = [C(tag + "ql0"), C(tag + "ql1"), C(tag + "ql2"), C(tag + "ql3")]
        sCSs = [C(tag + "cs0"), C(tag + "cs1")]
        sZP = C(tag + "zp")
        npair = [(qt + 1) if layer == 0 else 8 for qt in range(NQT)]
        pairs = []
        cum = {}
        tot = 0
        for qt in range(NQT):
            for h in range(NH):
                gi = qt * NH + h
                cum[gi] = tot
                for kp in range(npair[qt]):
                    pairs.append((tot + kp, gi, qt, h, kp))
                tot += npair[qt]
        n_items = tot
        bd_cum = [0]
        for qt in range(NQT):
            bd_cum.append(bd_cum[-1] + 2 * npair[qt])
        jlast = {}
        for (i, gi, qt, h, kp) in pairs:
            if kp == npair[qt] - 1:
                jlast[gi] = i
        NHEADS = NH * NQT
        bcast_after_pv = {}
        recip_after = {}
        evac_after = {}
        for g in range(NHEADS):
            bcast_after_pv.setdefault(min(jlast[g] + 1, n_items - 1), []).append(g)
            recip_after.setdefault(min(jlast[g] + LAG, n_items - 1), []).append(g)
            evac_after.setdefault(min(jlast[g] + LAG + 1, n_items - 1), []).append(g)

        def kv_of(h):
            g = h // 2 if layer == 0 else h
            return g // 2, (g % 2) * 64, 65 * g

        with (
            nc.sbuf_tensor(utag + "_band", [128, 16 * QT], F32) as band_g,
            nc.sbuf_tensor(utag + "_q", [128, 4 * QT], F32) as q_g,
            nc.sbuf_tensor(utag + "_p", [128, 4 * 512], F32) as p_g,
            nc.sbuf_tensor(utag + "_rin", [128, 2 * QT], F32) as rin_g,
            nc.sbuf_tensor(utag + "_cst", [64, 2 * QT], F32) as cst_g,
            nc.sbuf_tensor(utag + "_rbs", [64, 2 * QT], F32) as rbs_g,
            nc.psum_tensor(utag + "_s0", [128, 512], F32) as s0_g,
            nc.psum_tensor(utag + "_s1", [128, 512], F32) as s1_g,
            nc.psum_tensor(utag + "_s2", [128, 512], F32) as s2_g,
            nc.psum_tensor(utag + "_s3", [128, 512], F32) as s3_g,
            nc.psum_tensor(utag + "_c0", [65, QT], F32) as c0_g,
            nc.psum_tensor(utag + "_c1", [65, QT], F32) as c1_g,
            nc.psum_tensor(utag + "_rb", [64, 2 * QT], F32) as rb_g,
        ):
            band = band_g.ap()
            qsb = q_g.ap()
            psb = p_g.ap()
            rin = rin_g.ap()
            cst = cst_g.ap()
            rbs = rbs_g.ap()
            spsum = [s0_g.ap(), s1_g.ap(), s2_g.ap(), s3_g.ap()]
            cpsum = [c0_g.ap(), c1_g.ap()]
            rbps = rb_g.ap()

            with nc.Block() as blk:
                @blk.sync
                def _(sy):
                    sZP.w(sy, 1)
                    for qt in range(NQT):
                        if qt >= 1:
                            sBA.w(sy, NH * (bd_cum[qt] // 2))
                        for kc in range(2 * npair[qt]):
                            c_off = qt * QT - kc * 128 + BC0
                            sBD.inc(sy.dma_start(
                                band[:, kc * QT:(kc + 1) * QT],
                                bass.AP(band_dram.tensor, band_dram.offset + c_off,
                                        [[BW, 128], [1, QT]])), 16)
                        for h in range(NH):
                            gi = qt * NH + h
                            kvoff_h = kv_of(h)[1]
                            if gi >= 4:
                                sSM.w(sy, cum[gi - 3])
                            sQLs[gi % 4].inc(sy.dma_start(
                                bass.AP(qsb.tensor,
                                        qsb.offset + kvoff_h * (4 * QT)
                                        + (gi % 4) * QT,
                                        [[4 * QT, 64], [1, QT]]),
                                q_dram_ap[h * 64:(h + 1) * 64,
                                          qt * QT:(qt + 1) * QT]), 16)

                @blk.tensor
                def _(t):
                    def emit_scores(pr):
                        i, gi, qt, h, kp = pr
                        kvc, kvoff, vcol = kv_of(h)
                        if kp == 0:
                            sQLs[gi % 4].w(t, 16 * (gi // 4 + 1))
                        if i >= 4:
                            sEX.w(t, i - 3)
                        ins = None
                        for half in range(2):
                            kc = kp * 2 + half
                            lhs = bass.AP(ksb.tensor,
                                          ksb.offset + kvc * T + kc * 128,
                                          [[DC * T, 128], [1, 128]])
                            rhsq = bass.AP(qsb.tensor,
                                           qsb.offset + (gi % 4) * QT,
                                           [[4 * QT, 128], [1, QT]])
                            ins = t.matmul(
                                spsum[i % 4][:, half * QT:(half + 1) * QT], lhs,
                                rhsq, start=True, stop=True)
                        sSM.inc(ins)

                    def emit_pv(pr):
                        i, gi, qt, h, kp = pr
                        kvc, kvoff, vcol = kv_of(h)
                        sEX.w(t, i + 1)
                        if kp == 0 and gi >= 2:
                            sCX.w(t, gi - 1)
                        ins = None
                        for half in range(2):
                            kc = kp * 2 + half
                            lhs = bass.AP(vsb.tensor, vsb.offset + kc * 1040 + vcol,
                                          [[16640, 128], [1, 65]])
                            ins = t.matmul(
                                cpsum[gi % 2][:, :], lhs,
                                psb[:, (i % 4) * 512 + half * QT:
                                    (i % 4) * 512 + (half + 1) * QT],
                                start=(kp == 0 and half == 0),
                                stop=(kp == npair[qt] - 1 and half == 1))
                        sPV.inc(ins)
                        for g2 in bcast_after_pv.get(i, []):
                            sRC.w(t, g2 + 1)
                            if g2 >= 2:
                                sCP.w(t, g2 - 1)
                            sBC2.inc(t.matmul(
                                rbps[:, (g2 % 2) * QT:(g2 % 2 + 1) * QT],
                                onespad_c[:, 0:64],
                                rin[:, (g2 % 2) * QT:(g2 % 2 + 1) * QT],
                                start=True, stop=True))

                    for idx, pr in enumerate(pairs):
                        emit_scores(pr)
                        if idx >= LAG:
                            emit_pv(pairs[idx - LAG])
                    for j in range(max(n_items - LAG, 0), n_items):
                        emit_pv(pairs[j])

                @blk.vector
                def _(v):
                    def vhead_recip(g):
                        sPV.w(v, jlast[g] + 1)
                        if g >= 2:
                            sBC2.w(v, g - 1)
                        sRC.inc(v.reciprocal(
                            rin[0:1, (g % 2) * QT:(g % 2 + 1) * QT],
                            cpsum[g % 2][64:65, :]))

                    def vhead_evac(g):
                        sBC2.w(v, g + 1)
                        if g >= 2:
                            sCSs[g % 2].w(v, 16 * (g // 2))
                        sCP.inc(v.tensor_copy(
                            rbs[:, (g % 2) * QT:(g % 2 + 1) * QT],
                            rbps[:, (g % 2) * QT:(g % 2 + 1) * QT]))
                        sCX.inc(v.tensor_tensor(
                            cst[:, (g % 2) * QT:(g % 2 + 1) * QT],
                            cpsum[g % 2][0:64, :],
                            rbs[:, (g % 2) * QT:(g % 2 + 1) * QT], ALU.mult))

                    v.memset(qsb[:, :], 0.0)
                    sZP.inc(v.memset(rin[:, :], 0.0))
                    for idx, pr in enumerate(pairs):
                        i, gi, qt, h, kp = pr
                        if kp == 0 and h == 0:
                            sBD.w(v, 16 * bd_cum[qt + 1])
                        sSM.w(v, i + 1)
                        sps = spsum[i % 4]
                        if layer == 0:
                            ins = v.scalar_tensor_tensor(
                                sps[:, :], band[:, kp * 2 * QT:(kp * 2 + 2) * QT],
                                slopes8[h], sps[:, :], ALU.mult, ALU.add)
                        else:
                            ins = v.tensor_tensor(
                                sps[:, :], sps[:, :],
                                band[:, kp * 2 * QT:(kp * 2 + 2) * QT], ALU.add)
                        sBA.inc(ins)
                        for g2 in recip_after.get(i, []):
                            vhead_recip(g2)
                        for g2 in evac_after.get(i, []):
                            vhead_evac(g2)

                @blk.scalar
                def _(sc):
                    for idx, pr in enumerate(pairs):
                        i, gi, qt, h, kp = pr
                        sBA.w(sc, i + 1)
                        if i >= 4:
                            sPV.w(sc, i - 3)
                        sEX.inc(sc.activation(
                            psb[:, (i % 4) * 512:(i % 4 + 1) * 512],
                            spsum[i % 4][:, :], AF.Exp, scale=0.125))

                @blk.gpsimd
                def _(g):
                    for qt in range(NQT):
                        for h in range(NH):
                            gi = qt * NH + h
                            sCX.w(g, gi + 1)
                            sCSs[gi % 2].inc(g.dma_start(
                                ctxl_dram[h * 64:(h + 1) * 64, qt * QT:(qt + 1) * QT],
                                cst[:, (gi % 2) * QT:(gi % 2 + 1) * QT]), 16)
                    w(g, sCSs[0])
                    w(g, sCSs[1])
            bar()

    def emit_wout0():
        sCL, sM, sE = C("w0cl"), C("w0mm"), C("w0ev")
        sWs = [C("w0wl0"), C("w0wl1")]
        sXs = [C("w0xl0"), C("w0xl1")]
        sOs = [C("w0os0"), C("w0os1")]
        with (
            nc.sbuf_tensor("w0_ctx", [128, 16 * 512], F32) as ctx_g,
            nc.sbuf_tensor("w0_wst", [128, 2 * 16 * 128], F32) as wst_g,
            nc.sbuf_tensor("w0_xr", [128, 2 * 512], F32) as xr_g,
            nc.sbuf_tensor("w0_ob", [128, 2 * 512], F32) as ob_g,
            nc.psum_tensor("w0_ps0", [128, 512], F32) as p0_g,
            nc.psum_tensor("w0_ps1", [128, 512], F32) as p1_g,
        ):
            ctxt = ctx_g.ap()
            wst = wst_g.ap()
            xr = xr_g.ap()
            ob = ob_g.ap()
            pj = [p0_g.ap(), p1_g.ap()]
            with nc.Block() as blk:
                @blk.sync
                def _(sy):
                    for tt in range(4):
                        if tt >= 1:
                            sM.w(sy, tt * 8)
                        sCL.inc(sy.dma_start(
                            bass.AP(ctxt.tensor, ctxt.offset,
                                    [[16 * 512, 128], [512, 16], [1, 512]]),
                            bass.AP(ctx0_dram.tensor,
                                    ctx0_dram.offset + tt * 512,
                                    [[T, 128], [128 * T, 16], [1, 512]])), 16)
                        for dc in range(DC):
                            j = tt * 8 + dc
                            if j >= 2:
                                sE.w(sy, j - 1)
                            sXs[j % 2].inc(sy.dma_start(
                                xr[:, (j % 2) * 512:(j % 2 + 1) * 512],
                                x_in[dc * 128:(dc + 1) * 128,
                                     tt * 512:(tt + 1) * 512]), 16)
                            if j >= 2:
                                sM.w(sy, j - 1)
                            sWs[j % 2].inc(sy.dma_start(
                                bass.AP(wst.tensor,
                                        wst.offset + (j % 2) * 2048,
                                        [[2 * 2048, 128], [128, 16], [1, 128]]),
                                bass.AP(w_out.tensor, w_out.offset + dc * 128,
                                        [[D, 128], [128 * D, 16], [1, 128]])), 16)

                @blk.tensor
                def _(t):
                    for tt in range(4):
                        sCL.w(t, 16 * (tt + 1))
                        for dc in range(DC):
                            j = tt * 8 + dc
                            if j >= 2:
                                t_ = sE.w(t, j - 1)
                            sWs[j % 2].w(t, 16 * (j // 2 + 1))
                            ins = None
                            for pr in range(16):
                                ins = t.matmul(pj[j % 2][:, :],
                                               wst[:, (j % 2) * 2048 + pr * 128:
                                                   (j % 2) * 2048 + (pr + 1) * 128],
                                               ctxt[:, pr * 512:(pr + 1) * 512],
                                               start=(pr == 0), stop=(pr == 15))
                            sM.inc(ins)

                @blk.vector
                def _(v):
                    for tt in range(4):
                        for dc in range(DC):
                            j = tt * 8 + dc
                            sM.w(v, j + 1)
                            sXs[j % 2].w(v, 16 * (j // 2 + 1))
                            if j >= 2:
                                sOs[j % 2].w(v, 16 * (j // 2))
                            sE.inc(v.tensor_tensor(
                                ob[:, (j % 2) * 512:(j % 2 + 1) * 512],
                                pj[j % 2][:, :],
                                xr[:, (j % 2) * 512:(j % 2 + 1) * 512], ALU.add))

                @blk.gpsimd
                def _(g):
                    for tt in range(4):
                        for dc in range(DC):
                            j = tt * 8 + dc
                            sE.w(g, j + 1)
                            sOs[j % 2].inc(g.dma_start(
                                bass.AP(x1_dram.tensor,
                                        x1_dram.offset + dc * 128 * T + tt * 512,
                                        [[T, 128], [1, 512]]),
                                ob[:, (j % 2) * 512:(j % 2 + 1) * 512]), 16)
                    w(g, sOs[0])
                    w(g, sOs[1])
            bar()

    def emit_cb():
        sWL, sXT, sM, sE = C("cbwl"), C("cbxt"), C("cbmm"), C("cbev")
        sXs = [C("cbxr0"), C("cbxr1")]
        sOs = [C("cbos0"), C("cbos1")]
        with (
            nc.sbuf_tensor("cbt_sb", [128, DC * 1024], F32) as cbt_g,
            nc.sbuf_tensor("cb_xt", [128, DC * 512], F32) as xt_g,
            nc.sbuf_tensor("cb_xr", [128, 2 * 512], F32) as xr_g,
            nc.sbuf_tensor("cb_ob", [128, 2 * 512], F32) as ob_g,
            nc.sbuf_tensor("cb_obt", [128, 2 * 512], F32) as obt_g,
            nc.psum_tensor("cb_ps0", [128, 512], F32) as cps0,
            nc.psum_tensor("cb_ps1", [128, 512], F32) as cps1,
        ):
            cbtsb = cbt_g.ap()
            xt = xt_g.ap()
            xr = xr_g.ap()
            ob = ob_g.ap()
            obt = obt_g.ap()
            cps = [cps0.ap(), cps1.ap()]
            with nc.Block() as blk:
                @blk.sync
                def _(sy):
                    for dc in range(DC):
                        srcw = bass.AP(cbt.tensor, cbt.offset + dc * 128 * D,
                                       [[D, 128], [1, D]])
                        sWL.inc(sy.dma_start(cbtsb[:, dc * 1024:(dc + 1) * 1024],
                                             srcw), 16)
                    for tt in range(4):
                        if tt >= 1:
                            sM.w(sy, tt * 8)
                        sXT.inc(sy.dma_start(
                            bass.AP(xt.tensor, xt.offset,
                                    [[DC * 512, 128], [512, DC], [1, 512]]),
                            bass.AP(xln_dram.tensor,
                                    xln_dram.offset + tt * 512,
                                    [[T, 128], [128 * T, DC], [1, 512]])), 16)
                        for nc_ in range(8):
                            j = tt * 8 + nc_
                            if j >= 2:
                                sE.w(sy, j - 1)
                            sXs[j % 2].inc(sy.dma_start(
                                xr[:, (j % 2) * 512:(j % 2 + 1) * 512],
                                bass.AP(x1_dram.tensor,
                                        x1_dram.offset + nc_ * 128 * T + tt * 512,
                                        [[T, 128], [1, 512]])), 16)

                @blk.tensor
                def _(t):
                    w(t, sWL)
                    for tt in range(4):
                        sXT.w(t, 16 * (tt + 1))
                        for nc_ in range(8):
                            j = tt * 8 + nc_
                            if j >= 2:
                                sE.w(t, j - 1)
                            ins = None
                            for dc in range(DC):
                                ins = t.matmul(
                                    cps[j % 2][:, :],
                                    cbtsb[:, dc * 1024 + nc_ * 128:
                                          dc * 1024 + (nc_ + 1) * 128],
                                    xt[:, dc * 512:(dc + 1) * 512],
                                    start=(dc == 0), stop=(dc == DC - 1))
                            sM.inc(ins)

                @blk.vector
                def _(v):
                    for tt in range(4):
                        for nc_ in range(8):
                            j = tt * 8 + nc_
                            sM.w(v, j + 1)
                            sXs[j % 2].w(v, 16 * (j // 2 + 1))
                            if j >= 2:
                                sOs[j % 2].w(v, 16 * (j // 2))
                            v.tensor_copy(obt[:, (j % 2) * 512:(j % 2 + 1) * 512],
                                          cps[j % 2][:, :])
                            v.tensor_scalar(ob[:, (j % 2) * 512:(j % 2 + 1) * 512],
                                            obt[:, (j % 2) * 512:(j % 2 + 1) * 512],
                                            scbq_sb[:, nc_:nc_ + 1],
                                            cbias_sb[:, nc_:nc_ + 1],
                                            ALU.mult, ALU.add)
                            sE.inc(v.tensor_tensor(
                                ob[:, (j % 2) * 512:(j % 2 + 1) * 512],
                                ob[:, (j % 2) * 512:(j % 2 + 1) * 512],
                                xr[:, (j % 2) * 512:(j % 2 + 1) * 512], ALU.add))

                @blk.gpsimd
                def _(g):
                    for tt in range(4):
                        for nc_ in range(8):
                            j = tt * 8 + nc_
                            sE.w(g, j + 1)
                            sOs[j % 2].inc(g.dma_start(
                                bass.AP(x2_dram.tensor,
                                        x2_dram.offset + nc_ * 128 * T + tt * 512,
                                        [[T, 128], [1, 512]]),
                                ob[:, (j % 2) * 512:(j % 2 + 1) * 512]), 16)
                    w(g, sOs[0])
                    w(g, sOs[1])
            bar()

    def emit_bvbc(bvbc):
        sB = C("bvb")
        with (
            nc.sbuf_tensor("bvrow", [1, D], F32) as bvr_g,
            nc.psum_tensor("bv_ps", [128, 512], F32) as bvps_g,
        ):
            bvr = bvr_g.ap()
            bvps = bvps_g.ap()
            with nc.Block() as blk:
                @blk.sync
                def _(sy):
                    sB.inc(sy.dma_start(bvr[:, :], bv_d[:]), 16)

                @blk.tensor
                def _(t):
                    sB.w(t, 16)
                    sB.inc(t.matmul(bvps[:, :], ones1[:, :], bvr[:, 0:512],
                                    start=True, stop=True))
                    sB.w(t, 18)
                    sB.inc(t.matmul(bvps[:, :], ones1[:, :], bvr[:, 512:1024],
                                    start=True, stop=True))

                @blk.vector
                def _(v):
                    sB.w(v, 17)
                    sB.inc(v.tensor_copy(bvbc[:, 0:512], bvps[:, :]))
                    sB.w(v, 19)
                    v.tensor_copy(bvbc[:, 512:1024], bvps[:, :])
            bar()

    def emit_band1():
        sBP = C("bprol")
        sBPg = C("bprolg")
        with (
            nc.sbuf_tensor("brsb", [1, 4095], F32) as brsb_g,
            nc.sbuf_tensor("etsb", [1, 2], F32) as etsb_g,
        ):
            brsb = brsb_g.ap()
            etsb = etsb_g.ap()
            with nc.Block() as blk:
                @blk.sync
                def _(sy):
                    sBP.inc(sy.dma_start(brsb[:, :], bandrev1[:]), 16)
                    sBP.inc(sy.dma_start(etsb[:, 0:1], etg[:, :]), 16)
                    sBP.inc(sy.dma_start(etsb[:, 1:2], isg[:, :]), 16)

                @blk.scalar
                def _(sc):
                    sBP.w(sc, 48)
                    sBP.inc(sc.activation(etsb[:, 0:1], etsb[:, 0:1], AF.Sigmoid))

                @blk.vector
                def _(v):
                    sBP.w(v, 49)
                    v.tensor_single_scalar(etsb[:, 0:1], etsb[:, 0:1], 8.0, ALU.mult)
                    sBP.inc(v.tensor_single_scalar(etsb[:, 1:2], etsb[:, 1:2], 8.0,
                                                   ALU.mult))
                    sBP.w(v, 50)
                    sBP.inc(v.tensor_scalar(brsb[:, :], brsb[:, :], etsb[:, 0:1],
                                            etsb[:, 1:2], ALU.mult, ALU.add))

                @blk.gpsimd
                def _(g):
                    sBP.w(g, 51)
                    sBPg.inc(g.dma_start(bdev_dram[:], brsb[:, :]), 16)
                    sBPg.w(g, 16)
                    for p in range(128):
                        sBPg.inc(g.dma_start(
                            bm1_dram[p:p + 1, :],
                            bass.AP(bdev_dram.tensor, bdev_dram.offset + 127 - p,
                                    [[1, 1], [1, BW]])), 16)
                    w(g, sBPg)
            bar()

    def emit_scramble_proj():
        sCL, sM, sE = C("p12cl"), C("p12mm"), C("p12ev")
        sCLF = C("p12clf")
        sWs = [C("p12wl0"), C("p12wl1")]
        sXs = [C("p12xl0"), C("p12xl1")]
        sSs = [C("p12st0"), C("p12st1")]
        with (
            nc.sbuf_tensor("csrc", [128, 8 * T], F32) as csrc_g,
            nc.sbuf_tensor("wost", [128, 2 * 2048], F32) as wost_g,
            nc.sbuf_tensor("x2st", [128, 2 * T], F32) as x2st_g,
            nc.sbuf_tensor("x3st", [128, 2 * T], F32) as x3st_g,
            nc.psum_tensor("p12_ps", [128, 2048], F32) as ps_g,
        ):
            csrc = csrc_g.ap()
            wost = wost_g.ap()
            x2st = x2st_g.ap()
            x3st = x3st_g.ap()
            ps = ps_g.ap()
            with nc.Block() as blk:
                @blk.sync
                def _(sy):
                    for di_, dc in enumerate([1, 2, 3, 4, 5, 6, 7, 0]):
                        if di_ >= 2:
                            sM.w(sy, di_ - 1)
                        for uh in range(2):
                            srcw = bass.AP(woT.tensor, woT.offset + dc * 128,
                                           [[D, 64], [64 * D, 16], [1, 128]])
                            sWs[di_ % 2].inc(sy.dma_start(
                                bass.AP(wost.tensor,
                                        wost.offset + uh * 64 * (2 * 2048)
                                        + (di_ % 2) * 2048,
                                        [[2 * 2048, 64], [128, 16], [1, 128]]),
                                srcw), 16)
                        if di_ >= 2:
                            sE.w(sy, di_ - 1)
                        sXs[di_ % 2].inc(sy.dma_start(
                            x2st[:, (di_ % 2) * T:(di_ % 2 + 1) * T],
                            x2_dram[dc * 128:(dc + 1) * 128, :]), 16)

                @blk.tensor
                def _(t):
                    w(t, sCL)
                    sCLF.w(t, 16)
                    for di_, dc in enumerate([1, 2, 3, 4, 5, 6, 7, 0]):
                        sWs[di_ % 2].w(t, 32 * (di_ // 2 + 1))
                        if di_ >= 1:
                            sE.w(t, di_)
                        ins = None
                        for s in range(16):
                            for u in range(2):
                                lhs = bass.AP(wost.tensor,
                                              wost.offset + u * 64 * (2 * 2048)
                                              + (di_ % 2) * 2048 + s * 128,
                                              [[2 * 2048, 64], [1, 128]])
                                for m in range(8):
                                    k = u * 8 + m
                                    rhs = bass.AP(csrc.tensor,
                                                  csrc.offset + m * T
                                                  + (u * 64) * (8 * T) + s,
                                                  [[8 * T, 64], [16, 128]])
                                    ins = t.matmul(
                                        ps[:, k * 128:(k + 1) * 128], lhs, rhs,
                                        start=(s == 0 and k % 4 == 0),
                                        stop=(s == 15))
                        sM.inc(ins)

                @blk.vector
                def _(v):
                    for di_, dc in enumerate([1, 2, 3, 4, 5, 6, 7, 0]):
                        sM.w(v, di_ + 1)
                        sXs[di_ % 2].w(v, 16 * (di_ // 2 + 1))
                        if di_ >= 2:
                            sSs[di_ % 2].w(v, 16 * (di_ // 2))
                        sE.inc(v.scalar_tensor_tensor(
                            x3st[:, (di_ % 2) * T:(di_ % 2 + 1) * T],
                            ps[:, :], bo_sb[:, dc:dc + 1],
                            x2st[:, (di_ % 2) * T:(di_ % 2 + 1) * T],
                            ALU.add, ALU.add))

                @blk.gpsimd
                def _(g):
                    with g.register("p12r") as r12:
                        pid = g.partition_id()
                        for m in range(8):
                            g.reg_mul(r12, pid, 128 * T)
                            g.reg_add(r12, r12, m * D * T)
                            sCL.inc(g.dma_start(
                                csrc[:, m * T:(m + 1) * T],
                                bass.AP(agctx.tensor, r12,
                                        [[T, 128], [1, T]])), 16)
                    sCL.w(g, 128)
                    sCLF.inc(g.dma_start(x3st[:, 0:4], x2st[:, 0:4]), 16)
                    if "d_csrc" in dump_aps:
                        sDc = C("dmpg")
                        sDc.inc(g.dma_start(dump_aps["d_csrc"][:, :], csrc[:, :]), 16)
                        w(g, sDc)
                    for di_, dc in enumerate([1, 2, 3, 4, 5, 6, 7, 0]):
                        sE.w(g, di_ + 1)
                        sSs[di_ % 2].inc(g.dma_start(
                            x3_dram[dc * 128:(dc + 1) * 128, :],
                            x3st[:, (di_ % 2) * T:(di_ % 2 + 1) * T]), 16)
                    w(g, sSs[0])
                    w(g, sSs[1])
            bar()

    def emit_moe():
        TT = 512
        (sXL, sLSQ, sLSM, sLVA, sLRT, sLST, sLBC, sLAP) = (
            C("m" + s) for s in ("xl", "lsq", "lsm", "lva", "lrt", "lst", "lbc", "lap"))
        (sGM, sGB, sTR, sMK, sT2, sMF, sMB, sXE, sCU, sFW) = (
            C("m" + s) for s in
            ("gm", "gb", "tr", "mk", "t2", "mf", "mb", "xe", "cu", "fw"))
        (sHM, sGE, sW2T, sYE, sEV) = (
            C("m" + s) for s in ("hm", "ge", "w2t", "ye", "ev"))
        sW1s = [C("atql0"), C("atql1"), C("atql2"), C("atql3")]
        sW2s = [C("atcs0"), C("atcs1"), C("p12st0"), C("p12st1")]
        sOSs2 = [C("mosa"), C("mosb")]
        import contextlib
        with contextlib.ExitStack() as _st:
            x3_g = _st.enter_context(nc.sbuf_tensor("m_x3", [128, DC * TT], F32))
            x3ln_g = _st.enter_context(nc.sbuf_tensor("m_x3ln", [128, DC * TT], F32))
            xe_g = _st.enter_context(nc.sbuf_tensor("m_xe", [128, DC * TT], F32))
            h_g = _st.enter_context(nc.sbuf_tensor("m_h", [128, 32 * TT], F32))
            ysb_g = _st.enter_context(nc.sbuf_tensor("m_ysb", [128, DC * TT], F32))
            w1st_g = _st.enter_context(nc.sbuf_tensor("m_w1st", [128, 4 * 1024], F32))
            w2st_g = _st.enter_context(nc.sbuf_tensor("m_w2st", [128, 2 * 1024], F32))
            lg_g = _st.enter_context(nc.sbuf_tensor("m_lg", [8, TT], F32))
            mtk_g = _st.enter_context(nc.sbuf_tensor("m_mtk", [128, 8], F32))
            mtk2_g = _st.enter_context(nc.sbuf_tensor("m_mtk2", [128, 8], F32))
            gcap_g = _st.enter_context(nc.sbuf_tensor("m_gcap", [128, 32], F32))
            run_g = _st.enter_context(nc.sbuf_tensor("m_run", [128, 2], F32))
            mfm_g = _st.enter_context(nc.sbuf_tensor("m_mfm", [8, TT], F32))
            mbc_g = _st.enter_context(nc.sbuf_tensor("m_mbc", [128, TT], F32))
            ost_g = _st.enter_context(nc.sbuf_tensor("m_ost", [128, 2 * TT], F32))
            msq_g = _st.enter_context(nc.sbuf_tensor("m_sq", [128, DC * TT], F32))
            mmean_g = _st.enter_context(nc.sbuf_tensor("m_mean", [1, TT], F32))
            mvarp_g = _st.enter_context(nc.sbuf_tensor("m_varp", [1, TT], F32))
            mt1_g = _st.enter_context(nc.sbuf_tensor("m_t1", [1, TT], F32))
            minv_g = _st.enter_context(nc.sbuf_tensor("m_inv", [1, TT], F32))
            selsb_g = _st.enter_context(nc.sbuf_tensor("m_selsb", [8, 8 * 128], F32))
            id8_g = _st.enter_context(nc.sbuf_tensor("m_id8", [8, 8], F32))
            id128_g = _st.enter_context(nc.sbuf_tensor("m_id128", [128, 128], F32))
            wg_g = _st.enter_context(nc.sbuf_tensor("m_wg", [128, 64], F32))
            ucum_g = _st.enter_context(nc.sbuf_tensor("m_ucum", [8, 8], F32))
            ls_g = _st.enter_context(nc.psum_tensor("m_ls12", [33, TT], F32))
            gbc_g = _st.enter_context(nc.psum_tensor("m_gbc", [128, 512], F32))
            hp0_g = _st.enter_context(nc.psum_tensor("m_hps0", [128, 512], F32))
            hp1_g = _st.enter_context(nc.psum_tensor("m_hps1", [128, 512], F32))
            y0_g = _st.enter_context(nc.psum_tensor("m_y0", [128, 512], F32))
            y1_g = _st.enter_context(nc.psum_tensor("m_y1", [128, 512], F32))
            y2_g = _st.enter_context(nc.psum_tensor("m_y2", [128, 512], F32))
            y3_g = _st.enter_context(nc.psum_tensor("m_y3", [128, 512], F32))
            x3sb = x3_g.ap()
            x3ln = x3ln_g.ap()
            xesb = xe_g.ap()
            hsb = h_g.ap()
            ysb = ysb_g.ap()
            w1st = w1st_g.ap()
            w2st = w2st_g.ap()
            lg = lg_g.ap()
            mtk = mtk_g.ap()
            mtk2 = mtk2_g.ap()
            gcap = gcap_g.ap()
            runb = run_g.ap()
            mfm = mfm_g.ap()
            mbcsb = mbc_g.ap()
            ost = ost_g.ap()
            msq = msq_g.ap()
            mean, varp, t1v, inv = (mmean_g.ap(), mvarp_g.ap(), mt1_g.ap(),
                                    minv_g.ap())
            selsb = selsb_g.ap()
            id8 = id8_g.ap()
            id128 = id128_g.ap()
            wg_sb = wg_g.ap()
            ucum = ucum_g.ap()
            ls12 = ls_g.ap()
            gbc = gbc_g.ap()
            hps = [hp0_g.ap(), hp1_g.ap()]
            ypsl = [y0_g.ap(), y1_g.ap(), y2_g.ap(), y3_g.ap()]

            prol = C("mprol")
            with nc.Block() as blk:
                @blk.sync
                def _(sy):
                    prol.inc(sy.dma_start(selsb[:, :], sel_c[:, :]), 16)
                    prol.inc(sy.dma_start(id8[:, :], id8_c[:, :]), 16)
                    prol.inc(sy.dma_start(id128[:, :], id128_c[:, :]), 16)
                    prol.inc(sy.dma_start(ucum[:, :], ucum_c[:, :]), 16)
                    for dc in range(DC):
                        prol.inc(sy.dma_start(wg_sb[:, dc * 8:(dc + 1) * 8],
                                              wg_d[dc * 128:(dc + 1) * 128, :]), 16)
                    w(sy, prol)
            bar()

            for tt in range(4):
                with nc.Block() as blk:
                    @blk.sync
                    def _(sy):
                        if tt >= 1:
                            sEV.w(sy, tt * 8)
                            sXE.w(sy, tt * 8)
                        sXL.inc(sy.dma_start(
                            bass.AP(x3sb.tensor, x3sb.offset,
                                    [[DC * TT, 128], [TT, DC], [1, TT]]),
                            bass.AP(x3_dram.tensor,
                                    x3_dram.offset + tt * TT,
                                    [[T, 128], [128 * T, DC], [1, TT]])), 16)

                    @blk.scalar
                    def _(sc):
                        sXL.w(sc, 16 * (tt + 1))
                        for dc in range(DC):
                            i = tt * DC + dc
                            sLSQ.inc(sc.activation(
                                msq[:, dc * TT:(dc + 1) * TT],
                                x3sb[:, dc * TT:(dc + 1) * TT], AF.Square))
                        sLVA.w(sc, tt + 1)
                        sLRT.inc(sc.activation(t1v, varp, AF.Sqrt))

                    @blk.tensor
                    def _(t):
                        sXL.w(t, 16 * (tt + 1))
                        for dc in range(DC):
                            i = tt * DC + dc
                            t.matmul(ls12[0:1, :], onesc[:, :],
                                     x3sb[:, dc * TT:(dc + 1) * TT],
                                     start=(dc == 0), stop=(dc == DC - 1))
                            sLSQ.w(t, i + 1)
                            ins = t.matmul(ls12[32:33, :], onesc[:, :],
                                           msq[:, dc * TT:(dc + 1) * TT],
                                           start=(dc == 0), stop=(dc == DC - 1))
                        sLSM.inc(ins)
                        sLST.w(t, tt + 1)
                        t.matmul(gbc[:, :], ones1[:, :], inv, start=True, stop=True)
                        sLBC.inc(t.matmul(hps[0][:, :], ones1[:, :], mean,
                                          start=True, stop=True))

                    @blk.vector
                    def _(v):
                        sLSM.w(v, tt + 1)
                        v.tensor_single_scalar(mean, ls12[0:1, :], 1.0 / 1024.0,
                                               ALU.mult)
                        v.tensor_single_scalar(varp, ls12[32:33, :], 1.0 / 1024.0,
                                               ALU.mult)
                        v.tensor_tensor(t1v, mean, mean, ALU.mult)
                        v.tensor_tensor(varp, varp, t1v, ALU.subtract)
                        sLVA.inc(v.tensor_single_scalar(varp, varp, EPS, ALU.add))
                        sLRT.w(v, tt + 1)
                        v.reciprocal(inv, t1v)
                        for _i in range(2):
                            v.tensor_tensor(t1v, inv, inv, ALU.mult)
                            v.tensor_tensor(t1v, t1v, varp, ALU.mult)
                            v.tensor_scalar(t1v, t1v, -0.5, 1.5, ALU.mult, ALU.add)
                            v.tensor_tensor(inv, inv, t1v, ALU.mult)
                        sLST.inc(v.tensor_tensor(mean, mean, inv, ALU.mult))
                        sLBC.w(v, tt + 1)
                        ins = None
                        for dc in range(DC):
                            sl = slice(dc * TT, (dc + 1) * TT)
                            v.tensor_tensor(x3ln[:, sl], x3sb[:, sl],
                                            gbc[:, :], ALU.mult)
                            ins = v.tensor_tensor(x3ln[:, sl], x3ln[:, sl],
                                                  hps[0][:, :], ALU.subtract)
                        sLAP.inc(ins)

                bar()
                gm0, gb0, tr0, mk0, t20, mf0 = (sGM.rval, sGB.rval, sTR.rval,
                                                sMK.rval, sT2.rval, sMF.rval)
                cu0, fw0 = sCU.rval, sFW.rval
                with nc.Block() as blk:
                    @blk.tensor
                    def _(t):
                        ins = None
                        for dc in range(DC):
                            ins = t.matmul(gbc[0:8, :], wg_sb[:, dc * 8:(dc + 1) * 8],
                                           x3ln[:, dc * TT:(dc + 1) * TT],
                                           start=(dc == 0), stop=(dc == DC - 1))
                        sGM.inc(ins)
                        for q in range(4):
                            sGB.w(t, gb0 + 1)
                            if q >= 1:
                                sFW.w(t, fw0 + q)
                                sMK.w(t, mk0 + 3 * q)
                            sTR.inc(t.transpose(gbc[0:128, 0:8],
                                                lg[:, q * 128:(q + 1) * 128],
                                                id8[:, :]))
                            sMK.w(t, mk0 + 3 * q + 3)
                            sT2.inc(t.transpose(gbc[0:8, 64:192],
                                                mtk[:, :], id128[:, :]))
                            sMF.w(t, mf0 + q + 1)
                            sCU.inc(t.matmul(gbc[0:8, 192:320], ucum[:, :],
                                             mfm[:, q * 128:(q + 1) * 128],
                                             start=True, stop=True))

                    @blk.scalar
                    def _(sc):
                        sGM.w(sc, gm0 + 1)
                        sGB.inc(sc.activation(lg[:, :], gbc[0:8, :], AF.Identity,
                                              bias=bg_sb[:, :]))
                        for q in range(4):
                            sT2.w(sc, t20 + q + 1)
                            sMF.inc(sc.copy(mfm[:, q * 128:(q + 1) * 128],
                                            gbc[0:8, 64:192]))

                    @blk.vector
                    def _(v):
                        for q in range(4):
                            sTR.w(v, tr0 + q + 1)
                            sMK.inc(v.tensor_copy(mtk2[:, :], gbc[0:128, 0:8]))
                            sMK.w(v, mk0 + 3 * q + 1)
                            sMK.inc(v.reduce_max(runb[:, 0:1], mtk2[:, :],
                                                 mybir.AxisListType.X))
                            sMK.w(v, mk0 + 3 * q + 2)
                            sMK.inc(v.tensor_single_scalar(mtk[:, :], mtk2[:, :],
                                                           runb[:, 0:1], ALU.is_ge))
                            sCU.w(v, cu0 + q + 1)
                            sFW.inc(v.scalar_tensor_tensor(
                                mfm[:, q * 128:(q + 1) * 128],
                                gbc[0:8, 192:320], 0.0,
                                mfm[:, q * 128:(q + 1) * 128],
                                ALU.is_equal, ALU.mult))
                            if "d_gdbg" in dump_aps and tt == 0 and q == 0:
                                v.tensor_copy(gcap[:, 0:8], mtk2[:, :])
                                v.tensor_copy(gcap[:, 8:9], runb[:, 0:1])
                                v.tensor_copy(gcap[:, 9:17], mtk[:, :])
                bar()
                if "d_mask" in dump_aps:
                    sD = C(f"dmask{tt}")
                    with nc.Block() as blk:
                        @blk.gpsimd
                        def _(g):
                            sD.inc(g.dma_start(
                                dump_aps["d_mask"][:, tt * TT:(tt + 1) * TT],
                                mfm[:, :]), 16)
                            w(g, sD)
                    bar()
                hm0, ge0, mb0, xe0 = sHM.rval, sGE.rval, sMB.rval, sXE.val
                w2t0, ye0 = sW2T.rval, sYE.rval
                w1s0 = [s.rval for s in sW1s]
                w2s0 = [s.rval for s in sW2s]
                with nc.Block() as blk:
                    @blk.sync
                    def _(sy):
                        for e in range(8):
                            for hc in range(32):
                                i = e * 32 + hc
                                if i >= 4:
                                    sHM.w(sy, hm0 + i - 3)
                                src = bass.AP(
                                    w1_d.tensor,
                                    w1_d.offset + e * D * 4096 + hc * 128,
                                    [[4096, 128], [4096 * 128, 8], [1, 128]])
                                sW1s[i % 4].inc(sy.dma_start(
                                    bass.AP(w1st.tensor,
                                            w1st.offset + (i % 4) * 1024,
                                            [[4 * 1024, 128], [128, 8], [1, 128]]),
                                    src), 16)

                    @blk.scalar
                    def _(sc):
                        for e in range(8):
                            for hc in range(32):
                                i = e * 32 + hc
                                sHM.w(sc, hm0 + i + 1)
                                sGE.inc(sc.activation(
                                    hsb[:, hc * TT:(hc + 1) * TT],
                                    hps[i % 2][:, :], AF.Gelu_apprx_tanh))

                    @blk.gpsimd
                    def _(g):
                        for e in range(8):
                            for half in range(2):
                                for hc in range(32):
                                    ii = (2 * e + half) * 32 + hc
                                    if ii >= 4:
                                        sW2T.w(g, w2t0 + ii - 3)
                                    src = bass.AP(
                                        w2_d.tensor,
                                        w2_d.offset + (e * 4096 + hc * 128) * D
                                        + half * 512,
                                        [[D, 128], [1, 512]])
                                    sW2s[ii % 4].inc(g.dma_start(
                                        w2st[:, (ii % 4) * 512:(ii % 4 + 1) * 512],
                                        src), 16)

                    @blk.tensor
                    def _(t):
                        def w2_half(e, half):
                            if 2 * e + half >= 1:
                                sYE.w(t, ye0 + 2 * e + half)
                            for hc in range(32):
                                ii = (2 * e + half) * 32 + hc
                                sW2s[ii % 4].w(t, w2s0[ii % 4] + 16 * (ii // 4 + 1))
                                sGE.w(t, ge0 + e * 32 + hc + 1)
                                ins = None
                                for di in range(4):
                                    ins = t.matmul(
                                        ypsl[di][:, :],
                                        w2st[:, (ii % 4) * 512 + di * 128:
                                             (ii % 4) * 512 + (di + 1) * 128],
                                        hsb[:, hc * TT:(hc + 1) * TT],
                                        start=(hc == 0), stop=(hc == 31))
                                sW2T.inc(ins)

                        for e in range(8):
                            sFW.w(t, fw0 + 4)
                            sLAP.w(t, tt + 1)
                            if e == 0:
                                sMB.inc(t.matmul(gbc[0:128, :],
                                                 selsb[:, 0:128],
                                                 mfm[:, :], start=True, stop=True))
                            for hc in range(32):
                                i = e * 32 + hc
                                sW1s[i % 4].w(t, w1s0[i % 4] + 16 * (i // 4 + 1))
                                sXE.w(t, xe0 + e + 1)
                                if i >= 2:
                                    sGE.w(t, ge0 + i - 1)
                                ins = None
                                for dc in range(DC):
                                    ins = t.matmul(
                                        hps[i % 2][:, :],
                                        w1st[:, (i % 4) * 1024 + dc * 128:
                                             (i % 4) * 1024 + (dc + 1) * 128],
                                        xesb[:, dc * TT:(dc + 1) * TT],
                                        start=(dc == 0), stop=(dc == DC - 1))
                                sHM.inc(ins)
                            if e < 7:
                                sXE.w(t, xe0 + e + 1)
                                sMB.inc(t.matmul(gbc[0:128, :],
                                                 selsb[:, (e + 1) * 128:(e + 2) * 128],
                                                 mfm[:, :], start=True, stop=True))
                            w2_half(e, 0)
                            w2_half(e, 1)

                    @blk.vector
                    def _(v):
                        def vcopy_xe(e):
                            sMB.w(v, mb0 + e + 1)
                            v.tensor_copy(mbcsb[:, :], gbc[0:128, :])
                            ins = None
                            for dc in range(DC):
                                ins = v.tensor_tensor(
                                    xesb[:, dc * TT:(dc + 1) * TT],
                                    x3ln[:, dc * TT:(dc + 1) * TT],
                                    mbcsb[:, :], ALU.mult)
                            sXE.inc(ins)

                        def vevac(e, half):
                            sW2T.w(v, w2t0 + 32 * (2 * e + half + 1))
                            ins = None
                            for di in range(4):
                                dcw = half * 4 + di
                                if e == 0:
                                    ins = v.tensor_copy(
                                        ysb[:, dcw * TT:(dcw + 1) * TT],
                                        ypsl[di][:, :])
                                else:
                                    ins = v.tensor_tensor(
                                        ysb[:, dcw * TT:(dcw + 1) * TT],
                                        ysb[:, dcw * TT:(dcw + 1) * TT],
                                        ypsl[di][:, :], ALU.add)
                            sYE.inc(ins)

                        vcopy_xe(0)
                        for e in range(8):
                            if e < 7:
                                vcopy_xe(e + 1)
                            vevac(e, 0)
                            vevac(e, 1)
                bar()
                with nc.Block() as blk:
                    @blk.vector
                    def _(v):
                        for dc in range(DC):
                            j = tt * 8 + dc
                            if j >= 2:
                                sOSs2[j % 2].w(v, 16 * (j // 2))
                            sEV.inc(v.tensor_tensor(
                                ost[:, (j % 2) * TT:(j % 2 + 1) * TT],
                                ysb[:, dc * TT:(dc + 1) * TT],
                                x3sb[:, dc * TT:(dc + 1) * TT], ALU.add))

                    @blk.gpsimd
                    def _(g):
                        for dc in range(DC):
                            j = tt * 8 + dc
                            sEV.w(g, j + 1)
                            sOSs2[j % 2].inc(g.dma_start(
                                bass.AP(out_d.tensor,
                                        out_d.offset + dc * 128 * T + tt * TT,
                                        [[T, 128], [1, TT]]),
                                ost[:, (j % 2) * TT:(j % 2 + 1) * TT]), 16)
                bar()
            if "d_gdbg" in dump_aps:
                sDg = C("dmpg")
                with nc.Block() as blk:
                    @blk.gpsimd
                    def _(g):
                        sDg.inc(g.dma_start(dump_aps["d_gdbg"][:, :], gcap[:, :]), 16)
                        w(g, sDg)
                bar()
            with nc.Block() as blk:
                @blk.gpsimd
                def _(g):
                    w(g, sOSs2[0])
                    w(g, sOSs2[1])
            bar()

    if "sm0" in dumps:
        dump_smalls()
    emit_ln_stream(x_in, xln_dram)
    if "sm1" in dumps:
        dump_smalls()
    dump_rows("d_xln", xln_dram, D)
    with (
        nc.sbuf_tensor("vsb0", [128, 16 * 1040], F32) as vsb0_g,
        nc.sbuf_tensor("ksb0", [128, DC * T], F32) as ksb0_g,
    ):
        vsb0, ksb0 = vsb0_g.ap(), ksb0_g.ap()
        if upto >= 2:
            emit_v(vsb0,
                   lambda sp, dc: w_qkv[dc * 128:(dc + 1) * 128,
                                        3072 + sp * 256:3072 + (sp + 1) * 256],
                   None)
            chunks0 = [(m, "q") for m in range(16)] + [(m, "k") for m in range(8)]

            def wsrc0(mch, kind):
                c0 = mch * 128 if kind == "q" else 2048 + mch * 128
                return bass.AP(w_qkv.tensor, w_qkv.offset + c0,
                               [[4096, 128], [128 * 4096, 8], [1, 128]])

            emit_qk(ksb0, chunks0, wsrc0, lambda mch, kind: None, q0_dram)
            if "sm2" in dumps:
                dump_smalls()
            if "d_v" in dump_aps:
                sD = C("dmpg")
                with nc.Block() as blk:
                    @blk.gpsimd
                    def _(g):
                        sD.inc(g.dma_start(dump_aps["d_v"][:, :], vsb0[:, :]), 16)
                        w(g, sD)
                bar()
            if "d_k" in dump_aps:
                sD = C("dmpg")
                with nc.Block() as blk:
                    @blk.gpsimd
                    def _(g):
                        for dc in range(DC):
                            sD.inc(g.dma_start(
                                dump_aps["d_k"][dc * 128:(dc + 1) * 128, :],
                                ksb0[:, dc * T:(dc + 1) * T]), 16)
                        w(g, sD)
                bar()
            dump_rows("d_q0", q0_dram, 2048)
        if upto >= 3:
            emit_attention(0, ksb0, vsb0, q0_dram, bm0, ctx0_dram)
            if "sm3" in dumps:
                dump_smalls()
            dump_rows("d_ctx0", ctx0_dram, 2048)
    if upto >= 4:
        emit_wout0()
        if "sm4" in dumps:
            dump_smalls()
        dump_rows("d_x1", x1_dram, D)
    if upto >= 5:
        emit_ln_stream(x1_dram, xln_dram)
        dump_rows("d_xln2", xln_dram, D)
        emit_cb()
        dump_rows("d_x2", x2_dram, D)
    if upto >= 6:
        emit_ln_stream(x2_dram, xln_dram)
        with (
            nc.sbuf_tensor("vsb1", [128, 16 * 1040], F32) as vsb1_g,
            nc.sbuf_tensor("ksb1", [128, DC * T], F32) as ksb1_g,
            nc.sbuf_tensor("bvbc", [128, D], F32) as bvbc_g,
        ):
            vsb1, ksb1, bvbc = vsb1_g.ap(), ksb1_g.ap(), bvbc_g.ap()
            emit_bvbc(bvbc)
            emit_v(vsb1,
                   lambda sp, dc: wvT[dc * 128:(dc + 1) * 128,
                                      sp * 256:(sp + 1) * 256],
                   bvbc)
            chunks1 = [(m, "q") for m in range(8)] + [(m, "k") for m in range(8)]

            def wsrc1(mch, kind):
                wsrc = wqT if kind == "q" else wkT
                return bass.AP(wsrc.tensor, wsrc.offset + mch * 128,
                               [[D, 128], [128 * D, 8], [1, 128]])

            def bias1(mch, kind):
                return (bq_sb if kind == "q" else bk_sb)[:, mch:mch + 1]

            emit_qk(ksb1, chunks1, wsrc1, bias1, q1_dram)
            dump_rows("d_q1", q1_dram, D)
            emit_band1()
            dump_rows("d_bm1", bm1_dram, 128)
            if upto >= 7:
                emit_attention(1, ksb1, vsb1, q1_dram, bm1_dram, ctx1_dram)
                dump_rows("d_ctx1", ctx1_dram, D)
    if upto >= 8:
        sA2 = C("a2a")
        with nc.Block() as blk:
            @blk.gpsimd
            def _(g):
                ins = g.collective_compute(
                    "AllGather", ALU.bypass, replica_groups=[list(range(8))],
                    ins=[ctx1_dram[:, :]], outs=[agctx[:, :]])
                sA2.inc(ins, 1)
                sA2.w(g, 1)
        bar()
        nc.all_core_barrier()
        bar()
        dump_rows("d_a2a", agctx, 2 * D)
    if upto >= 9:
        emit_scramble_proj()
        dump_rows("d_x3", x3_dram, D)
    if upto >= 10:
        emit_moe()
    bar()
    return nc


def _host_inputs(inputs):
    f32 = np.float32
    c = np.ascontiguousarray
    x = np.asarray(inputs["L0_attn_inputs"], f32)
    tb = np.asarray(inputs["L1_attn_time_bias"], f32).reshape(500)
    idx = np.clip(np.arange(4095) - 2047 + 250, 0, 499)
    band1 = tb[idx]
    band1_rev = c(band1[::-1])
    common = {
        "w_qkv": c(np.asarray(inputs["L0_attn_W_qkv"], f32)),
        "w_out": c(np.asarray(inputs["L0_attn_W_out"], f32)),
        "cbt": c(np.asarray(inputs["L0_ffn_CB"], f32).T),
        "scb": c(np.asarray(inputs["L0_ffn_SCB"], f32)),
        "cbias": c(np.asarray(inputs["L0_ffn_bias"], f32)),
        "wqT": c(np.asarray(inputs["L1_attn_Wq"], f32).T),
        "wkT": c(np.asarray(inputs["L1_attn_Wk"], f32).T),
        "wvT": c(np.asarray(inputs["L1_attn_Wv"], f32).T),
        "woT": c(np.asarray(inputs["L1_attn_Wo"], f32).T),
        "bq": c(np.asarray(inputs["L1_attn_bq"], f32)),
        "bk": c(np.asarray(inputs["L1_attn_bk"], f32)),
        "bv": c(np.asarray(inputs["L1_attn_bv"], f32)),
        "bo": c(np.asarray(inputs["L1_attn_bo"], f32)),
        "etg": c(np.asarray(inputs["L1_attn_et_gate"], f32).reshape(1, 1)),
        "isg": c(np.asarray(inputs["L1_attn_is_gate"], f32).reshape(1, 1)),
        "bandrev1": band1_rev,
        "w1": c(np.asarray(inputs["L1_ffn_W1"], f32)),
        "w2": c(np.asarray(inputs["L1_ffn_W2"], f32)),
        "wg": c(np.asarray(inputs["L1_ffn_Wg"], f32)),
        "bg": c(np.asarray(inputs["L1_ffn_bg"], f32)),
    }
    in_maps = []
    for j in range(8):
        m = dict(common)
        m["x_fm"] = c(x[j].T)
        in_maps.append(m)
    return in_maps


def kernel(**inputs):
    in_maps = _host_inputs(inputs)
    nc = build()
    res = run_bass_kernel_spmd(nc, in_maps, core_ids=list(range(8)))
    out = np.empty((8, T, D), np.float32)
    for j in range(8):
        out[j] = np.asarray(res.results[j]["out_fm"]).T
    return out


# version 05
# speedup vs baseline: 1.2617x; 1.0178x over previous
import math
import numpy as np
import concourse.bass as bass
import concourse.mybir as mybir
from concourse.bass_utils import run_bass_kernel_spmd

F32 = mybir.dt.float32
AF = mybir.ActivationFunctionType
ALU = mybir.AluOpType

D = 1024
T = 2048
DC = 8
QT = 256
NQT = T // QT
BW = 3968
BC0 = 1920
NEG = -1.0e30
EPS = 1e-5


def _alibi_slopes32():
    base = 2.0 ** (-(2.0 ** (-(math.log2(32) - 3))))
    powers = np.arange(1, 33, dtype=np.float32)
    return (np.float32(base) ** powers).astype(np.float32)


def _bandmat_l0():
    band = np.arange(4095, dtype=np.float64) - 2047.0
    band[band > 0] = NEG
    p = np.arange(128)[:, None]
    c = np.arange(BW)[None, :]
    idx = p - c + (BW - 1)
    ok = (idx >= 0) & (idx <= 4094)
    return (band[np.clip(idx, 0, 4094)] * ok).astype(np.float32)


class Cnt:
    def __init__(self, sem):
        self.sem = sem
        self.val = 0

    def inc(self, ins, n=1):
        ins.then_inc(self.sem, n)
        self.val += n
        return self.val


class Rel:

    def __init__(self, cnt):
        self.c = cnt
        self.b = cnt.val

    def inc(self, ins, n=1):
        return self.c.inc(ins, n)

    def w(self, eng, v):
        if self.b + v > 0:
            eng.wait_ge(self.c.sem, self.b + v)

    @property
    def val(self):
        return self.c.val

    @property
    def rval(self):
        return self.c.val - self.b

    @property
    def sem(self):
        return self.c.sem


def build(upto=99, dumps=(), race=False):
    nc = bass.Bass("TRN2", target_bir_lowering=False,
                   detect_race_conditions=race, num_devices=8)
    dumps = set(dumps)

    x_in = nc.dram_tensor("x_fm", [D, T], F32, kind="ExternalInput").ap()
    w_qkv = nc.dram_tensor("w_qkv", [D, 4096], F32, kind="ExternalInput").ap()
    w_out = nc.dram_tensor("w_out", [2048, D], F32, kind="ExternalInput").ap()
    cbt = nc.dram_tensor("cbt", [D, D], F32, kind="ExternalInput").ap()
    scb_d = nc.dram_tensor("scb", [D], F32, kind="ExternalInput").ap()
    cbias_d = nc.dram_tensor("cbias", [D], F32, kind="ExternalInput").ap()
    wqT = nc.dram_tensor("wqT", [D, D], F32, kind="ExternalInput").ap()
    wkT = nc.dram_tensor("wkT", [D, D], F32, kind="ExternalInput").ap()
    wvT = nc.dram_tensor("wvT", [D, D], F32, kind="ExternalInput").ap()
    woT = nc.dram_tensor("woT", [D, D], F32, kind="ExternalInput").ap()
    bq_d = nc.dram_tensor("bq", [D], F32, kind="ExternalInput").ap()
    bk_d = nc.dram_tensor("bk", [D], F32, kind="ExternalInput").ap()
    bv_d = nc.dram_tensor("bv", [D], F32, kind="ExternalInput").ap()
    bo_d = nc.dram_tensor("bo", [D], F32, kind="ExternalInput").ap()
    etg = nc.dram_tensor("etg", [1, 1], F32, kind="ExternalInput").ap()
    isg = nc.dram_tensor("isg", [1, 1], F32, kind="ExternalInput").ap()
    bandrev1 = nc.dram_tensor("bandrev1", [4095], F32, kind="ExternalInput").ap()
    w1_d = nc.dram_tensor("w1", [8, D, 4096], F32, kind="ExternalInput").ap()
    w2_d = nc.dram_tensor("w2", [8, 4096, D], F32, kind="ExternalInput").ap()
    wg_d = nc.dram_tensor("wg", [D, 8], F32, kind="ExternalInput").ap()
    bg_d = nc.dram_tensor("bg", [8], F32, kind="ExternalInput").ap()
    out_d = nc.dram_tensor("out_fm", [D, T], F32, kind="ExternalOutput").ap()

    q0_dram = nc.dram_tensor("q0_scr", [2048, T], F32).ap()
    q1_dram = nc.dram_tensor("q1_scr", [D, T], F32).ap()
    xln_dram = nc.dram_tensor("xln_scr", [D, T], F32).ap()
    x1_dram = nc.dram_tensor("x1_scr", [D, T], F32).ap()
    x2_dram = nc.dram_tensor("x2_scr", [D, T], F32).ap()
    x3_dram = nc.dram_tensor("x3_scr", [D, T], F32).ap()
    ctx0_dram = nc.dram_tensor("ctx0_scr", [2048, T], F32).ap()
    ctx1_dram = nc.dram_tensor("ctx1_scr", [D, T], F32).ap()
    agctx = nc.dram_tensor("agctx", [8 * D, T], F32).ap()
    bm1_dram = nc.dram_tensor("bm1_scr", [128, BW], F32).ap()
    bdev_dram = nc.dram_tensor("bdev_scr", [4095], F32).ap()

    bm0 = nc.inline_tensor(_bandmat_l0(), name="bm0").ap()
    selmat = np.zeros((8, 8 * 128), np.float32)
    for e in range(8):
        selmat[e, e * 128:(e + 1) * 128] = 1.0
    sel_c = nc.inline_tensor(selmat, name="selc").ap()
    id8_c = nc.inline_tensor(np.eye(8, dtype=np.float32), name="id8c").ap()
    id128_c = nc.inline_tensor(np.eye(128, dtype=np.float32), name="id128c").ap()
    ucum_np = np.triu(np.ones((8, 8), np.float32), 1).T
    ucum_c = nc.inline_tensor(np.ascontiguousarray(ucum_np.T), name="ucumc").ap()
    slopes8 = [float(s) * 8.0 for s in _alibi_slopes32()]

    dump_aps = {}
    for name, shape in [("d_xln", [D, T]), ("d_k", [D, T]), ("d_v", [128, 16 * 1040]),
                        ("d_q0", [2048, T]), ("d_ctx0", [2048, T]), ("d_x1", [D, T]),
                        ("d_x2", [D, T]), ("d_ctx1", [D, T]), ("d_a2a", [2 * D, T]),
                        ("d_x3", [D, T]), ("d_bm1", [128, BW]), ("d_q1", [D, T]),
                        ("d_mask", [8, T]), ("d_xln2", [D, T]), ("d_scbq", [128, 16]), ("d_csrc", [128, 8 * T]), ("d_gdbg", [128, 32])]:
        if name in dumps:
            dump_aps[name] = nc.dram_tensor(name, shape, F32, kind="ExternalOutput").ap()

    onesc = nc.alloc_sbuf_tensor("onesc", [128, 1], F32).ap()
    ones1 = nc.alloc_sbuf_tensor("ones1", [1, 128], F32).ap()
    scb_sb = nc.alloc_sbuf_tensor("scb_sb", [128, 8], F32).ap()
    cbias_sb = nc.alloc_sbuf_tensor("cbias_sb", [128, 8], F32).ap()
    bq_sb = nc.alloc_sbuf_tensor("bq_sb", [128, 8], F32).ap()
    bk_sb = nc.alloc_sbuf_tensor("bk_sb", [128, 8], F32).ap()
    bo_sb = nc.alloc_sbuf_tensor("bo_sb", [128, 8], F32).ap()
    bg_sb = nc.alloc_sbuf_tensor("bg_sb", [8, 1], F32).ap()
    r127 = nc.alloc_sbuf_tensor("r127", [128, 1], F32).ap()
    r127b = nc.alloc_sbuf_tensor("r127b", [128, 1], F32).ap()
    scbq_sb = nc.alloc_sbuf_tensor("scbq_sb", [128, 8], F32).ap()
    onespad_c = nc.alloc_sbuf_tensor("onespad", [128, 64], F32).ap()

    sems = {}

    def C(name):
        if name not in sems:
            sems[name] = Cnt(nc.alloc_semaphore(name))
        return Rel(sems[name])

    def bar():
        nc.all_engine_barrier()

    def w(eng, cnt):
        if cnt.val > 0:
            eng.wait_ge(cnt.sem, cnt.val)

    def dump_rows(name, dram_src, nrow):
        if name not in dump_aps:
            return
        sD = C("dmp")
        with nc.Block() as blk:
            @blk.scalar
            def _(sc):
                for r in range(nrow // 128):
                    sD.inc(sc.dma_start(dump_aps[name][r * 128:(r + 1) * 128, :],
                                        dram_src[r * 128:(r + 1) * 128, :]), 16)
                w(sc, sD)
        bar()

    sP0 = C("p0ld")
    sP0g = C("p0g")
    with nc.Block() as blk:
        @blk.sync
        def _(sy):
            for dst, src in [(scb_sb, scb_d), (cbias_sb, cbias_d), (bq_sb, bq_d),
                             (bk_sb, bk_d), (bo_sb, bo_d)]:
                for dc in range(DC):
                    sP0.inc(sy.dma_start(dst[:, dc:dc + 1],
                                         src[dc * 128:(dc + 1) * 128]), 16)
            sP0.inc(sy.dma_start(bg_sb[:, :], bg_d[:]), 16)
            w(sy, sP0)

        @blk.gpsimd
        def _(g):
            g.memset(onesc[:, :], 1.0)
            g.memset(ones1[:, :], 1.0)
            g.memset(onespad_c[:, :], 0.0)
            g.memset(onespad_c[0:1, :], 1.0)
            sP0g.inc(g.memset(r127[:, :], 127.0))

        @blk.vector
        def _(v):
            sP0.w(v, sP0.val)
            sP0g.w(v, 1)
            sP0g.inc(v.reciprocal(r127b[:, :], r127[:, :]))
            sP0g.w(v, 2)
            v.tensor_single_scalar(scbq_sb[:, :], scb_sb[:, :], r127b[:, 0:1], ALU.mult)
    bar()

    def dump_smalls():
        if "d_scbq" not in dump_aps:
            return
        sDx = C("dmpg")
        with nc.Block() as blk:
            @blk.gpsimd
            def _(g):
                sDx.inc(g.dma_start(dump_aps["d_scbq"][:, 0:8], scbq_sb[:, :]), 16)
                sDx.inc(g.dma_start(dump_aps["d_scbq"][:, 8:12], cbias_sb[:, 0:4]), 16)
                sDx.inc(g.dma_start(dump_aps["d_scbq"][:, 12:13], r127[:, :],
                                    allow_slow_non_contiguous=True), 16)
                sDx.inc(g.dma_start(dump_aps["d_scbq"][:, 13:14], r127b[:, :],
                                    allow_slow_non_contiguous=True), 16)
                sDx.inc(g.dma_start(dump_aps["d_scbq"][:, 14:15], scb_sb[:, 0:1],
                                    allow_slow_non_contiguous=True), 16)
                w(g, sDx)
        bar()

    lnid = [0]

    def emit_ln_stream(src_dram, dst_dram):
        lnid[0] += 1
        W_ = 512
        ntt = 4
        tag = "ln"
        utag = f"ln{lnid[0]}"
        sXL, sSQ, sSUM, sVA, sRT, sST2, sBC, sAPP = (C(tag + s) for s in
            ("xl", "sq", "sum", "va", "rt", "st2", "bc", "app"))
        sOSs = [C(tag + "os0"), C(tag + "os1")]
        with (
            nc.psum_tensor(utag + "_s1", [1, W_], F32) as ps1_g,
            nc.psum_tensor(utag + "_s2", [1, W_], F32) as ps2_g,
            nc.psum_tensor(utag + "_a", [128, W_], F32) as psa_g,
            nc.psum_tensor(utag + "_b", [128, W_], F32) as psb_g,
            nc.sbuf_tensor(utag + "_xh", [128, DC * W_], F32) as xh_g,
            nc.sbuf_tensor(utag + "_sq", [128, DC * W_], F32) as sq_g,
            nc.sbuf_tensor(utag + "_ob", [128, 2 * W_], F32) as ob_g,
            nc.sbuf_tensor(utag + "_mean", [1, W_], F32) as mean_g,
            nc.sbuf_tensor(utag + "_varp", [1, W_], F32) as varp_g,
            nc.sbuf_tensor(utag + "_t1", [1, W_], F32) as t1_g,
            nc.sbuf_tensor(utag + "_inv", [1, W_], F32) as inv_g,
        ):
            ps1, ps2, psa, psb = ps1_g.ap(), ps2_g.ap(), psa_g.ap(), psb_g.ap()
            xh, sq, ob = xh_g.ap(), sq_g.ap(), ob_g.ap()
            mean, varp, t1, inv = mean_g.ap(), varp_g.ap(), t1_g.ap(), inv_g.ap()
            with nc.Block() as blk:
                @blk.sync
                def _(sy):
                    for tt in range(ntt):
                        if tt >= 1:
                            sAPP.w(sy, tt * DC)
                        sXL.inc(sy.dma_start(
                            bass.AP(xh.tensor, xh.offset,
                                    [[DC * W_, 128], [W_, DC], [1, W_]]),
                            bass.AP(src_dram.tensor,
                                    src_dram.offset + tt * W_,
                                    [[T, 128], [128 * T, DC], [1, W_]])), 16)

                @blk.scalar
                def _(sc):
                    for tt in range(ntt):
                        if tt >= 1:
                            sSUM.w(sc, tt)
                        sXL.w(sc, 16 * (tt + 1))
                        for dc in range(DC):
                            sSQ.inc(sc.activation(sq[:, dc * W_:(dc + 1) * W_],
                                                  xh[:, dc * W_:(dc + 1) * W_],
                                                  AF.Square))
                        sVA.w(sc, tt + 1)
                        sRT.inc(sc.activation(t1, varp, AF.Sqrt))

                @blk.tensor
                def _(t):
                    for tt in range(ntt):
                        if tt >= 1:
                            sVA.w(t, tt)
                        sXL.w(t, 16 * (tt + 1))
                        ins = None
                        for dc in range(DC):
                            i = tt * DC + dc
                            t.matmul(ps1[:, :], onesc[:, :],
                                     xh[:, dc * W_:(dc + 1) * W_],
                                     start=(dc == 0), stop=(dc == DC - 1))
                            sSQ.w(t, i + 1)
                            ins = t.matmul(ps2[:, :], onesc[:, :],
                                           sq[:, dc * W_:(dc + 1) * W_],
                                           start=(dc == 0), stop=(dc == DC - 1))
                        sSUM.inc(ins)
                        sST2.w(t, tt + 1)
                        if tt >= 1:
                            sAPP.w(t, tt * DC)
                        t.matmul(psa[:, :], ones1[:, :], inv, start=True, stop=True)
                        sBC.inc(t.matmul(psb[:, :], ones1[:, :], mean,
                                         start=True, stop=True))

                @blk.vector
                def _(v):
                    for tt in range(ntt):
                        sSUM.w(v, tt + 1)
                        v.tensor_single_scalar(mean, ps1[0:1, :], 1.0 / 1024.0, ALU.mult)
                        v.tensor_single_scalar(varp, ps2[0:1, :], 1.0 / 1024.0, ALU.mult)
                        v.tensor_tensor(t1, mean, mean, ALU.mult)
                        v.tensor_tensor(varp, varp, t1, ALU.subtract)
                        sVA.inc(v.tensor_single_scalar(varp, varp, EPS, ALU.add))
                        sRT.w(v, tt + 1)
                        v.reciprocal(inv, t1)
                        for _i in range(2):
                            v.tensor_tensor(t1, inv, inv, ALU.mult)
                            v.tensor_tensor(t1, t1, varp, ALU.mult)
                            v.tensor_scalar(t1, t1, -0.5, 1.5, ALU.mult, ALU.add)
                            v.tensor_tensor(inv, inv, t1, ALU.mult)
                        sST2.inc(v.tensor_tensor(mean, mean, inv, ALU.mult))
                        sBC.w(v, tt + 1)
                        for dc in range(DC):
                            i = tt * DC + dc
                            if i >= 2:
                                sOSs[i % 2].w(v, 16 * (i // 2))
                            v.tensor_tensor(ob[:, (i % 2) * W_:(i % 2 + 1) * W_],
                                            xh[:, dc * W_:(dc + 1) * W_],
                                            psa[:, :], ALU.mult)
                            sAPP.inc(v.tensor_tensor(
                                ob[:, (i % 2) * W_:(i % 2 + 1) * W_],
                                ob[:, (i % 2) * W_:(i % 2 + 1) * W_],
                                psb[:, :], ALU.subtract))

                @blk.gpsimd
                def _(g):
                    for tt in range(ntt):
                        for dc in range(DC):
                            i = tt * DC + dc
                            sAPP.w(g, i + 1)
                            sOSs[i % 2].inc(g.dma_start(
                                bass.AP(dst_dram.tensor,
                                        dst_dram.offset + dc * 128 * T + tt * W_,
                                        [[T, 128], [1, W_]]),
                                ob[:, (i % 2) * W_:(i % 2 + 1) * W_]), 16)
                    w(g, sOSs[0])
                    w(g, sOSs[1])
            bar()

    vpid = [0]

    def emit_v(vsb, wv_src, bias_bc):
        vpid[0] += 1
        tag = "vp"
        utag = f"vp{vpid[0]}"
        sWL, sXT, sM, sE, sZ = (C(tag + s) for s in ("wl", "xt", "mm", "ev", "z"))
        with (
            nc.sbuf_tensor(utag + "_wst", [128, 32 * 256], F32) as wst_g,
            nc.sbuf_tensor(utag + "_xt", [128, DC * 512], F32) as xt_g,
            nc.psum_tensor(utag + "_ps0", [128, 256], F32) as ps0_g,
            nc.psum_tensor(utag + "_ps1", [128, 256], F32) as ps1_g,
        ):
            wst = wst_g.ap()
            xt = xt_g.ap()
            psv = [ps0_g.ap(), ps1_g.ap()]
            with nc.Block() as blk:
                @blk.sync
                def _(sy):
                    for sp in range(4):
                        for dc in range(DC):
                            sWL.inc(sy.dma_start(
                                wst[:, (sp * DC + dc) * 256:(sp * DC + dc + 1) * 256],
                                wv_src(sp, dc)), 16)
                    for tt in range(4):
                        if tt >= 1:
                            sM.w(sy, tt * 16)
                        sXT.inc(sy.dma_start(
                            bass.AP(xt.tensor, xt.offset,
                                    [[DC * 512, 128], [512, DC], [1, 512]]),
                            bass.AP(xln_dram.tensor,
                                    xln_dram.offset + tt * 512,
                                    [[T, 128], [128 * T, DC], [1, 512]])), 16)

                @blk.gpsimd
                def _(g):
                    sZ.inc(g.memset(vsb[:, :], 1.0))

                @blk.tensor
                def _(t):
                    w(t, sWL)
                    j = 0
                    for tt in range(4):
                        sXT.w(t, 16 * (tt + 1))
                        for ksub in range(4):
                            for sp in range(4):
                                if j >= 2:
                                    sE.w(t, j - 1)
                                ins = None
                                for dc in range(DC):
                                    ins = t.matmul(
                                        psv[j % 2][:, :],
                                        xt[:, dc * 512 + ksub * 128:
                                           dc * 512 + (ksub + 1) * 128],
                                        wst[:, (sp * DC + dc) * 256:
                                            (sp * DC + dc + 1) * 256],
                                        start=(dc == 0), stop=(dc == DC - 1))
                                sM.inc(ins)
                                j += 1

                @blk.vector
                def _(v):
                    sZ.w(v, 1)
                    j = 0
                    for tt in range(4):
                        for ksub in range(4):
                            kt = tt * 4 + ksub
                            for sp in range(4):
                                sM.w(v, j + 1)
                                dst = bass.AP(vsb.tensor,
                                              vsb.offset + kt * 1040 + sp * 260,
                                              [[16640, 128], [65, 4], [1, 64]])
                                if bias_bc is None:
                                    sE.inc(v.tensor_copy(dst, psv[j % 2][:, :]))
                                else:
                                    sE.inc(v.tensor_tensor(
                                        dst, psv[j % 2][:, :],
                                        bias_bc[:, sp * 256:(sp + 1) * 256], ALU.add))
                                j += 1
            bar()

    qkid = [0]

    def emit_qk(ksb, chunks, wgroup_src, bias_for, q_dram_ap):
        qkid[0] += 1
        tag = "qk"
        utag = f"qk{qkid[0]}"
        sXT, sM, sE = C(tag + "xt"), C(tag + "mm"), C(tag + "ev")
        sWs = [C(tag + "wl0"), C(tag + "wl1")]
        sQs = [C(tag + "qs0"), C(tag + "qs1")]
        nch = len(chunks)
        with (
            nc.sbuf_tensor(utag + "_wst", [128, 2 * 8 * 128], F32) as wst_g,
            nc.sbuf_tensor(utag + "_xt", [128, DC * 512], F32) as xt_g,
            nc.sbuf_tensor(utag + "_qst", [128, 2 * 512], F32) as qst_g,
            nc.psum_tensor(utag + "_ps0", [128, 512], F32) as ps0_g,
            nc.psum_tensor(utag + "_ps1", [128, 512], F32) as ps1_g,
        ):
            wst, xt, qst = wst_g.ap(), xt_g.ap(), qst_g.ap()
            psq = [ps0_g.ap(), ps1_g.ap()]
            with nc.Block() as blk:
                @blk.sync
                def _(sy):
                    for tt in range(4):
                        if tt >= 1:
                            sM.w(sy, tt * nch)
                        sXT.inc(sy.dma_start(
                            bass.AP(xt.tensor, xt.offset,
                                    [[DC * 512, 128], [512, DC], [1, 512]]),
                            bass.AP(xln_dram.tensor,
                                    xln_dram.offset + tt * 512,
                                    [[T, 128], [128 * T, DC], [1, 512]])), 16)
                        for ci, (mch, kind) in enumerate(chunks):
                            j = tt * nch + ci
                            if j >= 2:
                                sM.w(sy, j - 1)
                            sWs[j % 2].inc(sy.dma_start(
                                bass.AP(wst.tensor,
                                        wst.offset + (j % 2) * 1024,
                                        [[2 * 1024, 128], [128, 8], [1, 128]]),
                                wgroup_src(mch, kind)), 16)

                @blk.tensor
                def _(t):
                    for tt in range(4):
                        sXT.w(t, 16 * (tt + 1))
                        for ci, (mch, kind) in enumerate(chunks):
                            j = tt * nch + ci
                            if j >= 2:
                                sE.w(t, j - 1)
                            sWs[j % 2].w(t, 16 * (j // 2 + 1))
                            ins = None
                            for dc in range(DC):
                                ins = t.matmul(
                                    psq[j % 2][:, :],
                                    wst[:, (j % 2) * 1024 + dc * 128:
                                        (j % 2) * 1024 + (dc + 1) * 128],
                                    xt[:, dc * 512:(dc + 1) * 512],
                                    start=(dc == 0), stop=(dc == DC - 1))
                            sM.inc(ins)

                @blk.scalar
                def _(sc):
                    jq = 0
                    for tt in range(4):
                        for ci, (mch, kind) in enumerate(chunks):
                            j = tt * nch + ci
                            sM.w(sc, j + 1)
                            bias = bias_for(mch, kind)
                            if kind == "k":
                                dst = ksb[:, mch * T + tt * 512: mch * T + (tt + 1) * 512]
                            else:
                                if jq >= 2:
                                    sQs[jq % 2].w(sc, 16 * (jq // 2))
                                dst = qst[:, (jq % 2) * 512:(jq % 2 + 1) * 512]
                            if bias is None:
                                sE.inc(sc.copy(dst, psq[j % 2][:, :]))
                            else:
                                sE.inc(sc.activation(dst, psq[j % 2][:, :],
                                                     AF.Identity, bias=bias))
                            if kind == "q":
                                jq += 1

                @blk.gpsimd
                def _(g):
                    jq = 0
                    for tt in range(4):
                        for ci, (mch, kind) in enumerate(chunks):
                            j = tt * nch + ci
                            if kind != "q":
                                continue
                            sE.w(g, j + 1)
                            sQs[jq % 2].inc(g.dma_start(
                                q_dram_ap[mch * 128:(mch + 1) * 128,
                                          tt * 512:(tt + 1) * 512],
                                qst[:, (jq % 2) * 512:(jq % 2 + 1) * 512]), 16)
                            jq += 1
                    w(g, sQs[0])
                    w(g, sQs[1])
            bar()

    atid = [0]

    def emit_attention(layer, ksb, vsb, q_dram_ap, band_dram, ctxl_dram,
                       ag_out=None, ag_sem=None):
        atid[0] += 1
        tag = "at"
        utag = f"at{atid[0]}"
        NH = 32 if layer == 0 else 16
        LAG = 3
        sBD, sSM, sBA, sEX, sPV, sRC, sBC2, sCX, sCP = (
            C(tag + s) for s in
            ("bd", "sm", "ba", "ex", "pv", "rc", "bc", "cx", "cp"))
        sQLs = [C(tag + "ql0"), C(tag + "ql1"), C(tag + "ql2"), C(tag + "ql3")]
        sCSs = [C(tag + "cs0"), C(tag + "cs1")]
        sZP = C(tag + "zp")
        npair = [(qt + 1) if layer == 0 else 8 for qt in range(NQT)]
        pairs = []
        cum = {}
        tot = 0
        for qt in range(NQT):
            for h in range(NH):
                gi = qt * NH + h
                cum[gi] = tot
                for kp in range(npair[qt]):
                    pairs.append((tot + kp, gi, qt, h, kp))
                tot += npair[qt]
        n_items = tot
        bd_cum = [0]
        for qt in range(NQT):
            bd_cum.append(bd_cum[-1] + 2 * npair[qt])
        jlast = {}
        for (i, gi, qt, h, kp) in pairs:
            if kp == npair[qt] - 1:
                jlast[gi] = i
        NHEADS = NH * NQT
        bcast_after_pv = {}
        recip_after = {}
        evac_after = {}
        for g in range(NHEADS):
            bcast_after_pv.setdefault(min(jlast[g] + 1, n_items - 1), []).append(g)
            recip_after.setdefault(min(jlast[g] + LAG, n_items - 1), []).append(g)
            evac_after.setdefault(min(jlast[g] + LAG + 1, n_items - 1), []).append(g)

        def kv_of(h):
            g = h // 2 if layer == 0 else h
            return g // 2, (g % 2) * 64, 65 * g

        with (
            nc.sbuf_tensor(utag + "_band", [128, 16 * QT], F32) as band_g,
            nc.sbuf_tensor(utag + "_q", [128, 4 * QT], F32) as q_g,
            nc.sbuf_tensor(utag + "_p", [128, 4 * 512], F32) as p_g,
            nc.sbuf_tensor(utag + "_rin", [128, 2 * QT], F32) as rin_g,
            nc.sbuf_tensor(utag + "_cst", [64, 2 * QT], F32) as cst_g,
            nc.sbuf_tensor(utag + "_rbs", [64, 2 * QT], F32) as rbs_g,
            nc.psum_tensor(utag + "_s0", [128, 512], F32) as s0_g,
            nc.psum_tensor(utag + "_s1", [128, 512], F32) as s1_g,
            nc.psum_tensor(utag + "_s2", [128, 512], F32) as s2_g,
            nc.psum_tensor(utag + "_s3", [128, 512], F32) as s3_g,
            nc.psum_tensor(utag + "_c0", [65, QT], F32) as c0_g,
            nc.psum_tensor(utag + "_c1", [65, QT], F32) as c1_g,
            nc.psum_tensor(utag + "_rb", [64, 2 * QT], F32) as rb_g,
        ):
            band = band_g.ap()
            qsb = q_g.ap()
            psb = p_g.ap()
            rin = rin_g.ap()
            cst = cst_g.ap()
            rbs = rbs_g.ap()
            spsum = [s0_g.ap(), s1_g.ap(), s2_g.ap(), s3_g.ap()]
            cpsum = [c0_g.ap(), c1_g.ap()]
            rbps = rb_g.ap()

            with nc.Block() as blk:
                @blk.sync
                def _(sy):
                    sZP.w(sy, 1)
                    for qt in range(NQT):
                        if qt >= 1:
                            sBA.w(sy, NH * (bd_cum[qt] // 2))
                        for kc in range(2 * npair[qt]):
                            c_off = qt * QT - kc * 128 + BC0
                            sBD.inc(sy.dma_start(
                                band[:, kc * QT:(kc + 1) * QT],
                                bass.AP(band_dram.tensor, band_dram.offset + c_off,
                                        [[BW, 128], [1, QT]])), 16)
                        for h in range(NH):
                            gi = qt * NH + h
                            kvoff_h = kv_of(h)[1]
                            if gi >= 4:
                                sSM.w(sy, cum[gi - 3])
                            sQLs[gi % 4].inc(sy.dma_start(
                                bass.AP(qsb.tensor,
                                        qsb.offset + kvoff_h * (4 * QT)
                                        + (gi % 4) * QT,
                                        [[4 * QT, 64], [1, QT]]),
                                q_dram_ap[h * 64:(h + 1) * 64,
                                          qt * QT:(qt + 1) * QT]), 16)

                @blk.tensor
                def _(t):
                    def emit_scores(pr):
                        i, gi, qt, h, kp = pr
                        kvc, kvoff, vcol = kv_of(h)
                        if kp == 0:
                            sQLs[gi % 4].w(t, 16 * (gi // 4 + 1))
                        if i >= 4:
                            sEX.w(t, i - 3)
                        ins = None
                        for half in range(2):
                            kc = kp * 2 + half
                            lhs = bass.AP(ksb.tensor,
                                          ksb.offset + kvc * T + kc * 128,
                                          [[DC * T, 128], [1, 128]])
                            rhsq = bass.AP(qsb.tensor,
                                           qsb.offset + (gi % 4) * QT,
                                           [[4 * QT, 128], [1, QT]])
                            ins = t.matmul(
                                spsum[i % 4][:, half * QT:(half + 1) * QT], lhs,
                                rhsq, start=True, stop=True)
                        sSM.inc(ins)

                    def emit_pv(pr):
                        i, gi, qt, h, kp = pr
                        kvc, kvoff, vcol = kv_of(h)
                        sEX.w(t, i + 1)
                        if kp == 0 and gi >= 2:
                            sCX.w(t, gi - 1)
                        ins = None
                        for half in range(2):
                            kc = kp * 2 + half
                            lhs = bass.AP(vsb.tensor, vsb.offset + kc * 1040 + vcol,
                                          [[16640, 128], [1, 65]])
                            ins = t.matmul(
                                cpsum[gi % 2][:, :], lhs,
                                psb[:, (i % 4) * 512 + half * QT:
                                    (i % 4) * 512 + (half + 1) * QT],
                                start=(kp == 0 and half == 0),
                                stop=(kp == npair[qt] - 1 and half == 1))
                        sPV.inc(ins)
                        for g2 in bcast_after_pv.get(i, []):
                            sRC.w(t, g2 + 1)
                            if g2 >= 2:
                                sCP.w(t, g2 - 1)
                            sBC2.inc(t.matmul(
                                rbps[:, (g2 % 2) * QT:(g2 % 2 + 1) * QT],
                                onespad_c[:, 0:64],
                                rin[:, (g2 % 2) * QT:(g2 % 2 + 1) * QT],
                                start=True, stop=True))

                    for idx, pr in enumerate(pairs):
                        emit_scores(pr)
                        if idx >= LAG:
                            emit_pv(pairs[idx - LAG])
                    for j in range(max(n_items - LAG, 0), n_items):
                        emit_pv(pairs[j])

                @blk.vector
                def _(v):
                    def vhead_recip(g):
                        sPV.w(v, jlast[g] + 1)
                        if g >= 2:
                            sBC2.w(v, g - 1)
                        sRC.inc(v.reciprocal(
                            rin[0:1, (g % 2) * QT:(g % 2 + 1) * QT],
                            cpsum[g % 2][64:65, :]))

                    def vhead_evac(g):
                        sBC2.w(v, g + 1)
                        if g >= 2:
                            sCSs[g % 2].w(v, 16 * (g // 2))
                        sCP.inc(v.tensor_copy(
                            rbs[:, (g % 2) * QT:(g % 2 + 1) * QT],
                            rbps[:, (g % 2) * QT:(g % 2 + 1) * QT]))
                        sCX.inc(v.tensor_tensor(
                            cst[:, (g % 2) * QT:(g % 2 + 1) * QT],
                            cpsum[g % 2][0:64, :],
                            rbs[:, (g % 2) * QT:(g % 2 + 1) * QT], ALU.mult))

                    v.memset(qsb[:, :], 0.0)
                    sZP.inc(v.memset(rin[:, :], 0.0))
                    for idx, pr in enumerate(pairs):
                        i, gi, qt, h, kp = pr
                        if kp == 0 and h == 0:
                            sBD.w(v, 16 * bd_cum[qt + 1])
                        sSM.w(v, i + 1)
                        sps = spsum[i % 4]
                        if layer == 0:
                            ins = v.scalar_tensor_tensor(
                                sps[:, :], band[:, kp * 2 * QT:(kp * 2 + 2) * QT],
                                slopes8[h], sps[:, :], ALU.mult, ALU.add)
                        else:
                            ins = v.tensor_tensor(
                                sps[:, :], sps[:, :],
                                band[:, kp * 2 * QT:(kp * 2 + 2) * QT], ALU.add)
                        sBA.inc(ins)
                        for g2 in recip_after.get(i, []):
                            vhead_recip(g2)
                        for g2 in evac_after.get(i, []):
                            vhead_evac(g2)

                @blk.scalar
                def _(sc):
                    for idx, pr in enumerate(pairs):
                        i, gi, qt, h, kp = pr
                        sBA.w(sc, i + 1)
                        if i >= 4:
                            sPV.w(sc, i - 3)
                        sEX.inc(sc.activation(
                            psb[:, (i % 4) * 512:(i % 4 + 1) * 512],
                            spsum[i % 4][:, :], AF.Exp, scale=0.125))

                @blk.gpsimd
                def _(g):
                    for qt in range(NQT):
                        for h in range(NH):
                            gi = qt * NH + h
                            sCX.w(g, gi + 1)
                            if ag_out is None:
                                dst = ctxl_dram[h * 64:(h + 1) * 64,
                                                qt * QT:(qt + 1) * QT]
                            else:
                                dst = bass.AP(
                                    ctxl_dram.tensor,
                                    ctxl_dram.offset + qt * D * QT + h * 64 * QT,
                                    [[QT, 64], [1, QT]])
                            sCSs[gi % 2].inc(g.dma_start(
                                dst,
                                cst[:, (gi % 2) * QT:(gi % 2 + 1) * QT]), 16)
                        if ag_out is not None:
                            sCSs[0].w(g, 16 * ((qt + 1) * NH // 2))
                            sCSs[1].w(g, 16 * ((qt + 1) * NH // 2))
                            ag_sem.inc(g.collective_compute(
                                "AllGather", ALU.bypass,
                                replica_groups=[list(range(8))],
                                ins=[bass.AP(ctxl_dram.tensor,
                                             ctxl_dram.offset + qt * D * QT,
                                             [[QT, D], [1, QT]])],
                                outs=[bass.AP(ag_out.tensor,
                                              ag_out.offset + qt * 8 * D * QT,
                                              [[QT, 8 * D], [1, QT]])]), 1)
                    w(g, sCSs[0])
                    w(g, sCSs[1])
                    if ag_out is not None:
                        ag_sem.w(g, NQT)
            bar()

    def emit_wout0():
        sCL, sM, sE = C("w0cl"), C("w0mm"), C("w0ev")
        sWs = [C("w0wl0"), C("w0wl1")]
        sXs = [C("w0xl0"), C("w0xl1")]
        sOs = [C("w0os0"), C("w0os1")]
        with (
            nc.sbuf_tensor("w0_ctx", [128, 16 * 512], F32) as ctx_g,
            nc.sbuf_tensor("w0_wst", [128, 2 * 16 * 128], F32) as wst_g,
            nc.sbuf_tensor("w0_xr", [128, 2 * 512], F32) as xr_g,
            nc.sbuf_tensor("w0_ob", [128, 2 * 512], F32) as ob_g,
            nc.psum_tensor("w0_ps0", [128, 512], F32) as p0_g,
            nc.psum_tensor("w0_ps1", [128, 512], F32) as p1_g,
        ):
            ctxt = ctx_g.ap()
            wst = wst_g.ap()
            xr = xr_g.ap()
            ob = ob_g.ap()
            pj = [p0_g.ap(), p1_g.ap()]
            with nc.Block() as blk:
                @blk.sync
                def _(sy):
                    for tt in range(4):
                        if tt >= 1:
                            sM.w(sy, tt * 8)
                        sCL.inc(sy.dma_start(
                            bass.AP(ctxt.tensor, ctxt.offset,
                                    [[16 * 512, 128], [512, 16], [1, 512]]),
                            bass.AP(ctx0_dram.tensor,
                                    ctx0_dram.offset + tt * 512,
                                    [[T, 128], [128 * T, 16], [1, 512]])), 16)
                        for dc in range(DC):
                            j = tt * 8 + dc
                            if j >= 2:
                                sE.w(sy, j - 1)
                            sXs[j % 2].inc(sy.dma_start(
                                xr[:, (j % 2) * 512:(j % 2 + 1) * 512],
                                x_in[dc * 128:(dc + 1) * 128,
                                     tt * 512:(tt + 1) * 512]), 16)
                            if j >= 2:
                                sM.w(sy, j - 1)
                            sWs[j % 2].inc(sy.dma_start(
                                bass.AP(wst.tensor,
                                        wst.offset + (j % 2) * 2048,
                                        [[2 * 2048, 128], [128, 16], [1, 128]]),
                                bass.AP(w_out.tensor, w_out.offset + dc * 128,
                                        [[D, 128], [128 * D, 16], [1, 128]])), 16)

                @blk.tensor
                def _(t):
                    for tt in range(4):
                        sCL.w(t, 16 * (tt + 1))
                        for dc in range(DC):
                            j = tt * 8 + dc
                            if j >= 2:
                                t_ = sE.w(t, j - 1)
                            sWs[j % 2].w(t, 16 * (j // 2 + 1))
                            ins = None
                            for pr in range(16):
                                ins = t.matmul(pj[j % 2][:, :],
                                               wst[:, (j % 2) * 2048 + pr * 128:
                                                   (j % 2) * 2048 + (pr + 1) * 128],
                                               ctxt[:, pr * 512:(pr + 1) * 512],
                                               start=(pr == 0), stop=(pr == 15))
                            sM.inc(ins)

                @blk.vector
                def _(v):
                    for tt in range(4):
                        for dc in range(DC):
                            j = tt * 8 + dc
                            sM.w(v, j + 1)
                            sXs[j % 2].w(v, 16 * (j // 2 + 1))
                            if j >= 2:
                                sOs[j % 2].w(v, 16 * (j // 2))
                            sE.inc(v.tensor_tensor(
                                ob[:, (j % 2) * 512:(j % 2 + 1) * 512],
                                pj[j % 2][:, :],
                                xr[:, (j % 2) * 512:(j % 2 + 1) * 512], ALU.add))

                @blk.gpsimd
                def _(g):
                    for tt in range(4):
                        for dc in range(DC):
                            j = tt * 8 + dc
                            sE.w(g, j + 1)
                            sOs[j % 2].inc(g.dma_start(
                                bass.AP(x1_dram.tensor,
                                        x1_dram.offset + dc * 128 * T + tt * 512,
                                        [[T, 128], [1, 512]]),
                                ob[:, (j % 2) * 512:(j % 2 + 1) * 512]), 16)
                    w(g, sOs[0])
                    w(g, sOs[1])
            bar()

    def emit_cb():
        sWL, sXT, sM, sE = C("cbwl"), C("cbxt"), C("cbmm"), C("cbev")
        sXs = [C("cbxr0"), C("cbxr1")]
        sOs = [C("cbos0"), C("cbos1")]
        with (
            nc.sbuf_tensor("cbt_sb", [128, DC * 1024], F32) as cbt_g,
            nc.sbuf_tensor("cb_xt", [128, DC * 512], F32) as xt_g,
            nc.sbuf_tensor("cb_xr", [128, 2 * 512], F32) as xr_g,
            nc.sbuf_tensor("cb_ob", [128, 2 * 512], F32) as ob_g,
            nc.sbuf_tensor("cb_obt", [128, 2 * 512], F32) as obt_g,
            nc.psum_tensor("cb_ps0", [128, 512], F32) as cps0,
            nc.psum_tensor("cb_ps1", [128, 512], F32) as cps1,
        ):
            cbtsb = cbt_g.ap()
            xt = xt_g.ap()
            xr = xr_g.ap()
            ob = ob_g.ap()
            obt = obt_g.ap()
            cps = [cps0.ap(), cps1.ap()]
            with nc.Block() as blk:
                @blk.sync
                def _(sy):
                    for dc in range(DC):
                        srcw = bass.AP(cbt.tensor, cbt.offset + dc * 128 * D,
                                       [[D, 128], [1, D]])
                        sWL.inc(sy.dma_start(cbtsb[:, dc * 1024:(dc + 1) * 1024],
                                             srcw), 16)
                    for tt in range(4):
                        if tt >= 1:
                            sM.w(sy, tt * 8)
                        sXT.inc(sy.dma_start(
                            bass.AP(xt.tensor, xt.offset,
                                    [[DC * 512, 128], [512, DC], [1, 512]]),
                            bass.AP(xln_dram.tensor,
                                    xln_dram.offset + tt * 512,
                                    [[T, 128], [128 * T, DC], [1, 512]])), 16)
                        for nc_ in range(8):
                            j = tt * 8 + nc_
                            if j >= 2:
                                sE.w(sy, j - 1)
                            sXs[j % 2].inc(sy.dma_start(
                                xr[:, (j % 2) * 512:(j % 2 + 1) * 512],
                                bass.AP(x1_dram.tensor,
                                        x1_dram.offset + nc_ * 128 * T + tt * 512,
                                        [[T, 128], [1, 512]])), 16)

                @blk.tensor
                def _(t):
                    w(t, sWL)
                    for tt in range(4):
                        sXT.w(t, 16 * (tt + 1))
                        for nc_ in range(8):
                            j = tt * 8 + nc_
                            if j >= 2:
                                sE.w(t, j - 1)
                            ins = None
                            for dc in range(DC):
                                ins = t.matmul(
                                    cps[j % 2][:, :],
                                    cbtsb[:, dc * 1024 + nc_ * 128:
                                          dc * 1024 + (nc_ + 1) * 128],
                                    xt[:, dc * 512:(dc + 1) * 512],
                                    start=(dc == 0), stop=(dc == DC - 1))
                            sM.inc(ins)

                @blk.vector
                def _(v):
                    for tt in range(4):
                        for nc_ in range(8):
                            j = tt * 8 + nc_
                            sM.w(v, j + 1)
                            sXs[j % 2].w(v, 16 * (j // 2 + 1))
                            if j >= 2:
                                sOs[j % 2].w(v, 16 * (j // 2))
                            v.tensor_copy(obt[:, (j % 2) * 512:(j % 2 + 1) * 512],
                                          cps[j % 2][:, :])
                            v.tensor_scalar(ob[:, (j % 2) * 512:(j % 2 + 1) * 512],
                                            obt[:, (j % 2) * 512:(j % 2 + 1) * 512],
                                            scbq_sb[:, nc_:nc_ + 1],
                                            cbias_sb[:, nc_:nc_ + 1],
                                            ALU.mult, ALU.add)
                            sE.inc(v.tensor_tensor(
                                ob[:, (j % 2) * 512:(j % 2 + 1) * 512],
                                ob[:, (j % 2) * 512:(j % 2 + 1) * 512],
                                xr[:, (j % 2) * 512:(j % 2 + 1) * 512], ALU.add))

                @blk.gpsimd
                def _(g):
                    for tt in range(4):
                        for nc_ in range(8):
                            j = tt * 8 + nc_
                            sE.w(g, j + 1)
                            sOs[j % 2].inc(g.dma_start(
                                bass.AP(x2_dram.tensor,
                                        x2_dram.offset + nc_ * 128 * T + tt * 512,
                                        [[T, 128], [1, 512]]),
                                ob[:, (j % 2) * 512:(j % 2 + 1) * 512]), 16)
                    w(g, sOs[0])
                    w(g, sOs[1])
            bar()

    def emit_bvbc(bvbc):
        sB = C("bvb")
        with (
            nc.sbuf_tensor("bvrow", [1, D], F32) as bvr_g,
            nc.psum_tensor("bv_ps", [128, 512], F32) as bvps_g,
        ):
            bvr = bvr_g.ap()
            bvps = bvps_g.ap()
            with nc.Block() as blk:
                @blk.sync
                def _(sy):
                    sB.inc(sy.dma_start(bvr[:, :], bv_d[:]), 16)

                @blk.tensor
                def _(t):
                    sB.w(t, 16)
                    sB.inc(t.matmul(bvps[:, :], ones1[:, :], bvr[:, 0:512],
                                    start=True, stop=True))
                    sB.w(t, 18)
                    sB.inc(t.matmul(bvps[:, :], ones1[:, :], bvr[:, 512:1024],
                                    start=True, stop=True))

                @blk.vector
                def _(v):
                    sB.w(v, 17)
                    sB.inc(v.tensor_copy(bvbc[:, 0:512], bvps[:, :]))
                    sB.w(v, 19)
                    v.tensor_copy(bvbc[:, 512:1024], bvps[:, :])
            bar()

    def emit_band1():
        sBP = C("bprol")
        sBPg = C("bprolg")
        with (
            nc.sbuf_tensor("brsb", [1, 4095], F32) as brsb_g,
            nc.sbuf_tensor("etsb", [1, 2], F32) as etsb_g,
        ):
            brsb = brsb_g.ap()
            etsb = etsb_g.ap()
            with nc.Block() as blk:
                @blk.sync
                def _(sy):
                    sBP.inc(sy.dma_start(brsb[:, :], bandrev1[:]), 16)
                    sBP.inc(sy.dma_start(etsb[:, 0:1], etg[:, :]), 16)
                    sBP.inc(sy.dma_start(etsb[:, 1:2], isg[:, :]), 16)

                @blk.scalar
                def _(sc):
                    sBP.w(sc, 48)
                    sBP.inc(sc.activation(etsb[:, 0:1], etsb[:, 0:1], AF.Sigmoid))

                @blk.vector
                def _(v):
                    sBP.w(v, 49)
                    v.tensor_single_scalar(etsb[:, 0:1], etsb[:, 0:1], 8.0, ALU.mult)
                    sBP.inc(v.tensor_single_scalar(etsb[:, 1:2], etsb[:, 1:2], 8.0,
                                                   ALU.mult))
                    sBP.w(v, 50)
                    sBP.inc(v.tensor_scalar(brsb[:, :], brsb[:, :], etsb[:, 0:1],
                                            etsb[:, 1:2], ALU.mult, ALU.add))

                @blk.gpsimd
                def _(g):
                    sBP.w(g, 51)
                    sBPg.inc(g.dma_start(bdev_dram[:], brsb[:, :]), 16)
                    sBPg.w(g, 16)
                    for p in range(128):
                        sBPg.inc(g.dma_start(
                            bm1_dram[p:p + 1, :],
                            bass.AP(bdev_dram.tensor, bdev_dram.offset + 127 - p,
                                    [[1, 1], [1, BW]])), 16)
                    w(g, sBPg)
            bar()

    def emit_scramble_proj():
        sCL, sM, sE = C("p12cl"), C("p12mm"), C("p12ev")
        sCLF = C("p12clf")
        sWs = [C("p12wl0"), C("p12wl1")]
        sXs = [C("p12xl0"), C("p12xl1")]
        sSs = [C("p12st0"), C("p12st1")]
        with (
            nc.sbuf_tensor("csrc", [128, 8 * T], F32) as csrc_g,
            nc.sbuf_tensor("wost", [128, 2 * 2048], F32) as wost_g,
            nc.sbuf_tensor("x2st", [128, 2 * T], F32) as x2st_g,
            nc.sbuf_tensor("x3st", [128, 2 * T], F32) as x3st_g,
            nc.psum_tensor("p12_ps", [128, 2048], F32) as ps_g,
        ):
            csrc = csrc_g.ap()
            wost = wost_g.ap()
            x2st = x2st_g.ap()
            x3st = x3st_g.ap()
            ps = ps_g.ap()
            with nc.Block() as blk:
                @blk.sync
                def _(sy):
                    for di_, dc in enumerate([1, 2, 3, 4, 5, 6, 7, 0]):
                        if di_ >= 2:
                            sM.w(sy, di_ - 1)
                        for uh in range(2):
                            srcw = bass.AP(woT.tensor, woT.offset + dc * 128,
                                           [[D, 64], [64 * D, 16], [1, 128]])
                            sWs[di_ % 2].inc(sy.dma_start(
                                bass.AP(wost.tensor,
                                        wost.offset + uh * 64 * (2 * 2048)
                                        + (di_ % 2) * 2048,
                                        [[2 * 2048, 64], [128, 16], [1, 128]]),
                                srcw), 16)
                        if di_ >= 2:
                            sE.w(sy, di_ - 1)
                        sXs[di_ % 2].inc(sy.dma_start(
                            x2st[:, (di_ % 2) * T:(di_ % 2 + 1) * T],
                            x2_dram[dc * 128:(dc + 1) * 128, :]), 16)

                @blk.tensor
                def _(t):
                    w(t, sCL)
                    sCLF.w(t, 16)
                    for di_, dc in enumerate([1, 2, 3, 4, 5, 6, 7, 0]):
                        sWs[di_ % 2].w(t, 32 * (di_ // 2 + 1))
                        if di_ >= 1:
                            sE.w(t, di_)
                        ins = None
                        for s in range(16):
                            for u in range(2):
                                lhs = bass.AP(wost.tensor,
                                              wost.offset + u * 64 * (2 * 2048)
                                              + (di_ % 2) * 2048 + s * 128,
                                              [[2 * 2048, 64], [1, 128]])
                                for m in range(8):
                                    k = u * 8 + m
                                    rhs = bass.AP(csrc.tensor,
                                                  csrc.offset + m * T
                                                  + (u * 64) * (8 * T) + s,
                                                  [[8 * T, 64], [16, 128]])
                                    ins = t.matmul(
                                        ps[:, k * 128:(k + 1) * 128], lhs, rhs,
                                        start=(s == 0 and k % 4 == 0),
                                        stop=(s == 15))
                        sM.inc(ins)

                @blk.vector
                def _(v):
                    for di_, dc in enumerate([1, 2, 3, 4, 5, 6, 7, 0]):
                        sM.w(v, di_ + 1)
                        sXs[di_ % 2].w(v, 16 * (di_ // 2 + 1))
                        if di_ >= 2:
                            sSs[di_ % 2].w(v, 16 * (di_ // 2))
                        sE.inc(v.scalar_tensor_tensor(
                            x3st[:, (di_ % 2) * T:(di_ % 2 + 1) * T],
                            ps[:, :], bo_sb[:, dc:dc + 1],
                            x2st[:, (di_ % 2) * T:(di_ % 2 + 1) * T],
                            ALU.add, ALU.add))

                @blk.gpsimd
                def _(g):
                    with g.register("p12r") as r12:
                        pid = g.partition_id()
                        for m in range(8):
                            g.reg_mul(r12, pid, 128 * QT)
                            g.reg_add(r12, r12, m * D * QT)
                            sCL.inc(g.dma_start(
                                bass.AP(csrc.tensor, csrc.offset + m * T,
                                        [[8 * T, 128], [QT, NQT], [1, QT]]),
                                bass.AP(agctx.tensor, r12,
                                        [[QT, 128], [8 * D * QT, NQT], [1, QT]])), 16)
                    sCL.w(g, 128)
                    sCLF.inc(g.dma_start(x3st[:, 0:4], x2st[:, 0:4]), 16)
                    if "d_csrc" in dump_aps:
                        sDc = C("dmpg")
                        sDc.inc(g.dma_start(dump_aps["d_csrc"][:, :], csrc[:, :]), 16)
                        w(g, sDc)
                    for di_, dc in enumerate([1, 2, 3, 4, 5, 6, 7, 0]):
                        sE.w(g, di_ + 1)
                        sSs[di_ % 2].inc(g.dma_start(
                            x3_dram[dc * 128:(dc + 1) * 128, :],
                            x3st[:, (di_ % 2) * T:(di_ % 2 + 1) * T]), 16)
                    w(g, sSs[0])
                    w(g, sSs[1])
            bar()

    def emit_moe():
        TT = 512
        (sXL, sLSQ, sLSM, sLVA, sLRT, sLST, sLBC, sLAP) = (
            C("m" + s) for s in ("xl", "lsq", "lsm", "lva", "lrt", "lst", "lbc", "lap"))
        (sGM, sGB, sTR, sMK, sT2, sMF, sMB, sXE, sCU, sFW) = (
            C("m" + s) for s in
            ("gm", "gb", "tr", "mk", "t2", "mf", "mb", "xe", "cu", "fw"))
        (sHM, sGE, sW2T, sYE, sEV) = (
            C("m" + s) for s in ("hm", "ge", "w2t", "ye", "ev"))
        sW1s = [C("atql0"), C("atql1"), C("atql2"), C("atql3")]
        sW2s = [C("atcs0"), C("atcs1"), C("p12st0"), C("p12st1")]
        sOSs2 = [C("mosa"), C("mosb")]
        import contextlib
        with contextlib.ExitStack() as _st:
            x3_g = _st.enter_context(nc.sbuf_tensor("m_x3", [128, DC * TT], F32))
            x3ln_g = _st.enter_context(nc.sbuf_tensor("m_x3ln", [128, DC * TT], F32))
            xe_g = _st.enter_context(nc.sbuf_tensor("m_xe", [128, DC * TT], F32))
            h_g = _st.enter_context(nc.sbuf_tensor("m_h", [128, 32 * TT], F32))
            ysb_g = _st.enter_context(nc.sbuf_tensor("m_ysb", [128, DC * TT], F32))
            w1st_g = _st.enter_context(nc.sbuf_tensor("m_w1st", [128, 4 * 1024], F32))
            w2st_g = _st.enter_context(nc.sbuf_tensor("m_w2st", [128, 2 * 1024], F32))
            lg_g = _st.enter_context(nc.sbuf_tensor("m_lg", [8, TT], F32))
            mtk_g = _st.enter_context(nc.sbuf_tensor("m_mtk", [128, 8], F32))
            mtk2_g = _st.enter_context(nc.sbuf_tensor("m_mtk2", [128, 8], F32))
            gcap_g = _st.enter_context(nc.sbuf_tensor("m_gcap", [128, 32], F32))
            run_g = _st.enter_context(nc.sbuf_tensor("m_run", [128, 2], F32))
            mfm_g = _st.enter_context(nc.sbuf_tensor("m_mfm", [8, TT], F32))
            mbc_g = _st.enter_context(nc.sbuf_tensor("m_mbc", [128, TT], F32))
            ost_g = _st.enter_context(nc.sbuf_tensor("m_ost", [128, 2 * TT], F32))
            msq_g = _st.enter_context(nc.sbuf_tensor("m_sq", [128, DC * TT], F32))
            mmean_g = _st.enter_context(nc.sbuf_tensor("m_mean", [1, TT], F32))
            mvarp_g = _st.enter_context(nc.sbuf_tensor("m_varp", [1, TT], F32))
            mt1_g = _st.enter_context(nc.sbuf_tensor("m_t1", [1, TT], F32))
            minv_g = _st.enter_context(nc.sbuf_tensor("m_inv", [1, TT], F32))
            selsb_g = _st.enter_context(nc.sbuf_tensor("m_selsb", [8, 8 * 128], F32))
            id8_g = _st.enter_context(nc.sbuf_tensor("m_id8", [8, 8], F32))
            id128_g = _st.enter_context(nc.sbuf_tensor("m_id128", [128, 128], F32))
            wg_g = _st.enter_context(nc.sbuf_tensor("m_wg", [128, 64], F32))
            ucum_g = _st.enter_context(nc.sbuf_tensor("m_ucum", [8, 8], F32))
            ls_g = _st.enter_context(nc.psum_tensor("m_ls12", [33, TT], F32))
            gbc_g = _st.enter_context(nc.psum_tensor("m_gbc", [128, 512], F32))
            hp0_g = _st.enter_context(nc.psum_tensor("m_hps0", [128, 512], F32))
            hp1_g = _st.enter_context(nc.psum_tensor("m_hps1", [128, 512], F32))
            y0_g = _st.enter_context(nc.psum_tensor("m_y0", [128, 512], F32))
            y1_g = _st.enter_context(nc.psum_tensor("m_y1", [128, 512], F32))
            y2_g = _st.enter_context(nc.psum_tensor("m_y2", [128, 512], F32))
            y3_g = _st.enter_context(nc.psum_tensor("m_y3", [128, 512], F32))
            x3sb = x3_g.ap()
            x3ln = x3ln_g.ap()
            xesb = xe_g.ap()
            hsb = h_g.ap()
            ysb = ysb_g.ap()
            w1st = w1st_g.ap()
            w2st = w2st_g.ap()
            lg = lg_g.ap()
            mtk = mtk_g.ap()
            mtk2 = mtk2_g.ap()
            gcap = gcap_g.ap()
            runb = run_g.ap()
            mfm = mfm_g.ap()
            mbcsb = mbc_g.ap()
            ost = ost_g.ap()
            msq = msq_g.ap()
            mean, varp, t1v, inv = (mmean_g.ap(), mvarp_g.ap(), mt1_g.ap(),
                                    minv_g.ap())
            selsb = selsb_g.ap()
            id8 = id8_g.ap()
            id128 = id128_g.ap()
            wg_sb = wg_g.ap()
            ucum = ucum_g.ap()
            ls12 = ls_g.ap()
            gbc = gbc_g.ap()
            hps = [hp0_g.ap(), hp1_g.ap()]
            ypsl = [y0_g.ap(), y1_g.ap(), y2_g.ap(), y3_g.ap()]

            prol = C("mprol")
            with nc.Block() as blk:
                @blk.sync
                def _(sy):
                    prol.inc(sy.dma_start(selsb[:, :], sel_c[:, :]), 16)
                    prol.inc(sy.dma_start(id8[:, :], id8_c[:, :]), 16)
                    prol.inc(sy.dma_start(id128[:, :], id128_c[:, :]), 16)
                    prol.inc(sy.dma_start(ucum[:, :], ucum_c[:, :]), 16)
                    for dc in range(DC):
                        prol.inc(sy.dma_start(wg_sb[:, dc * 8:(dc + 1) * 8],
                                              wg_d[dc * 128:(dc + 1) * 128, :]), 16)
                    w(sy, prol)
            bar()

            for tt in range(4):
                with nc.Block() as blk:
                    @blk.sync
                    def _(sy):
                        if tt >= 1:
                            sEV.w(sy, tt * 8)
                            sXE.w(sy, tt * 8)
                        sXL.inc(sy.dma_start(
                            bass.AP(x3sb.tensor, x3sb.offset,
                                    [[DC * TT, 128], [TT, DC], [1, TT]]),
                            bass.AP(x3_dram.tensor,
                                    x3_dram.offset + tt * TT,
                                    [[T, 128], [128 * T, DC], [1, TT]])), 16)

                    @blk.scalar
                    def _(sc):
                        sXL.w(sc, 16 * (tt + 1))
                        for dc in range(DC):
                            i = tt * DC + dc
                            sLSQ.inc(sc.activation(
                                msq[:, dc * TT:(dc + 1) * TT],
                                x3sb[:, dc * TT:(dc + 1) * TT], AF.Square))
                        sLVA.w(sc, tt + 1)
                        sLRT.inc(sc.activation(t1v, varp, AF.Sqrt))

                    @blk.tensor
                    def _(t):
                        sXL.w(t, 16 * (tt + 1))
                        for dc in range(DC):
                            i = tt * DC + dc
                            t.matmul(ls12[0:1, :], onesc[:, :],
                                     x3sb[:, dc * TT:(dc + 1) * TT],
                                     start=(dc == 0), stop=(dc == DC - 1))
                            sLSQ.w(t, i + 1)
                            ins = t.matmul(ls12[32:33, :], onesc[:, :],
                                           msq[:, dc * TT:(dc + 1) * TT],
                                           start=(dc == 0), stop=(dc == DC - 1))
                        sLSM.inc(ins)
                        sLST.w(t, tt + 1)
                        t.matmul(gbc[:, :], ones1[:, :], inv, start=True, stop=True)
                        sLBC.inc(t.matmul(hps[0][:, :], ones1[:, :], mean,
                                          start=True, stop=True))

                    @blk.vector
                    def _(v):
                        sLSM.w(v, tt + 1)
                        v.tensor_single_scalar(mean, ls12[0:1, :], 1.0 / 1024.0,
                                               ALU.mult)
                        v.tensor_single_scalar(varp, ls12[32:33, :], 1.0 / 1024.0,
                                               ALU.mult)
                        v.tensor_tensor(t1v, mean, mean, ALU.mult)
                        v.tensor_tensor(varp, varp, t1v, ALU.subtract)
                        sLVA.inc(v.tensor_single_scalar(varp, varp, EPS, ALU.add))
                        sLRT.w(v, tt + 1)
                        v.reciprocal(inv, t1v)
                        for _i in range(2):
                            v.tensor_tensor(t1v, inv, inv, ALU.mult)
                            v.tensor_tensor(t1v, t1v, varp, ALU.mult)
                            v.tensor_scalar(t1v, t1v, -0.5, 1.5, ALU.mult, ALU.add)
                            v.tensor_tensor(inv, inv, t1v, ALU.mult)
                        sLST.inc(v.tensor_tensor(mean, mean, inv, ALU.mult))
                        sLBC.w(v, tt + 1)
                        ins = None
                        for dc in range(DC):
                            sl = slice(dc * TT, (dc + 1) * TT)
                            v.tensor_tensor(x3ln[:, sl], x3sb[:, sl],
                                            gbc[:, :], ALU.mult)
                            ins = v.tensor_tensor(x3ln[:, sl], x3ln[:, sl],
                                                  hps[0][:, :], ALU.subtract)
                        sLAP.inc(ins)

                bar()
                gm0, gb0, tr0, mk0, t20, mf0 = (sGM.rval, sGB.rval, sTR.rval,
                                                sMK.rval, sT2.rval, sMF.rval)
                cu0, fw0 = sCU.rval, sFW.rval
                with nc.Block() as blk:
                    @blk.tensor
                    def _(t):
                        ins = None
                        for dc in range(DC):
                            ins = t.matmul(gbc[0:8, :], wg_sb[:, dc * 8:(dc + 1) * 8],
                                           x3ln[:, dc * TT:(dc + 1) * TT],
                                           start=(dc == 0), stop=(dc == DC - 1))
                        sGM.inc(ins)
                        for q in range(4):
                            sGB.w(t, gb0 + 1)
                            if q >= 1:
                                sFW.w(t, fw0 + q)
                                sMK.w(t, mk0 + 3 * q)
                            sTR.inc(t.transpose(gbc[0:128, 0:8],
                                                lg[:, q * 128:(q + 1) * 128],
                                                id8[:, :]))
                            sMK.w(t, mk0 + 3 * q + 3)
                            sT2.inc(t.transpose(gbc[0:8, 64:192],
                                                mtk[:, :], id128[:, :]))
                            sMF.w(t, mf0 + q + 1)
                            sCU.inc(t.matmul(gbc[0:8, 192:320], ucum[:, :],
                                             mfm[:, q * 128:(q + 1) * 128],
                                             start=True, stop=True))

                    @blk.scalar
                    def _(sc):
                        sGM.w(sc, gm0 + 1)
                        sGB.inc(sc.activation(lg[:, :], gbc[0:8, :], AF.Identity,
                                              bias=bg_sb[:, :]))
                        for q in range(4):
                            sT2.w(sc, t20 + q + 1)
                            sMF.inc(sc.copy(mfm[:, q * 128:(q + 1) * 128],
                                            gbc[0:8, 64:192]))

                    @blk.vector
                    def _(v):
                        for q in range(4):
                            sTR.w(v, tr0 + q + 1)
                            sMK.inc(v.tensor_copy(mtk2[:, :], gbc[0:128, 0:8]))
                            sMK.w(v, mk0 + 3 * q + 1)
                            sMK.inc(v.reduce_max(runb[:, 0:1], mtk2[:, :],
                                                 mybir.AxisListType.X))
                            sMK.w(v, mk0 + 3 * q + 2)
                            sMK.inc(v.tensor_single_scalar(mtk[:, :], mtk2[:, :],
                                                           runb[:, 0:1], ALU.is_ge))
                            sCU.w(v, cu0 + q + 1)
                            sFW.inc(v.scalar_tensor_tensor(
                                mfm[:, q * 128:(q + 1) * 128],
                                gbc[0:8, 192:320], 0.0,
                                mfm[:, q * 128:(q + 1) * 128],
                                ALU.is_equal, ALU.mult))
                            if "d_gdbg" in dump_aps and tt == 0 and q == 0:
                                v.tensor_copy(gcap[:, 0:8], mtk2[:, :])
                                v.tensor_copy(gcap[:, 8:9], runb[:, 0:1])
                                v.tensor_copy(gcap[:, 9:17], mtk[:, :])
                bar()
                if "d_mask" in dump_aps:
                    sD = C(f"dmask{tt}")
                    with nc.Block() as blk:
                        @blk.gpsimd
                        def _(g):
                            sD.inc(g.dma_start(
                                dump_aps["d_mask"][:, tt * TT:(tt + 1) * TT],
                                mfm[:, :]), 16)
                            w(g, sD)
                    bar()
                hm0, ge0, mb0, xe0 = sHM.rval, sGE.rval, sMB.rval, sXE.val
                w2t0, ye0 = sW2T.rval, sYE.rval
                w1s0 = [s.rval for s in sW1s]
                w2s0 = [s.rval for s in sW2s]
                with nc.Block() as blk:
                    @blk.sync
                    def _(sy):
                        for e in range(8):
                            for hc in range(32):
                                i = e * 32 + hc
                                if i >= 4:
                                    sHM.w(sy, hm0 + i - 3)
                                src = bass.AP(
                                    w1_d.tensor,
                                    w1_d.offset + e * D * 4096 + hc * 128,
                                    [[4096, 128], [4096 * 128, 8], [1, 128]])
                                sW1s[i % 4].inc(sy.dma_start(
                                    bass.AP(w1st.tensor,
                                            w1st.offset + (i % 4) * 1024,
                                            [[4 * 1024, 128], [128, 8], [1, 128]]),
                                    src), 16)

                    @blk.scalar
                    def _(sc):
                        for e in range(8):
                            for hc in range(32):
                                i = e * 32 + hc
                                sHM.w(sc, hm0 + i + 1)
                                sGE.inc(sc.activation(
                                    hsb[:, hc * TT:(hc + 1) * TT],
                                    hps[i % 2][:, :], AF.Gelu_apprx_tanh))

                    @blk.gpsimd
                    def _(g):
                        for e in range(8):
                            for half in range(2):
                                for hc in range(32):
                                    ii = (2 * e + half) * 32 + hc
                                    if ii >= 4:
                                        sW2T.w(g, w2t0 + ii - 3)
                                    src = bass.AP(
                                        w2_d.tensor,
                                        w2_d.offset + (e * 4096 + hc * 128) * D
                                        + half * 512,
                                        [[D, 128], [1, 512]])
                                    sW2s[ii % 4].inc(g.dma_start(
                                        w2st[:, (ii % 4) * 512:(ii % 4 + 1) * 512],
                                        src), 16)

                    @blk.tensor
                    def _(t):
                        def w2_half(e, half):
                            if 2 * e + half >= 1:
                                sYE.w(t, ye0 + 2 * e + half)
                            for hc in range(32):
                                ii = (2 * e + half) * 32 + hc
                                sW2s[ii % 4].w(t, w2s0[ii % 4] + 16 * (ii // 4 + 1))
                                sGE.w(t, ge0 + e * 32 + hc + 1)
                                ins = None
                                for di in range(4):
                                    ins = t.matmul(
                                        ypsl[di][:, :],
                                        w2st[:, (ii % 4) * 512 + di * 128:
                                             (ii % 4) * 512 + (di + 1) * 128],
                                        hsb[:, hc * TT:(hc + 1) * TT],
                                        start=(hc == 0), stop=(hc == 31))
                                sW2T.inc(ins)

                        for e in range(8):
                            sFW.w(t, fw0 + 4)
                            sLAP.w(t, tt + 1)
                            if e == 0:
                                sMB.inc(t.matmul(gbc[0:128, :],
                                                 selsb[:, 0:128],
                                                 mfm[:, :], start=True, stop=True))
                            for hc in range(32):
                                i = e * 32 + hc
                                sW1s[i % 4].w(t, w1s0[i % 4] + 16 * (i // 4 + 1))
                                sXE.w(t, xe0 + e + 1)
                                if i >= 2:
                                    sGE.w(t, ge0 + i - 1)
                                ins = None
                                for dc in range(DC):
                                    ins = t.matmul(
                                        hps[i % 2][:, :],
                                        w1st[:, (i % 4) * 1024 + dc * 128:
                                             (i % 4) * 1024 + (dc + 1) * 128],
                                        xesb[:, dc * TT:(dc + 1) * TT],
                                        start=(dc == 0), stop=(dc == DC - 1))
                                sHM.inc(ins)
                            if e < 7:
                                sXE.w(t, xe0 + e + 1)
                                sMB.inc(t.matmul(gbc[0:128, :],
                                                 selsb[:, (e + 1) * 128:(e + 2) * 128],
                                                 mfm[:, :], start=True, stop=True))
                            w2_half(e, 0)
                            w2_half(e, 1)

                    @blk.vector
                    def _(v):
                        def vcopy_xe(e):
                            sMB.w(v, mb0 + e + 1)
                            v.tensor_copy(mbcsb[:, :], gbc[0:128, :])
                            ins = None
                            for dc in range(DC):
                                ins = v.tensor_tensor(
                                    xesb[:, dc * TT:(dc + 1) * TT],
                                    x3ln[:, dc * TT:(dc + 1) * TT],
                                    mbcsb[:, :], ALU.mult)
                            sXE.inc(ins)

                        def vevac(e, half):
                            sW2T.w(v, w2t0 + 32 * (2 * e + half + 1))
                            ins = None
                            for di in range(4):
                                dcw = half * 4 + di
                                if e == 0:
                                    ins = v.tensor_copy(
                                        ysb[:, dcw * TT:(dcw + 1) * TT],
                                        ypsl[di][:, :])
                                else:
                                    ins = v.tensor_tensor(
                                        ysb[:, dcw * TT:(dcw + 1) * TT],
                                        ysb[:, dcw * TT:(dcw + 1) * TT],
                                        ypsl[di][:, :], ALU.add)
                            sYE.inc(ins)

                        vcopy_xe(0)
                        for e in range(8):
                            if e < 7:
                                vcopy_xe(e + 1)
                            vevac(e, 0)
                            vevac(e, 1)
                bar()
                with nc.Block() as blk:
                    @blk.vector
                    def _(v):
                        for dc in range(DC):
                            j = tt * 8 + dc
                            if j >= 2:
                                sOSs2[j % 2].w(v, 16 * (j // 2))
                            sEV.inc(v.tensor_tensor(
                                ost[:, (j % 2) * TT:(j % 2 + 1) * TT],
                                ysb[:, dc * TT:(dc + 1) * TT],
                                x3sb[:, dc * TT:(dc + 1) * TT], ALU.add))

                    @blk.gpsimd
                    def _(g):
                        for dc in range(DC):
                            j = tt * 8 + dc
                            sEV.w(g, j + 1)
                            sOSs2[j % 2].inc(g.dma_start(
                                bass.AP(out_d.tensor,
                                        out_d.offset + dc * 128 * T + tt * TT,
                                        [[T, 128], [1, TT]]),
                                ost[:, (j % 2) * TT:(j % 2 + 1) * TT]), 16)
                bar()
            if "d_gdbg" in dump_aps:
                sDg = C("dmpg")
                with nc.Block() as blk:
                    @blk.gpsimd
                    def _(g):
                        sDg.inc(g.dma_start(dump_aps["d_gdbg"][:, :], gcap[:, :]), 16)
                        w(g, sDg)
                bar()
            with nc.Block() as blk:
                @blk.gpsimd
                def _(g):
                    w(g, sOSs2[0])
                    w(g, sOSs2[1])
            bar()

    if "sm0" in dumps:
        dump_smalls()
    emit_ln_stream(x_in, xln_dram)
    if "sm1" in dumps:
        dump_smalls()
    dump_rows("d_xln", xln_dram, D)
    with (
        nc.sbuf_tensor("vsb0", [128, 16 * 1040], F32) as vsb0_g,
        nc.sbuf_tensor("ksb0", [128, DC * T], F32) as ksb0_g,
    ):
        vsb0, ksb0 = vsb0_g.ap(), ksb0_g.ap()
        if upto >= 2:
            emit_v(vsb0,
                   lambda sp, dc: w_qkv[dc * 128:(dc + 1) * 128,
                                        3072 + sp * 256:3072 + (sp + 1) * 256],
                   None)
            chunks0 = [(m, "q") for m in range(16)] + [(m, "k") for m in range(8)]

            def wsrc0(mch, kind):
                c0 = mch * 128 if kind == "q" else 2048 + mch * 128
                return bass.AP(w_qkv.tensor, w_qkv.offset + c0,
                               [[4096, 128], [128 * 4096, 8], [1, 128]])

            emit_qk(ksb0, chunks0, wsrc0, lambda mch, kind: None, q0_dram)
            if "sm2" in dumps:
                dump_smalls()
            if "d_v" in dump_aps:
                sD = C("dmpg")
                with nc.Block() as blk:
                    @blk.gpsimd
                    def _(g):
                        sD.inc(g.dma_start(dump_aps["d_v"][:, :], vsb0[:, :]), 16)
                        w(g, sD)
                bar()
            if "d_k" in dump_aps:
                sD = C("dmpg")
                with nc.Block() as blk:
                    @blk.gpsimd
                    def _(g):
                        for dc in range(DC):
                            sD.inc(g.dma_start(
                                dump_aps["d_k"][dc * 128:(dc + 1) * 128, :],
                                ksb0[:, dc * T:(dc + 1) * T]), 16)
                        w(g, sD)
                bar()
            dump_rows("d_q0", q0_dram, 2048)
        if upto >= 3:
            emit_attention(0, ksb0, vsb0, q0_dram, bm0, ctx0_dram)
            if "sm3" in dumps:
                dump_smalls()
            dump_rows("d_ctx0", ctx0_dram, 2048)
    if upto >= 4:
        emit_wout0()
        if "sm4" in dumps:
            dump_smalls()
        dump_rows("d_x1", x1_dram, D)
    if upto >= 5:
        emit_ln_stream(x1_dram, xln_dram)
        dump_rows("d_xln2", xln_dram, D)
        emit_cb()
        dump_rows("d_x2", x2_dram, D)
    if upto >= 6:
        emit_ln_stream(x2_dram, xln_dram)
        with (
            nc.sbuf_tensor("vsb1", [128, 16 * 1040], F32) as vsb1_g,
            nc.sbuf_tensor("ksb1", [128, DC * T], F32) as ksb1_g,
            nc.sbuf_tensor("bvbc", [128, D], F32) as bvbc_g,
        ):
            vsb1, ksb1, bvbc = vsb1_g.ap(), ksb1_g.ap(), bvbc_g.ap()
            emit_bvbc(bvbc)
            emit_v(vsb1,
                   lambda sp, dc: wvT[dc * 128:(dc + 1) * 128,
                                      sp * 256:(sp + 1) * 256],
                   bvbc)
            chunks1 = [(m, "q") for m in range(8)] + [(m, "k") for m in range(8)]

            def wsrc1(mch, kind):
                wsrc = wqT if kind == "q" else wkT
                return bass.AP(wsrc.tensor, wsrc.offset + mch * 128,
                               [[D, 128], [128 * D, 8], [1, 128]])

            def bias1(mch, kind):
                return (bq_sb if kind == "q" else bk_sb)[:, mch:mch + 1]

            emit_qk(ksb1, chunks1, wsrc1, bias1, q1_dram)
            dump_rows("d_q1", q1_dram, D)
            emit_band1()
            dump_rows("d_bm1", bm1_dram, 128)
            if upto >= 7:
                sA2 = C("a2a")
                emit_attention(1, ksb1, vsb1, q1_dram, bm1_dram, ctx1_dram,
                               ag_out=agctx, ag_sem=sA2)
                dump_rows("d_ctx1", ctx1_dram, D)
    if upto >= 8:
        nc.all_core_barrier()
        bar()
        dump_rows("d_a2a", agctx, 2 * D)
    if upto >= 9:
        emit_scramble_proj()
        dump_rows("d_x3", x3_dram, D)
    if upto >= 10:
        emit_moe()
    bar()
    return nc


def _host_inputs(inputs):
    f32 = np.float32
    c = np.ascontiguousarray
    x = np.asarray(inputs["L0_attn_inputs"], f32)
    tb = np.asarray(inputs["L1_attn_time_bias"], f32).reshape(500)
    idx = np.clip(np.arange(4095) - 2047 + 250, 0, 499)
    band1 = tb[idx]
    band1_rev = c(band1[::-1])
    common = {
        "w_qkv": c(np.asarray(inputs["L0_attn_W_qkv"], f32)),
        "w_out": c(np.asarray(inputs["L0_attn_W_out"], f32)),
        "cbt": c(np.asarray(inputs["L0_ffn_CB"], f32).T),
        "scb": c(np.asarray(inputs["L0_ffn_SCB"], f32)),
        "cbias": c(np.asarray(inputs["L0_ffn_bias"], f32)),
        "wqT": c(np.asarray(inputs["L1_attn_Wq"], f32).T),
        "wkT": c(np.asarray(inputs["L1_attn_Wk"], f32).T),
        "wvT": c(np.asarray(inputs["L1_attn_Wv"], f32).T),
        "woT": c(np.asarray(inputs["L1_attn_Wo"], f32).T),
        "bq": c(np.asarray(inputs["L1_attn_bq"], f32)),
        "bk": c(np.asarray(inputs["L1_attn_bk"], f32)),
        "bv": c(np.asarray(inputs["L1_attn_bv"], f32)),
        "bo": c(np.asarray(inputs["L1_attn_bo"], f32)),
        "etg": c(np.asarray(inputs["L1_attn_et_gate"], f32).reshape(1, 1)),
        "isg": c(np.asarray(inputs["L1_attn_is_gate"], f32).reshape(1, 1)),
        "bandrev1": band1_rev,
        "w1": c(np.asarray(inputs["L1_ffn_W1"], f32)),
        "w2": c(np.asarray(inputs["L1_ffn_W2"], f32)),
        "wg": c(np.asarray(inputs["L1_ffn_Wg"], f32)),
        "bg": c(np.asarray(inputs["L1_ffn_bg"], f32)),
    }
    in_maps = []
    for j in range(8):
        m = dict(common)
        m["x_fm"] = c(x[j].T)
        in_maps.append(m)
    return in_maps


def kernel(**inputs):
    in_maps = _host_inputs(inputs)
    nc = build()
    res = run_bass_kernel_spmd(nc, in_maps, core_ids=list(range(8)))
    out = np.empty((8, T, D), np.float32)
    for j in range(8):
        out[j] = np.asarray(res.results[j]["out_fm"]).T
    return out
